# version 06
# baseline (speedup 1.0000x reference)
import numpy as np
import ml_dtypes
from contextlib import ExitStack

import concourse.bass as bass
import concourse.mybir as mybir
from concourse.bass_utils import run_bass_kernel_spmd

F32 = mybir.dt.float32
BF16 = mybir.dt.bfloat16
I32 = mybir.dt.int32
AF = mybir.ActivationFunctionType
ALU = mybir.AluOpType
AX = mybir.AxisListType.X
BF = ml_dtypes.bfloat16

NCORES = 8
B, L, D = 4, 4096, 1024
TPC = 2048
NT = TPC // 128
NFT = TPC // 512
NDC = D // 128


class Sem:
    def __init__(self, es, nc, name):
        self.h = es.enter_context(nc.semaphore(name))
        self.n = 0

    def inc(self, inst, k=1):
        inst.then_inc(self.h, k)
        self.n += k


class Ctx:
    ENGINES = ("sync", "vector", "scalar", "tensor", "gpsimd")

    def __init__(self, nc, es):
        self.nc = nc
        self.es = es
        self.block = es.enter_context(nc.Block())
        self.sem = {e: Sem(es, nc, f"s_{e}") for e in self.ENGINES}
        self.dma = Sem(es, nc, "s_dma")
        self.dmo = Sem(es, nc, "s_dmo")
        self.cc = Sem(es, nc, "s_cc")
        self.epoch = Sem(es, nc, "s_epoch")
        self._consts = {}
        self.ident = None
        self.identb = None
        self.onesk = None
        self.onesc = None
        self.onescb = None

    def on(self, eng, fn):
        getattr(self.block, eng)(fn)

    _uid = 0

    def sbuf(self, name, shape, dtype):
        Ctx._uid += 1
        return self.es.enter_context(
            self.nc.sbuf_tensor(f"{name}_{Ctx._uid}", shape, dtype))

    def psum(self, name, shape, dtype=F32):
        Ctx._uid += 1
        return self.es.enter_context(
            self.nc.psum_tensor(f"{name}_{Ctx._uid}", shape, dtype))

    def const_pool(self, values):
        vals = sorted(set(float(v) for v in values))
        tile = self.sbuf("constpool", [128, max(len(vals), 1)], F32)
        for i, v in enumerate(vals):
            self._consts[v] = tile[:, i:i + 1]

    def c(self, v):
        return self._consts[float(v)]

    def emit_init(self):
        self.ident = self.sbuf("c_ident", [128, 128], F32)
        self.identb = self.sbuf("c_identb", [128, 128], BF16)
        self.onesk = self.sbuf("c_onesk", [128, 1], BF16)
        self.oneskf = self.sbuf("c_oneskf", [128, 1], F32)
        self.onesc = self.sbuf("c_onesc", [1, 128], F32)
        self.onescb = self.sbuf("c_onescb", [1, 128], BF16)

        def _f(g):
            for v, ap in self._consts.items():
                g.memset(ap, v)
            g.memset(self.onesk[:], 1.0)
            g.memset(self.oneskf[:], 1.0)
            g.memset(self.onesc[:], 1.0)
            g.memset(self.onescb[:], 1.0)
            for t in (self.ident, self.identb):
                g.memset(t[:], 0.0)
                inst = g.affine_select(out=t[:], in_=t[:], compare_op=ALU.not_equal,
                                       fill=1.0, base=0, pattern=[[-1, 128]],
                                       channel_multiplier=1)
            inst = g.tensor_copy(self.onescb[:], self.onescb[:])
            self.sem["gpsimd"].inc(inst)
        self.on("gpsimd", _f)
        self.barrier()

    def barrier(self):
        sems = [self.sem[e] for e in self.ENGINES] + [self.dma, self.dmo, self.cc]
        counts = [(s, s.n) for s in sems if s.n]
        for e in self.ENGINES:
            def _f(eng, counts=counts):
                for s, n in counts:
                    eng.wait_ge(s.h, n)
            self.on(e, _f)

    def seam_reset(self):
        self.barrier()
        sems = [self.sem[e] for e in self.ENGINES] + [self.dma, self.dmo, self.cc]
        def _clr(g, sems=sems):
            inst = None
            for s in sems:
                inst = g.sem_clear(s.h)
                s.n = 0
            self.epoch.inc(inst if inst is not None else g.engine_nop())
        self.on("gpsimd", _clr)
        en = self.epoch.n
        for e in self.ENGINES:
            self.on(e, lambda eng, en=en: eng.wait_ge(self.epoch.h, en))



def vchain(ctx, v, inst):
    ctx.sem["vector"].inc(inst)
    v.wait_ge(ctx.sem["vector"].h, ctx.sem["vector"].n)



def _host_prep(inputs):
    g = {}
    f32 = lambda a: np.ascontiguousarray(np.asarray(a), dtype=np.float32)
    b16 = lambda a: np.ascontiguousarray(np.asarray(a).astype(BF))

    g["x"] = f32(inputs["L0_attn_hidden_states"])

    def w_lay0(a):
        Dd, O = a.shape
        return np.ascontiguousarray(
            a.reshape(Dd // 128, 128, O).transpose(1, 0, 2).reshape(128, (Dd // 128) * O))

    def diag_stack(w_cols, taps):
        out = np.zeros((taps, 4, 128, 128), dtype=BF)
        idx = np.arange(128)
        for tap in range(taps):
            for c in range(4):
                out[tap, c, idx, idx] = w_cols[c * 128:(c + 1) * 128, tap].astype(BF)
        return out

    Wq0 = np.asarray(inputs["L0_attn_Wq"], np.float32)
    Wk0 = np.asarray(inputs["L0_attn_Wk"], np.float32)
    Wv0 = np.asarray(inputs["L0_attn_Wv"], np.float32)
    Wb0 = np.asarray(inputs["L0_attn_Wb"], np.float32)
    Wo0 = np.asarray(inputs["L0_attn_Wo"], np.float32)
    cq = f32(inputs["L0_attn_conv_q_w"])
    ck = f32(inputs["L0_attn_conv_k_w"])
    cv = f32(inputs["L0_attn_conv_v_w"])
    fs = f32(inputs["L0_attn_fir_short_w"]).reshape(1024, 5)
    fl = f32(inputs["L0_attn_fir_long_w"]).reshape(1024, 64)
    onw = f32(inputs["L0_attn_o_norm_w"])
    for hp in range(2):
        sl = slice(hp * 512, (hp + 1) * 512)
        g[f"l0_WqL{hp}"] = b16(w_lay0(Wq0[:, sl]))
        g[f"l0_WkL{hp}"] = b16(w_lay0(Wk0[:, sl]))
        g[f"l0_WvL{hp}"] = b16(w_lay0(Wv0[:, sl]))
        g[f"l0_WbL{hp}"] = b16(w_lay0(Wb0[:, 2 * hp:2 * hp + 2]))
        g[f"l0_dconv{hp}"] = np.concatenate([
            diag_stack(cq[sl], 4)[None], diag_stack(ck[sl], 4)[None],
            diag_stack(cv[sl], 4)[None]], axis=0)
        g[f"l0_dfirS{hp}"] = diag_stack(fs[sl], 5)
        g[f"l0_dfirL{hp}"] = diag_stack(fl[sl], 64)
        g[f"l0_WoL{hp}"] = b16(w_lay0(Wo0[sl, :]))
        ch = (np.arange(512) % 256)
        g[f"l0_onw{hp}"] = np.ascontiguousarray(
            onw[ch].reshape(4, 128).T.astype(np.float32))
    g["l0_w1hsL"] = b16(w_lay0(np.asarray(inputs["L0_attn_mlp_w1"], np.float32)[:D]))
    g["l0_w1st"] = b16(inputs["L0_attn_mlp_w1"][D:])
    g["l0_b1"] = np.ascontiguousarray(
        f32(inputs["L0_attn_mlp_b1"]).reshape(8, 128).T)
    g["l0_w2L"] = b16(w_lay0(np.asarray(inputs["L0_attn_mlp_w2"], np.float32)))
    g["l0_b2"] = f32(inputs["L0_attn_mlp_b2"]).reshape(4, 1)
    g["l0_lt"] = f32(inputs["L0_attn_log_temp"]).reshape(1, 4)

    qw = np.asarray(inputs["L0_ffn_qweights"])
    g["l0f_qwT"] = np.ascontiguousarray(np.transpose(qw, (0, 2, 1)))
    sc = f32(inputs["L0_ffn_scales"])
    g["l0f_screpT"] = np.ascontiguousarray(
        np.transpose(np.repeat(sc, 16, axis=2), (0, 2, 1)))
    g["l0f_bias"] = np.ascontiguousarray(
        f32(inputs["L0_ffn_biases"]).reshape(6, 8, 128).transpose(0, 2, 1))
    laT = np.transpose(np.asarray(inputs["L0_ffn_lora_a"]), (0, 2, 1))
    g["l0f_laT"] = b16(laT.reshape(6, 8, 128, 32).transpose(0, 2, 1, 3).reshape(6, 128, 256))
    g["l0f_lbT"] = b16(np.transpose(np.asarray(inputs["L0_ffn_lora_b"]), (0, 2, 1)))
    g["l0f_gam"] = np.ascontiguousarray(
        f32(inputs["L0_ffn_ln_gamma"]).reshape(3, 8, 128).transpose(0, 2, 1))
    g["l0f_bet"] = np.ascontiguousarray(
        f32(inputs["L0_ffn_ln_beta"]).reshape(3, 8, 128).transpose(0, 2, 1))

    def w_lay(a):
        Dd, O = a.shape
        return np.ascontiguousarray(
            a.reshape(Dd // 128, 128, O).transpose(1, 0, 2).reshape(128, (Dd // 128) * O))
    g["l1_WqL"] = b16(w_lay(np.asarray(inputs["L1_attn_Wq"], np.float32).reshape(D, 1024)))
    g["l1_WkL"] = b16(w_lay(np.asarray(inputs["L1_attn_Wk"], np.float32).reshape(D, 1024)))
    g["l1_WvL"] = b16(w_lay(np.asarray(inputs["L1_attn_Wv"], np.float32).reshape(D, 1024)))
    g["l1_WoL"] = b16(w_lay(np.asarray(inputs["L1_attn_Wo"], np.float32).reshape(1024, D)))
    g["l1_projT"] = b16(np.asarray(inputs["L1_attn_proj"]).T)
    g["l1_W1L"] = b16(w_lay(np.asarray(inputs["L1_attn_W1"], np.float32)))
    g["l1_b1"] = np.ascontiguousarray(f32(inputs["L1_attn_b1"]).reshape(32, 128).T)
    g["l1_W2L"] = b16(w_lay(np.asarray(inputs["L1_attn_W2"], np.float32)))
    g["l1_b2"] = np.ascontiguousarray(f32(inputs["L1_attn_b2"]).reshape(8, 128).T)
    for nm, key in (("ln1_g", "l1_ln1g"), ("ln1_b", "l1_ln1b"),
                    ("ln2_g", "l1_ln2g"), ("ln2_b", "l1_ln2b")):
        g[key] = np.ascontiguousarray(f32(inputs["L1_attn_" + nm]).reshape(8, 128).T)

    gateT = f32(inputs["L1_ffn_gate_w"]).T
    g["m_gateTf"] = np.ascontiguousarray(
        gateT.reshape(8, 128, 8).transpose(1, 0, 2).reshape(128, 64))
    g["m_lb"] = f32(inputs["L1_ffn_lb_bias"]).reshape(8, 1)
    egT = np.transpose(np.asarray(inputs["L1_ffn_expert_gate_w"]), (0, 2, 1))
    euT = np.transpose(np.asarray(inputs["L1_ffn_expert_up_w"]), (0, 2, 1))
    edT = np.transpose(np.asarray(inputs["L1_ffn_expert_down_w"]), (0, 2, 1))
    sgT = np.asarray(inputs["L1_ffn_shared_gate_w"]).T
    suT = np.asarray(inputs["L1_ffn_shared_up_w"]).T
    sdT = np.asarray(inputs["L1_ffn_shared_down_w"]).T
    def up_lay(a):
        return a.reshape(8, 128, 512).transpose(1, 0, 2).reshape(128, 4096)
    def dn_lay(a):
        return a.reshape(4, 128, 1024).transpose(1, 0, 2).reshape(128, 4096)
    ex = np.zeros((10, 128, 3 * 4096), dtype=BF)
    for e in range(8):
        ex[e, :, 0:4096] = up_lay(egT[e]).astype(BF)
        ex[e, :, 4096:8192] = up_lay(euT[e]).astype(BF)
        ex[e, :, 8192:12288] = dn_lay(edT[e]).astype(BF)
    for h in range(2):
        ex[8 + h, :, 0:4096] = up_lay(sgT[:, h * 512:(h + 1) * 512]).astype(BF)
        ex[8 + h, :, 4096:8192] = up_lay(suT[:, h * 512:(h + 1) * 512]).astype(BF)
        ex[8 + h, :, 8192:12288] = dn_lay(sdT[h * 512:(h + 1) * 512, :]).astype(BF)
    g["m_EX"] = ex
    return g



def emit_tm_ln_to_fm(ctx, xin, xt, lnst, rows, sqf, hf, hb, pt2, ntile=NT,
                     tpc=TPC, eps=1e-5, hb_only=False, stageF=None, hook=None,
                     rowsave=None):
    for i in range(ntile):
        buf = i % 2
        xv = xt[:, buf * D:(buf + 1) * D]

        def _load(sy, i=i, xv=xv):
            sy.wait_ge(ctx.sem["tensor"].h, ctx.sem["tensor"].n)
            ctx.dma.inc(sy.dma_start(out=xv, in_=xin[i * 128:(i + 1) * 128, :]), 16)
        ctx.on("sync", _load)
        dn = ctx.dma.n

        def _stats_s(s, xv=xv, dn=dn, vn_prev=ctx.sem["vector"].n):
            s.wait_ge(ctx.dma.h, dn)
            s.wait_ge(ctx.sem["vector"].h, vn_prev)
            ctx.sem["scalar"].inc(s.activation(sqf[:], xv, AF.Square))
        ctx.on("scalar", _stats_s)
        sqn = ctx.sem["scalar"].n

        def _rowmath_v(v, xv=xv, dn=dn, sn_prev=ctx.sem["scalar"].n, sqn=sqn):
            v.wait_ge(ctx.dma.h, dn)
            v.wait_ge(ctx.sem["scalar"].h, sn_prev)
            vchain(ctx, v, v.reduce_sum(rows[:, 0:1], xv, axis=AX))
            v.wait_ge(ctx.sem["scalar"].h, sqn)
            vchain(ctx, v, v.reduce_sum(rows[:, 1:2], sqf[:], axis=AX))
            vchain(ctx, v, v.tensor_scalar(rows[:, 2:3], rows[:, 0:1], ctx.c(1.0 / D), None, ALU.mult))
            vchain(ctx, v, v.tensor_scalar(rows[:, 3:4], rows[:, 1:2], ctx.c(1.0 / D), None, ALU.mult))
            vchain(ctx, v, v.tensor_tensor(rows[:, 4:5], rows[:, 2:3], rows[:, 2:3], ALU.mult))
            vchain(ctx, v, v.tensor_tensor(rows[:, 3:4], rows[:, 3:4], rows[:, 4:5], ALU.subtract))
            ctx.sem["vector"].inc(
                v.tensor_scalar(rows[:, 7:8], rows[:, 3:4], ctx.c(eps), None, ALU.add))
        ctx.on("vector", _rowmath_v)
        vn = ctx.sem["vector"].n

        def _rowmath_s(s, vn=vn):
            s.wait_ge(ctx.sem["vector"].h, vn)
            ctx.sem["scalar"].inc(
                s.activation(rows[:, 5:6], rows[:, 7:8], AF.Sqrt))
        ctx.on("scalar", _rowmath_s)
        sn = ctx.sem["scalar"].n

        def _apply_v(v, xv=xv, sn=sn, _i=i):
            v.wait_ge(ctx.sem["scalar"].h, sn)
            vchain(ctx, v, v.reciprocal(rows[:, 6:7], rows[:, 5:6]))
            vchain(ctx, v, v.tensor_tensor(rows[:, 4:5], rows[:, 6:7], rows[:, 6:7], ALU.mult))
            vchain(ctx, v, v.tensor_tensor(rows[:, 4:5], rows[:, 4:5], rows[:, 7:8], ALU.mult))
            vchain(ctx, v, v.tensor_scalar(rows[:, 4:5], rows[:, 4:5], ctx.c(-0.5), ctx.c(1.5),
                            ALU.mult, ALU.add))
            vchain(ctx, v, v.tensor_tensor(rows[:, 6:7], rows[:, 6:7], rows[:, 4:5], ALU.mult))
            if rowsave is not None:
                v.tensor_copy(rowsave[:, 2 * _i:2 * _i + 1], rows[:, 2:3])
                v.tensor_copy(rowsave[:, 2 * _i + 1:2 * _i + 2], rows[:, 6:7])
            ctx.sem["vector"].inc(
                v.tensor_scalar(lnst[:], xv, rows[:, 2:3], rows[:, 6:7],
                                ALU.subtract, ALU.mult))
        ctx.on("vector", _apply_v)
        vn = ctx.sem["vector"].n

        def _tr(te, vn=vn):
            te.wait_ge(ctx.sem["vector"].h, vn)
            te.wait_ge(ctx.sem["vector"].h, vn)
            for c in range(NDC):
                inst = te.transpose(pt2[:, c * 128:(c + 1) * 128],
                                    lnst[:, c * 128:(c + 1) * 128], ctx.ident[:])
            ctx.sem["tensor"].inc(inst)
        ctx.on("tensor", _tr)
        tn = ctx.sem["tensor"].n

        def _evac(v, i=i, tn=tn):
            v.wait_ge(ctx.sem["tensor"].h, tn)
            src = pt2[:].rearrange("p (c t) -> p c t", c=NDC)
            if not hb_only:
                dstf = hf[:].rearrange("p (c t) -> p c t", c=NDC)[:, :, i * 128:(i + 1) * 128]
                v.tensor_copy(dstf, src)
            if stageF is not None:
                v.tensor_copy(stageF[:], pt2[:])
            dstb = hb[:].rearrange("p (c t) -> p c t", c=NDC)[:, :, i * 128:(i + 1) * 128]
            ctx.sem["vector"].inc(v.tensor_copy(dstb, src))
        ctx.on("vector", _evac)
        if hook is not None:
            hook(i)


def emit_fm_to_tm_add(ctx, xin, xout, hf, xt, ost, pt2, ntile=NT, tpc=TPC):
    for i in range(ntile):
        buf = i % 2
        xv = xt[:, buf * D:(buf + 1) * D]

        def _load(sy, i=i, xv=xv):
            sy.wait_ge(ctx.sem["vector"].h, ctx.sem["vector"].n)
            ctx.dma.inc(sy.dma_start(out=xv, in_=xin[i * 128:(i + 1) * 128, :]), 16)
        ctx.on("sync", _load)
        dn = ctx.dma.n

        def _tr(te, i=i):
            te.wait_ge(ctx.sem["vector"].h, ctx.sem["vector"].n)
            hfv = hf[:].rearrange("p (c t) -> p c t", c=NDC)
            for c in range(NDC):
                inst = te.transpose(pt2[:, c * 128:(c + 1) * 128],
                                    hfv[:, c, i * 128:(i + 1) * 128], ctx.ident[:])
            ctx.sem["tensor"].inc(inst)
        ctx.on("tensor", _tr)
        tn = ctx.sem["tensor"].n

        def _add(v, tn=tn, dn=dn, xv=xv):
            v.wait_ge(ctx.sem["tensor"].h, tn)
            v.wait_ge(ctx.dma.h, dn)
            ctx.sem["vector"].inc(v.tensor_tensor(ost[:], pt2[:], xv, ALU.add))
        ctx.on("vector", _add)
        vn = ctx.sem["vector"].n

        def _store(sy, i=i, vn=vn):
            sy.wait_ge(ctx.sem["vector"].h, vn)
            ctx.dmo.inc(sy.dma_start(out=xout[i * 128:(i + 1) * 128, :], in_=ost[:]), 16)
        ctx.on("sync", _store)
        dmo_n = ctx.dmo.n

        def _guard(v, dmo_n=dmo_n):
            v.wait_ge(ctx.dmo.h, dmo_n)
        ctx.on("vector", _guard)



def emit_l0f(ctx, xin, xout, qwT, screpT, bias6, laT, lbT, gam, bet,
             stop_after=None, dbg=None):
    ctx.const_pool([1.0 / D, 1e-5, 0.0, -0.5, 1.5])
    ctx.emit_init()

    xt = ctx.sbuf("xt", [128, 2 * D], F32)
    lnst = ctx.sbuf("lnst", [128, D], F32)
    hf = ctx.sbuf("hf", [128, NDC * TPC], F32)
    hb = ctx.sbuf("hb", [128, NDC * TPC], BF16)
    yb = ctx.sbuf("yb", [128, NDC * TPC], BF16)
    W = ctx.sbuf("W", [128, NDC * D], BF16)
    wstage = ctx.sbuf("wstage", [128, D], I32)
    wf = ctx.sbuf("wf", [128, D], F32)
    wsc = ctx.sbuf("wsc", [128, D], F32)
    la = ctx.sbuf("la", [128, NDC * 32], BF16)
    lbt = ctx.sbuf("lbt", [32, D], BF16)
    rb = ctx.sbuf("rb", [32, TPC], BF16)
    bcol = ctx.sbuf("bcol", [128, 8], F32)
    gcol = ctx.sbuf("gcol", [128, 8], F32)
    bbcol = ctx.sbuf("bbcol", [128, 8], F32)
    rows = ctx.sbuf("rows", [128, 8], F32)
    rws = ctx.sbuf("rws", [1, 6 * 512], F32)
    sq = ctx.sbuf("sq", [128, 512], BF16)
    sqf = ctx.sbuf("sqf", [128, D], F32)
    tmp = ctx.sbuf("tmpf", [128, 512], F32)

    pm = [ctx.psum(f"pm{i}", [128, 512]) for i in range(2)]
    pb = ctx.psum("pb", [128, 1024])
    prow = [ctx.psum(f"prow{i}", [1, 512]) for i in range(2)]
    pt2 = ctx.psum("pt2", [128, 1024])

    def _dump(tag, tile, width):
        if dbg is not None and tag in dbg:
            ctx.barrier()
            def _st(sy, tile=tile, ap=dbg[tag], width=width):
                ctx.dmo.inc(sy.dma_start(out=ap, in_=tile[:, 0:width]), 16)
            ctx.on("sync", _st)

    emit_tm_ln_to_fm(ctx, xin, xt, lnst, rows, sqf, hf, hb, pt2)
    ctx.barrier()
    _dump("p1_hf", hf, NDC * TPC)
    _dump("p1_hb", hb, NDC * TPC)
    if stop_after == "p1":
        return

    for blk in range(3):
        emit_qlora(ctx, blk * 2, hb, yb, qwT, screpT, bias6, laT, lbT,
                   W, wstage, wf, wsc, la, lbt, rb, bcol, pm, "bias_only")
        ctx.barrier()
        if blk == 0:
            _dump("q0_yb", yb, NDC * TPC)
            _dump("q0_W", W, NDC * D)
            _dump("q0_rb", rb, TPC)
        if stop_after == f"q{blk}a":
            return
        emit_fm_ln_gelu(ctx, blk, yb, gam, bet, gcol, bbcol, rws, sq, tmp, prow, pb)
        ctx.barrier()
        if blk == 0:
            _dump("ln0_yb", yb, NDC * TPC)
        if stop_after == f"q{blk}ln":
            return
        emit_qlora(ctx, blk * 2 + 1, yb, None, qwT, screpT, bias6, laT, lbT,
                   W, wstage, wf, wsc, la, lbt, rb, bcol, pm, "residual",
                   hf=hf, hb=hb)
        ctx.barrier()
        if blk == 0:
            _dump("r0_hf", hf, NDC * TPC)
        if stop_after == f"q{blk}b":
            return

    emit_fm_to_tm_add(ctx, xin, xout, hf, xt, lnst, pt2)


def emit_qlora(ctx, m, src, dst, qwT, screpT, bias6, laT, lbT, W, wstage, wf,
               wsc, la, lbt, rb, bcol, pm, evac, hf=None, hb=None):
    def _small(sy, m=m):
        ctx.dma.inc(sy.dma_start(out=la[:], in_=laT[m]), 16)
        ctx.dma.inc(sy.dma_start(out=lbt[:], in_=lbT[m]), 16)
        ctx.dma.inc(sy.dma_start(out=bcol[:], in_=bias6[m]), 16)
    ctx.on("sync", _small)

    for c in range(NDC):
        def _ld(sy, m=m, c=c):
            sy.wait_ge(ctx.sem["vector"].h, ctx.sem["vector"].n)
            ctx.dma.inc(sy.dma_start(out=wstage[:], in_=qwT[m, c * 128:(c + 1) * 128, :]), 16)
            ctx.dma.inc(sy.dma_start(out=wsc[:], in_=screpT[m, c * 128:(c + 1) * 128, :]), 16)
        ctx.on("sync", _ld)
        dn = ctx.dma.n

        def _deq_s(s, dn=dn):
            s.wait_ge(ctx.dma.h, dn)
            ctx.sem["scalar"].inc(
                s.activation(wf[:], wstage[:], AF.Copy, bias=-1.0, scale=1.0 / 7.5))
        ctx.on("scalar", _deq_s)
        sn = ctx.sem["scalar"].n

        def _deq_v(v, c=c, sn=sn, dn=dn):
            v.wait_ge(ctx.sem["scalar"].h, sn)
            v.wait_ge(ctx.dma.h, dn)
            ctx.sem["vector"].inc(
                v.tensor_tensor(W[:, c * D:(c + 1) * D], wf[:], wsc[:], ALU.mult))
        ctx.on("vector", _deq_v)
    ctx.barrier()

    for t in range(NFT):
        def _r(te, t=t):
            for k in range(NDC):
                inst = te.matmul(pm[0][0:32, :], la[:, k * 32:(k + 1) * 32],
                                 src[:, k * TPC + t * 512: k * TPC + t * 512 + 512],
                                 start=(k == 0), stop=(k == NDC - 1))
            ctx.sem["tensor"].inc(inst)
        ctx.on("tensor", _r)
        tn = ctx.sem["tensor"].n

        def _rev(v, t=t, tn=tn):
            v.wait_ge(ctx.sem["tensor"].h, tn)
            ctx.sem["vector"].inc(v.tensor_copy(rb[:, t * 512:(t + 1) * 512], pm[0][0:32, :]))
        ctx.on("vector", _rev)
        vn = ctx.sem["vector"].n
        ctx.on("tensor", lambda te, vn=vn: te.wait_ge(ctx.sem["vector"].h, vn))

    bank_guard = [None, None]
    for t in range(NFT):
        for oc in range(NDC):
            bank = (t * NDC + oc) % 2

            def _mm(te, t=t, oc=oc, bank=bank, guard=bank_guard[bank]):
                if guard is not None:
                    te.wait_ge(guard[0].h, guard[1])
                for k in range(NDC):
                    te.matmul(pm[bank][:], W[:, k * D + oc * 128: k * D + oc * 128 + 128],
                              src[:, k * TPC + t * 512: k * TPC + t * 512 + 512],
                              start=(k == 0), stop=False)
                inst = te.matmul(pm[bank][:], lbt[:, oc * 128:(oc + 1) * 128],
                                 rb[:, t * 512:(t + 1) * 512], start=False, stop=True)
                ctx.sem["tensor"].inc(inst)
            ctx.on("tensor", _mm)
            tn = ctx.sem["tensor"].n

            if evac == "bias_only":
                def _ev(v, t=t, oc=oc, bank=bank, tn=tn):
                    v.wait_ge(ctx.sem["tensor"].h, tn)
                    ctx.sem["vector"].inc(v.tensor_scalar(
                        dst[:, oc * TPC + t * 512: oc * TPC + t * 512 + 512],
                        pm[bank][:], bcol[:, oc:oc + 1], None, ALU.add))
                ctx.on("vector", _ev)
                bank_guard[bank] = (ctx.sem["vector"], ctx.sem["vector"].n)
            else:
                def _ev(v, t=t, oc=oc, bank=bank, tn=tn):
                    v.wait_ge(ctx.sem["tensor"].h, tn)
                    reg = hf[:, oc * TPC + t * 512: oc * TPC + t * 512 + 512]
                    ctx.sem["vector"].inc(v.scalar_tensor_tensor(
                        reg, pm[bank][:], bcol[:, oc:oc + 1], reg, ALU.add, ALU.add))
                ctx.on("vector", _ev)
                vn = ctx.sem["vector"].n
                bank_guard[bank] = (ctx.sem["vector"], vn)

                def _cast(s, t=t, oc=oc, vn=vn):
                    s.wait_ge(ctx.sem["vector"].h, vn)
                    sl = slice(oc * TPC + t * 512, oc * TPC + t * 512 + 512)
                    ctx.sem["scalar"].inc(s.activation(hb[:, sl], hf[:, sl], AF.Copy))
                ctx.on("scalar", _cast)


def emit_fm_ln_gelu(ctx, blk, yb, gam, bet, gcol, bbcol, rws, sq, tmp, prow, pb,
                    eps=1e-5):
    def _ldgb(sy, blk=blk):
        ctx.dma.inc(sy.dma_start(out=gcol[:], in_=gam[blk]), 16)
        ctx.dma.inc(sy.dma_start(out=bbcol[:], in_=bet[blk]), 16)
    ctx.on("sync", _ldgb)
    gbn = ctx.dma.n

    for t in range(NFT):
        for k in range(NDC):
            def _sqk(s, t=t, k=k, prev_tn=ctx.sem["tensor"].n):
                s.wait_ge(ctx.sem["tensor"].h, prev_tn)
                ctx.sem["scalar"].inc(s.activation(
                    sq[:], yb[:, k * TPC + t * 512: k * TPC + t * 512 + 512], AF.Square))
            ctx.on("scalar", _sqk)
            sn = ctx.sem["scalar"].n

            def _mmk(te, t=t, k=k, sn=sn):
                te.wait_ge(ctx.sem["scalar"].h, sn)
                te.matmul(prow[0][:], ctx.onesk[:],
                          yb[:, k * TPC + t * 512: k * TPC + t * 512 + 512],
                          start=(k == 0), stop=(k == NDC - 1), skip_group_check=True)
                inst = te.matmul(prow[1][:], ctx.onesk[:], sq[:],
                                 start=(k == 0), stop=(k == NDC - 1), skip_group_check=True)
                ctx.sem["tensor"].inc(inst)
            ctx.on("tensor", _mmk)
        tn = ctx.sem["tensor"].n

        def _rows_v(v, tn=tn, gbn=gbn):
            v.wait_ge(ctx.sem["tensor"].h, tn)
            v.wait_ge(ctx.dma.h, gbn)
            s0 = rws[:, 0:512]
            mrow = rws[:, 2048:2560]
            vrow = rws[:, 2560:3072]
            vchain(ctx, v, v.tensor_scalar(mrow, prow[0][:], ctx.c(1.0 / D)[0:1, :], None, ALU.mult))
            vchain(ctx, v, v.tensor_scalar(vrow, prow[1][:], ctx.c(1.0 / D)[0:1, :], None, ALU.mult))
            vchain(ctx, v, v.tensor_tensor(s0, mrow, mrow, ALU.mult))
            vchain(ctx, v, v.tensor_tensor(vrow, vrow, s0, ALU.subtract))
            ctx.sem["vector"].inc(v.tensor_scalar(
                rws[:, 512:1024], vrow, ctx.c(eps)[0:1, :], None, ALU.add))
        ctx.on("vector", _rows_v)
        vn = ctx.sem["vector"].n

        def _rows_s(s, vn=vn):
            s.wait_ge(ctx.sem["vector"].h, vn)
            ctx.sem["scalar"].inc(s.activation(
                rws[:, 0:512], rws[:, 512:1024], AF.Sqrt))
        ctx.on("scalar", _rows_s)
        sn = ctx.sem["scalar"].n

        def _rows_v2(v, sn=sn):
            v.wait_ge(ctx.sem["scalar"].h, sn)
            r1 = rws[:, 1024:1536]
            r2 = rws[:, 1536:2048]
            t = rws[:, 0:512]
            vchain(ctx, v, v.reciprocal(r1, t))
            vchain(ctx, v, v.tensor_tensor(t, r1, r1, ALU.mult))
            vchain(ctx, v, v.tensor_tensor(t, t, rws[:, 512:1024], ALU.mult))
            vchain(ctx, v, v.tensor_scalar(t, t, ctx.c(-0.5)[0:1, :], ctx.c(1.5)[0:1, :],
                            ALU.mult, ALU.add))
            vchain(ctx, v, v.tensor_tensor(r1, r1, t, ALU.mult))
            ctx.sem["vector"].inc(v.tensor_tensor(r2, rws[:, 2048:2560], r1, ALU.mult))
        ctx.on("vector", _rows_v2)
        vn = ctx.sem["vector"].n

        def _bc(te, vn=vn):
            te.wait_ge(ctx.sem["vector"].h, vn)
            te.matmul(pb[:, 0:512], ctx.onesc[:], rws[:, 1024:1536], start=True, stop=True)
            inst = te.matmul(pb[:, 512:1024], ctx.onesc[:], rws[:, 1536:2048],
                             start=True, stop=True)
            ctx.sem["tensor"].inc(inst)
        ctx.on("tensor", _bc)
        tn = ctx.sem["tensor"].n

        for k in range(NDC):
            def _ap_v(v, t=t, k=k, tn=tn):
                v.wait_ge(ctx.sem["tensor"].h, tn)
                sl = slice(k * TPC + t * 512, k * TPC + t * 512 + 512)
                v.tensor_tensor(tmp[:], yb[:, sl], pb[:, 0:512], ALU.mult)
                v.tensor_tensor(tmp[:], tmp[:], pb[:, 512:1024], ALU.subtract)
                ctx.sem["vector"].inc(v.tensor_scalar(
                    tmp[:], tmp[:], gcol[:, k:k + 1], bbcol[:, k:k + 1], ALU.mult, ALU.add))
            ctx.on("vector", _ap_v)
            vn = ctx.sem["vector"].n

            def _ap_s(s, t=t, k=k, vn=vn):
                s.wait_ge(ctx.sem["vector"].h, vn)
                sl = slice(k * TPC + t * 512, k * TPC + t * 512 + 512)
                ctx.sem["scalar"].inc(s.activation(yb[:, sl], tmp[:], AF.Gelu))
            ctx.on("scalar", _ap_s)
            sn = ctx.sem["scalar"].n
            ctx.on("vector", lambda v, sn=sn: v.wait_ge(ctx.sem["scalar"].h, sn))


def build_l0f(stop_after=None, dbg_tags=()):
    nc = bass.Bass("TRN2", target_bir_lowering=False, debug=False)
    xin = nc.dram_tensor("xin", [TPC, D], F32, kind="ExternalInput").ap()
    qwT = nc.dram_tensor("qwT", [6, D, D], I32, kind="ExternalInput").ap()
    screpT = nc.dram_tensor("screpT", [6, D, D], F32, kind="ExternalInput").ap()
    bias6 = nc.dram_tensor("bias6", [6, 128, 8], F32, kind="ExternalInput").ap()
    laT = nc.dram_tensor("laT", [6, 128, 256], BF16, kind="ExternalInput").ap()
    lbT = nc.dram_tensor("lbT", [6, 32, D], BF16, kind="ExternalInput").ap()
    gam = nc.dram_tensor("gam", [3, 128, 8], F32, kind="ExternalInput").ap()
    bet = nc.dram_tensor("bet", [3, 128, 8], F32, kind="ExternalInput").ap()
    xout = nc.dram_tensor("xout", [TPC, D], F32, kind="ExternalOutput").ap()
    shapes = {"p1_hb": [128, NDC * TPC], "p1_hf": [128, NDC * TPC], "q0_yb": [128, NDC * TPC],
              "q0_W": [128, NDC * D], "q0_rb": [32, TPC],
              "ln0_yb": [128, NDC * TPC], "r0_hf": [128, NDC * TPC]}
    dtypes = {"p1_hb": BF16, "p1_hf": F32, "q0_yb": BF16, "q0_W": BF16, "q0_rb": BF16,
              "ln0_yb": BF16, "r0_hf": F32}
    dbg = {t: nc.dram_tensor(t, shapes[t], dtypes[t], kind="ExternalOutput").ap()
           for t in dbg_tags}
    with ExitStack() as es:
        ctx = Ctx(nc, es)
        emit_l0f(ctx, xin, xout, qwT, screpT, bias6, laT, lbT, gam, bet,
                 stop_after=stop_after, dbg=dbg)
        ctx.on("sync", lambda sy: sy.wait_ge(ctx.dmo.h, ctx.dmo.n))
    return nc


def run_l0f(x1_full, g, stop_after=None, dbg_tags=()):
    nc = build_l0f(stop_after, dbg_tags)
    in_maps = []
    for core in range(NCORES):
        b, j = core // 2, core % 2
        in_maps.append({
            "xin": np.ascontiguousarray(x1_full[b, j * TPC:(j + 1) * TPC]),
            "qwT": g["l0f_qwT"], "screpT": g["l0f_screpT"], "bias6": g["l0f_bias"],
            "laT": g["l0f_laT"], "lbT": g["l0f_lbT"],
            "gam": g["l0f_gam"], "bet": g["l0f_bet"],
        })
    res = run_bass_kernel_spmd(nc, in_maps, core_ids=list(range(NCORES)))
    out = np.empty((B, L, D), np.float32)
    for core in range(NCORES):
        b, j = core // 2, core % 2
        out[b, j * TPC:(j + 1) * TPC] = res.results[core]["xout"]
    if dbg_tags:
        return out, res
    return out






def emit_l1f(ctx, xin, xout, EX, gateTf, lbrow, cc_in, cc_out):
    ctx.const_pool([1.0 / D, 1e-5, 0.0, -0.5, 1.5, -1e9, 1e-8, 1.0])
    ctx.emit_init()

    xt = ctx.sbuf("xt", [128, 2 * D], F32)
    lnst = ctx.sbuf("lnst", [128, D], F32)
    rows = ctx.sbuf("rows", [128, 8], F32)
    sqf = ctx.sbuf("sqf", [128, D], F32)
    hb = ctx.sbuf("hb", [128, NDC * TPC], BF16)
    acc = ctx.sbuf("acc", [128, NT * D], F32)
    stageF = ctx.sbuf("stageF", [128, D], F32)
    gtw = ctx.sbuf("gtw", [128, 64], F32)
    lgf = ctx.sbuf("lgf", [8, TPC], F32)
    lbr = ctx.sbuf("lbr", [8, 1], F32)
    sct = ctx.sbuf("sct", [128, NT * 8], F32)
    scl = ctx.sbuf("scl", [128, NT * 8], F32)
    msk = ctx.sbuf("msk", [128, NT * 8], F32)
    mwt = ctx.sbuf("mwt", [128, NT * 8], F32)
    srow = ctx.sbuf("srow", [128, 16], F32)
    wrow = ctx.sbuf("wrow", [1, 16], F32)
    Wx = [ctx.sbuf(f"Wx{i}", [128, 3 * 4096], BF16) for i in range(2)]
    ig = ctx.sbuf("ig", [128, 4 * TPC], BF16)
    t1 = ctx.sbuf("t1", [128, 512], F32)

    pt2 = ctx.psum("pt2", [128, 1024])
    pg = ctx.psum("pg", [128, 512])
    pu = ctx.psum("pu", [128, 512])
    pd = [ctx.psum(f"pd{i}", [128, 512]) for i in range(2)]
    psm = ctx.psum("psm", [128, 512])

    def _ldr(sy):
        ctx.dma.inc(sy.dma_start(out=gtw[:], in_=gateTf), 16)
        ctx.dma.inc(sy.dma_start(out=lbr[:], in_=lbrow), 16)
    ctx.on("sync", _ldr)
    rn = ctx.dma.n

    def hook(i):
        vn = ctx.sem["vector"].n

        def _gmm(te, vn=vn):
            te.wait_ge(ctx.sem["vector"].h, vn)
            te.wait_ge(ctx.dma.h, rn)
            for c in range(NDC):
                inst = te.matmul(psm[0:8, 0:128], gtw[:, c * 8:(c + 1) * 8],
                                 stageF[:, c * 128:(c + 1) * 128],
                                 start=(c == 0), stop=(c == NDC - 1))
            ctx.sem["tensor"].inc(inst)
        ctx.on("tensor", _gmm)
        tn = ctx.sem["tensor"].n

        def _gev(v, i=i, tn=tn):
            v.wait_ge(ctx.sem["tensor"].h, tn)
            ctx.sem["vector"].inc(
                v.tensor_copy(lgf[:, i * 128:(i + 1) * 128], psm[0:8, 0:128]))
        ctx.on("vector", _gev)
        vn2 = ctx.sem["vector"].n
        ctx.on("tensor", lambda te, vn2=vn2: te.wait_ge(ctx.sem["vector"].h, vn2))

    emit_tm_ln_to_fm(ctx, xin, xt, lnst, rows, sqf, None, hb, pt2,
                     hb_only=True, stageF=stageF, hook=hook)
    ctx.barrier()

    def _lb(v):
        ctx.sem["vector"].inc(v.tensor_scalar(lgf[:], lgf[:], lbr[:], None, ALU.add))
    ctx.on("vector", _lb)
    ctx.barrier()
    for i in range(NT):
        def _trl(te, i=i):
            te.wait_ge(ctx.sem["scalar"].h, ctx.sem["scalar"].n)
            ctx.sem["tensor"].inc(te.transpose(
                pt2[:, 0:8], lgf[:, i * 128:(i + 1) * 128], ctx.ident[0:8, 0:8]))
        ctx.on("tensor", _trl)
        tn = ctx.sem["tensor"].n

        def _sig(s, i=i, tn=tn):
            s.wait_ge(ctx.sem["tensor"].h, tn)
            ctx.sem["scalar"].inc(s.activation(
                sct[:, i * 8:(i + 1) * 8], pt2[:, 0:8], AF.Sigmoid))
        ctx.on("scalar", _sig)
    ctx.barrier()

    for i in range(NT):
        def _mask(v, i=i):
            sl = slice(i * 8, (i + 1) * 8)
            sc = sct[:, sl]
            vchain(ctx, v, v.reduce_max(srow[:, 0:1], sc, axis=AX))
            vchain(ctx, v, v.tensor_scalar(msk[:, sl], sc, srow[:, 0:1], None, ALU.is_equal))
            vchain(ctx, v, v.scalar_tensor_tensor(mwt[:, sl], msk[:, sl], ctx.c(-1e9), sc,
                                                  ALU.mult, ALU.add))
            vchain(ctx, v, v.reduce_max(srow[:, 1:2], mwt[:, sl], axis=AX))
            vchain(ctx, v, v.tensor_scalar(mwt[:, sl], mwt[:, sl], srow[:, 1:2], None, ALU.is_equal))
            vchain(ctx, v, v.tensor_tensor(srow[:, 2:3], srow[:, 0:1], srow[:, 1:2], ALU.add))
            vchain(ctx, v, v.tensor_scalar(srow[:, 2:3], srow[:, 2:3], ctx.c(1e-8), None, ALU.add))
            vchain(ctx, v, v.reciprocal(srow[:, 3:4], srow[:, 2:3]))
            vchain(ctx, v, v.tensor_tensor(srow[:, 4:5], srow[:, 0:1], srow[:, 3:4], ALU.mult))
            vchain(ctx, v, v.tensor_tensor(srow[:, 5:6], srow[:, 1:2], srow[:, 3:4], ALU.mult))
            vchain(ctx, v, v.tensor_scalar(srow[:, 6:7], srow[:, 4:5], ctx.c(0.0), None, ALU.max))
            vchain(ctx, v, v.tensor_scalar(mwt[:, sl], mwt[:, sl], srow[:, 5:6], None, ALU.mult))
            vchain(ctx, v, v.scalar_tensor_tensor(mwt[:, sl], msk[:, sl], srow[:, 4:5],
                                                  mwt[:, sl], ALU.mult, ALU.add))
            ctx.sem["vector"].inc(v.tensor_scalar(
                msk[:, sl], mwt[:, sl], ctx.c(0.0), None, ALU.is_gt))
        ctx.on("vector", _mask)
    ctx.barrier()

    for i in range(NT):
        def _sums(te, i=i, vn=ctx.sem["vector"].n):
            if i == 0:
                te.wait_ge(ctx.sem["vector"].h, vn)
            te.matmul(psm[0:1, 0:8], ctx.oneskf[:], msk[:, i * 8:(i + 1) * 8],
                      start=(i == 0), stop=(i == NT - 1), skip_group_check=True)
            inst = te.matmul(psm[0:1, 8:16], ctx.oneskf[:], mwt[:, i * 8:(i + 1) * 8],
                             start=(i == 0), stop=(i == NT - 1), skip_group_check=True)
            if i == NT - 1:
                ctx.sem["tensor"].inc(inst)
        ctx.on("tensor", _sums)
    tn = ctx.sem["tensor"].n

    def _wev(v, tn=tn):
        v.wait_ge(ctx.sem["tensor"].h, tn)
        ctx.sem["vector"].inc(v.tensor_copy(wrow[:], psm[0:1, 0:16]))
    ctx.on("vector", _wev)
    vn = ctx.sem["vector"].n

    def _st_cc(sy, vn=vn):
        sy.wait_ge(ctx.sem["vector"].h, vn)
        ctx.dma.inc(sy.dma_start(out=cc_in, in_=wrow[:]), 16)
    ctx.on("sync", _st_cc)
    dn = ctx.dma.n

    def _cc(g, dn=dn):
        g.wait_ge(ctx.dma.h, dn)
        ctx.cc.inc(g.collective_compute("AllReduce", ALU.add,
                                        [[0, 1, 2, 3, 4, 5, 6, 7]],
                                        [cc_in.opt()], [cc_out.opt()]))
    ctx.on("gpsimd", _cc)
    ccn = ctx.cc.n

    def _ld_cc(sy, ccn=ccn):
        sy.wait_ge(ctx.cc.h, ccn)
        ctx.dma.inc(sy.dma_start(out=wrow[:], in_=cc_out), 16)
    ctx.on("sync", _ld_cc)
    dn = ctx.dma.n

    def _wm(v, dn=dn):
        v.wait_ge(ctx.dma.h, dn)
        vchain(ctx, v, v.tensor_scalar(wrow[:, 0:8], wrow[:, 0:8],
                                       ctx.c(1.0)[0:1, :], None, ALU.max))
        vchain(ctx, v, v.reciprocal(wrow[:, 0:8], wrow[:, 0:8]))
        ctx.sem["vector"].inc(
            v.tensor_tensor(wrow[:, 0:8], wrow[:, 8:16], wrow[:, 0:8], ALU.mult))
    ctx.on("vector", _wm)
    vn = ctx.sem["vector"].n

    def _bcw(te, vn=vn):
        te.wait_ge(ctx.sem["vector"].h, vn)
        ctx.sem["tensor"].inc(te.matmul(psm[:, 0:8], ctx.onesc[:], wrow[:, 0:8],
                                        start=True, stop=True))
    ctx.on("tensor", _bcw)
    tn = ctx.sem["tensor"].n

    for i in range(NT):
        def _scl(v, i=i, tn=tn):
            v.wait_ge(ctx.sem["tensor"].h, tn)
            sl = slice(i * 8, (i + 1) * 8)
            ctx.sem["vector"].inc(
                v.tensor_tensor(scl[:, sl], msk[:, sl], psm[:, 0:8], ALU.mult))
        ctx.on("vector", _scl)

    def _z(g):
        ctx.sem["gpsimd"].inc(g.memset(acc[:], 0.0))
    ctx.on("gpsimd", _z)
    ctx.barrier()

    wbuf_guard = [None, None]
    for e in range(10):
        wbuf = Wx[e % 2]

        def _ldw(sy, e=e, wbuf=wbuf, guard=wbuf_guard[e % 2]):
            if guard is not None:
                sy.wait_ge(guard[0].h, guard[1])
            ctx.dma.inc(sy.dma_start(out=wbuf[:], in_=EX[e]), 16)
        ctx.on("sync", _ldw)
        dn = ctx.dma.n
        Wg, Wu, Wd = wbuf[:, 0:4096], wbuf[:, 4096:8192], wbuf[:, 8192:12288]

        for t in range(NFT):
            for oc in range(4):
                def _gu(te, t=t, oc=oc, dn=dn, Wg=Wg, Wu=Wu,
                        vg=ctx.sem["vector"].n, sg=ctx.sem["scalar"].n):
                    te.wait_ge(ctx.dma.h, dn)
                    te.wait_ge(ctx.sem["vector"].h, vg)
                    te.wait_ge(ctx.sem["scalar"].h, sg)
                    for k in range(NDC):
                        te.matmul(pg[:], Wg[:, k * 512 + oc * 128: k * 512 + oc * 128 + 128],
                                  hb[:, k * TPC + t * 512: k * TPC + t * 512 + 512],
                                  start=(k == 0), stop=(k == NDC - 1), skip_group_check=True)
                    for k in range(NDC):
                        inst = te.matmul(pu[:], Wu[:, k * 512 + oc * 128: k * 512 + oc * 128 + 128],
                                         hb[:, k * TPC + t * 512: k * TPC + t * 512 + 512],
                                         start=(k == 0), stop=(k == NDC - 1), skip_group_check=True)
                    ctx.sem["tensor"].inc(inst)
                ctx.on("tensor", _gu)
                tn = ctx.sem["tensor"].n

                def _silu(s, tn=tn):
                    s.wait_ge(ctx.sem["tensor"].h, tn)
                    ctx.sem["scalar"].inc(s.activation(t1[:], pg[:], AF.Silu))
                ctx.on("scalar", _silu)
                sn = ctx.sem["scalar"].n

                def _ig(v, t=t, oc=oc, sn=sn, tn=tn):
                    v.wait_ge(ctx.sem["scalar"].h, sn)
                    v.wait_ge(ctx.sem["tensor"].h, tn)
                    sl = slice(oc * TPC + t * 512, oc * TPC + t * 512 + 512)
                    ctx.sem["vector"].inc(v.tensor_tensor(ig[:, sl], t1[:], pu[:], ALU.mult))
                ctx.on("vector", _ig)

            ig_vn = ctx.sem["vector"].n
            for tcol in range(4):
                for half in range(2):
                    def _dn(te, e=e, t=t, tcol=tcol, half=half, dn=dn, Wd=Wd,
                            ig_vn=ig_vn, vg=ctx.sem["vector"].n):
                        te.wait_ge(ctx.sem["vector"].h, max(ig_vn, vg))
                        for c in range(4):
                            inst = te.matmul(
                                pd[half][:],
                                ig[:, c * TPC + t * 512 + tcol * 128:
                                   c * TPC + t * 512 + tcol * 128 + 128],
                                Wd[:, c * 1024 + half * 512: c * 1024 + half * 512 + 512],
                                start=(c == 0), stop=(c == 3), skip_group_check=True)
                        ctx.sem["tensor"].inc(inst)
                    ctx.on("tensor", _dn)
                    tn = ctx.sem["tensor"].n

                    def _acc(v, e=e, t=t, tcol=tcol, half=half, tn=tn):
                        v.wait_ge(ctx.sem["tensor"].h, tn)
                        ti = t * 4 + tcol
                        reg = acc[:, ti * D + half * 512: ti * D + half * 512 + 512]
                        if e < 8:
                            ctx.sem["vector"].inc(v.scalar_tensor_tensor(
                                reg, pd[half][:], scl[:, ti * 8 + e: ti * 8 + e + 1],
                                reg, ALU.mult, ALU.add))
                        else:
                            ctx.sem["vector"].inc(
                                v.tensor_tensor(reg, reg, pd[half][:], ALU.add))
                    ctx.on("vector", _acc)
        wbuf_guard[e % 2] = (ctx.sem["tensor"], ctx.sem["tensor"].n)
    ctx.barrier()

    for i in range(NT):
        buf = i % 2
        xv = xt[:, buf * D:(buf + 1) * D]

        def _load(sy, i=i, xv=xv, guard=ctx.dmo.n):
            sy.wait_ge(ctx.dmo.h, guard)
            ctx.dma.inc(sy.dma_start(out=xv, in_=xin[i * 128:(i + 1) * 128, :]), 16)
        ctx.on("sync", _load)
        dn = ctx.dma.n

        def _add(v, i=i, dn=dn, xv=xv):
            v.wait_ge(ctx.dma.h, dn)
            ctx.sem["vector"].inc(v.tensor_tensor(
                acc[:, i * D:(i + 1) * D], acc[:, i * D:(i + 1) * D], xv, ALU.add))
        ctx.on("vector", _add)
        vn = ctx.sem["vector"].n

        def _store(sy, i=i, vn=vn):
            sy.wait_ge(ctx.sem["vector"].h, vn)
            ctx.dmo.inc(sy.dma_start(out=xout[i * 128:(i + 1) * 128, :],
                                     in_=acc[:, i * D:(i + 1) * D]), 16)
        ctx.on("sync", _store)


def build_l1f():
    nc = bass.Bass("TRN2", target_bir_lowering=False, debug=False)
    xin = nc.dram_tensor("xin", [TPC, D], F32, kind="ExternalInput").ap()
    EX = nc.dram_tensor("EX", [10, 128, 3 * 4096], BF16, kind="ExternalInput").ap()
    gateTf = nc.dram_tensor("gateTf", [128, 64], F32, kind="ExternalInput").ap()
    lbrow = nc.dram_tensor("lbrow", [8, 1], F32, kind="ExternalInput").ap()
    xout = nc.dram_tensor("xout", [TPC, D], F32, kind="ExternalOutput").ap()
    cc_in = nc.dram_tensor("cc_in", [1, 16], F32, kind="Internal").ap()
    cc_out = nc.dram_tensor("cc_out", [1, 16], F32, kind="Internal").ap()
    with ExitStack() as es:
        ctx = Ctx(nc, es)
        emit_l1f(ctx, xin, xout, EX, gateTf, lbrow, cc_in, cc_out)
        ctx.on("sync", lambda sy: sy.wait_ge(ctx.dmo.h, ctx.dmo.n))
    return nc


def run_l1f(x3_full, g):
    nc = build_l1f()
    in_maps = []
    for core in range(NCORES):
        b, j = core // 2, core % 2
        in_maps.append({
            "xin": np.ascontiguousarray(x3_full[b, j * TPC:(j + 1) * TPC]),
            "EX": g["m_EX"], "gateTf": g["m_gateTf"], "lbrow": g["m_lb"],
        })
    res = run_bass_kernel_spmd(nc, in_maps, core_ids=list(range(NCORES)))
    out = np.empty((B, L, D), np.float32)
    for core in range(NCORES):
        b, j = core // 2, core % 2
        out[b, j * TPC:(j + 1) * TPC] = res.results[core]["xout"]
    return out





LB = 4096
NTT = 32
NFT8 = 8
VST = 4160
PST = 4100


def emit_l0a(ctx, xin, xhalf, xout, WqL, WkL, WvL, WbL, dconv, dfirS, dfirL,
             WoL, onwc, w1hsL, w1st, w2L, b1c, b2r, ltr, rs_in, rs_out,
             stop_after=None, dbg=None):
    SW = ctx.sem["vector"]; SS = ctx.sem["scalar"]; ST = ctx.sem["tensor"]
    ctx.const_pool([1.0 / D, 1.0 / 256.0, 1e-5, 1e-6, 0.0, -0.5, 1.5, 1.0,
                    0.05, 0.8, -1.0])
    ctx.emit_init()

    scr = ctx.sbuf("scr", [128, D], F32)
    lnst = scr[:]
    rows = ctx.sbuf("rows", [128, 12], F32)
    rowstats = ctx.sbuf("rowstats", [128, 2 * NTT], F32)
    hs = ctx.sbuf("hs", [128, NDC * LB], BF16)
    A1 = ctx.sbuf("A1", [128, 4 * PST], BF16)
    A2 = ctx.sbuf("A2", [128, 4 * LB], BF16)
    A3 = ctx.sbuf("A3", [128, 4 * LB], BF16)
    A4 = ctx.sbuf("A4", [128, 4 * VST], BF16)
    delta = A1[:]
    beta_tm = ctx.sbuf("beta_tm", [128, 2 * NTT], F32)
    probs = ctx.sbuf("probs", [128, 8 * NTT], F32)
    dbuf = [ctx.sbuf(f"dbuf{i}", [128, 128], BF16) for i in range(2)]
    Wbb = ctx.sbuf("Wbb", [128, 16], BF16)
    rkq = ctx.sbuf("rkq", [128, 8], F32)
    w2sb = ctx.sbuf("w2sb", [128, 32], BF16)
    b1sb = ctx.sbuf("b1sb", [128, 8], F32)
    b2sb = ctx.sbuf("b2sb", [4, 1], F32)
    ltsb = ctx.sbuf("ltsb", [1, 4], F32)
    rtau = ctx.sbuf("rtau", [128, 2], F32)
    onws = ctx.sbuf("onws", [128, 4], F32)
    bsb = scr[0:2, 0:512]
    Ssb = ctx.sbuf("Ssb", [128, 1024], F32)
    Sb16 = ctx.sbuf("Sb16", [128, 1024], BF16)
    A2F = A2[:].bitcast(F32)
    A1F = A1[:].bitcast(F32)
    xt = A2F[:, 0:2048]
    xtg = A1F[:, 0:2048]
    Wp = A4[:, 64:4160]
    hsF = hs[:].bitcast(F32)
    w1hsb = hs[:, 0:8192]
    WoS = hs[:, 8192:12288]
    hsf = hs[:, 12288:16384]
    s_sh = hsF[:, 8192:8704]
    lgsb = hsF[0:4, 8704:9728]
    rrow = hsF[0:1, 8704:8960]
    pstg = hsF[:, 9728:10240]
    h1t = hs[:, 20480:20992]
    statsf = hs[0:16, 21504:22528]
    sqf2 = hs[:, 22528:23552]
    otm = hsF[:, 11776:12288]
    xtf = hsF[:, 12288:13312]
    sqs = hs[:, 26624:26880]
    stt = hsF[:, 13440:13456]
    SLm = hs[:, 26912:27040]
    SUm = hs[:, 27040:27168]
    IUm = hs[:, 27168:27296]
    Atile = hs[:, 27296:27424]
    Ptile = hs[:, 27424:27552]
    Qtile = hs[:, 27552:27680]
    Ctile = hs[:, 27680:27808]
    RKm = hs[:, 27808:27936]
    RKm2 = hs[:, 27936:28064]
    altl = hs[:, 28064:28192]
    khtm = hs[:, 28192:28448]
    qhtm = hs[:, 28448:28704]
    kbtm = hs[:, 28704:28960]
    vtm = hs[:, 28960:29216]
    usb = hs[:, 29216:29472]
    upsb = hs[:, 29472:29728]
    wsb = hs[:, 29728:29984]
    ofm = hs[:, 29984:30496]
    w1sb = hs[0:16, 30496:31520]

    pt2 = ctx.psum("pt2", [128, 1024])
    ptb = pt2[:].bitcast(BF16)
    pA = ctx.psum("pA", [128, 512])
    pB = ctx.psum("pB", [128, 512])
    pC = ctx.psum("pC", [128, 512])
    pD = ctx.psum("pD", [128, 512])
    pl = [ctx.psum(f"pl{i}", [4, 512]) for i in range(2)]
    banks4 = [pA, pB, pC, pD]

    def _msk(g):
        g.memset(Ssb[:], 0.0)
        g.memset(Sb16[:], 0.0)
        for c in range(4):
            g.memset(A4[:, c * VST: c * VST + 64], 0.0)
            g.memset(A1[:, c * PST: c * PST + 4], 0.0)
        ctx.sem["gpsimd"].inc(g.tensor_copy(Sb16[:, 0:16], Sb16[:, 0:16]))
    ctx.on("gpsimd", _msk)

    def _ldp(sy):
        for dst, srcap in ((w1sb, w1st), (w2sb, w2L), (b1sb, b1c), (b2sb, b2r),
                           (ltsb, ltr), (onws, onwc), (WoS, WoL), (w1hsb, w1hsL)):
            ctx.dma.inc(sy.dma_start(out=dst[:], in_=srcap), 16)
    ctx.on("sync", _ldp)
    ctx.barrier()

    def _tau_s(s):
        SS.inc(s.activation(ltsb[:, 2:4], ltsb[:, 0:2], AF.Exp))
    ctx.on("scalar", _tau_s)
    sn = SS.n

    def _tau_v(v, sn=sn):
        v.wait_ge(SS.h, sn)
        SW.inc(v.reciprocal(ltsb[:, 0:2], ltsb[:, 2:4]))
    ctx.on("vector", _tau_v)
    vn = SW.n

    def _tau_bc(te, vn=vn):
        te.wait_ge(SW.h, vn)
        te.matmul(pt2[:, 0:1], ctx.onesc[:], ltsb[:, 0:1], start=True, stop=True)
        inst = te.matmul(pt2[:, 1:2], ctx.onesc[:], ltsb[:, 1:2], start=True, stop=True)
        ST.inc(inst)
    ctx.on("tensor", _tau_bc)
    tn = ST.n

    def _tau_ev(v, tn=tn):
        v.wait_ge(ST.h, tn)
        SW.inc(v.tensor_copy(rtau[:], pt2[:, 0:2]))
    ctx.on("vector", _tau_ev)
    vn = SW.n

    ctx.barrier()

    for i in range(NTT):
        buf = i % 2
        xv = xt[:, buf * D:(buf + 1) * D]

        def _load(sy, i=i, xv=xv, guard=SW.n):
            sy.wait_ge(SW.h, guard)
            ctx.dma.inc(sy.dma_start(out=xv, in_=xin[i * 128:(i + 1) * 128, :]), 16)
        ctx.on("sync", _load)
        dn = ctx.dma.n

        def _sq(s, xv=xv, dn=dn, vg=SW.n):
            s.wait_ge(ctx.dma.h, dn)
            s.wait_ge(SW.h, vg)
            SS.inc(s.activation(sqf2, xv, AF.Square))
        ctx.on("scalar", _sq)
        sn = SS.n

        def _rm(v, i=i, xv=xv, dn=dn, sn=sn):
            v.wait_ge(ctx.dma.h, dn)
            vchain(ctx, v, v.reduce_sum(rows[:, 0:1], xv, axis=AX))
            v.wait_ge(SS.h, sn)
            vchain(ctx, v, v.reduce_sum(rows[:, 1:2], sqf2, axis=AX))
            vchain(ctx, v, v.tensor_scalar(rows[:, 2:3], rows[:, 0:1], ctx.c(1.0 / D), None, ALU.mult))
            vchain(ctx, v, v.tensor_scalar(rows[:, 3:4], rows[:, 1:2], ctx.c(1.0 / D), None, ALU.mult))
            vchain(ctx, v, v.tensor_tensor(rows[:, 4:5], rows[:, 2:3], rows[:, 2:3], ALU.mult))
            vchain(ctx, v, v.tensor_tensor(rows[:, 3:4], rows[:, 3:4], rows[:, 4:5], ALU.subtract))
            SW.inc(v.tensor_scalar(rows[:, 7:8], rows[:, 3:4], ctx.c(1e-5), None, ALU.add))
        ctx.on("vector", _rm)
        vn = SW.n

        def _sr(s, vn=vn):
            s.wait_ge(SW.h, vn)
            SS.inc(s.activation(rows[:, 5:6], rows[:, 7:8], AF.Sqrt))
        ctx.on("scalar", _sr)
        sn = SS.n

        def _nw(v, i=i, sn=sn):
            v.wait_ge(SS.h, sn)
            vchain(ctx, v, v.reciprocal(rows[:, 6:7], rows[:, 5:6]))
            vchain(ctx, v, v.tensor_tensor(rows[:, 4:5], rows[:, 6:7], rows[:, 6:7], ALU.mult))
            vchain(ctx, v, v.tensor_tensor(rows[:, 4:5], rows[:, 4:5], rows[:, 7:8], ALU.mult))
            vchain(ctx, v, v.tensor_scalar(rows[:, 4:5], rows[:, 4:5], ctx.c(-0.5), ctx.c(1.5), ALU.mult, ALU.add))
            vchain(ctx, v, v.tensor_tensor(rows[:, 6:7], rows[:, 6:7], rows[:, 4:5], ALU.mult))
            v.tensor_copy(rowstats[:, 2 * i:2 * i + 1], rows[:, 2:3])
            SW.inc(v.tensor_copy(rowstats[:, 2 * i + 1:2 * i + 2], rows[:, 6:7]))
        ctx.on("vector", _nw)
    ctx.barrier()

    def hs_tile(i, dst, dst_off, dst_stride, xtbuf=None, nbuf=2):
        xtbuf = xt if xtbuf is None else xtbuf
        buf = i % nbuf
        xv = xtbuf[:, buf * D:(buf + 1) * D]

        def _load(sy, i=i, xv=xv, guard=ST.n):
            sy.wait_ge(ST.h, guard)
            ctx.dma.inc(sy.dma_start(out=xv, in_=xin[i * 128:(i + 1) * 128, :]), 16)
        ctx.on("sync", _load)
        dn = ctx.dma.n

        def _ap(v, i=i, xv=xv, dn=dn):
            v.wait_ge(ctx.dma.h, dn)
            SW.inc(v.tensor_scalar(lnst[:], xv, rowstats[:, 2 * i:2 * i + 1],
                                   rowstats[:, 2 * i + 1:2 * i + 2],
                                   ALU.subtract, ALU.mult))
        ctx.on("vector", _ap)
        vn = SW.n

        def _tr(te, vn=vn):
            te.wait_ge(SW.h, vn)
            for c in range(NDC):
                inst = te.transpose(pt2[:, c * 128:(c + 1) * 128],
                                    lnst[:, c * 128:(c + 1) * 128], ctx.ident[:])
            ST.inc(inst)
        ctx.on("tensor", _tr)
        tn = ST.n

        def _ev(v, tn=tn, dst=dst, dst_off=dst_off, dst_stride=dst_stride):
            v.wait_ge(ST.h, tn)
            for c in range(NDC):
                inst = v.tensor_copy(
                    dst[:, dst_off + c * dst_stride: dst_off + c * dst_stride + 128],
                    pt2[:, c * 128:(c + 1) * 128])
            SW.inc(inst)
        ctx.on("vector", _ev)

    for i in range(NTT):
        hs_tile(i, hs, i * 128, LB)
    ctx.barrier()

    def proj_conv(WL, pidx, dst, dst_stride, dst_off):
        def _ldw(sy, WL=WL, guard=ST.n):
            sy.wait_ge(ST.h, guard)
            ctx.dma.inc(sy.dma_start(out=Wp, in_=WL), 16)
        ctx.on("sync", _ldw)
        dnw = ctx.dma.n
        for t in range(NFT8):
            for oc in range(4):
                bank = banks4[(t * 4 + oc) % 4]

                def _mm(te, t=t, oc=oc, bank=bank, dnw=dnw, vg=SW.n):
                    te.wait_ge(SW.h, vg)
                    te.wait_ge(ctx.dma.h, dnw)
                    for k in range(NDC):
                        inst = te.matmul(
                            bank[:], Wp[:, k * 512 + oc * 128: k * 512 + oc * 128 + 128],
                            hs[:, k * LB + t * 512: k * LB + t * 512 + 512],
                            start=(k == 0), stop=(k == NDC - 1), skip_group_check=True)
                    ST.inc(inst)
                ctx.on("tensor", _mm)
                tn = ST.n

                def _ev(v, t=t, oc=oc, bank=bank, tn=tn):
                    v.wait_ge(ST.h, tn)
                    SW.inc(v.tensor_copy(
                        A1[:, oc * PST + 4 + t * 512: oc * PST + 4 + t * 512 + 512],
                        bank[:]))
                ctx.on("vector", _ev)
        ctx.barrier()
        for c in range(4):
            for t in range(NFT8):
                bank = banks4[t % 4]
                for tap in range(4):
                    db = dbuf[tap % 2]

                    def _ld(sy, pidx=pidx, tap=tap, c=c, db=db, guard=ST.n):
                        sy.wait_ge(ST.h, guard)
                        ctx.dma.inc(sy.dma_start(out=db[:], in_=dconv[pidx, tap, c]), 16)
                    ctx.on("sync", _ld)
                    dn = ctx.dma.n

                    def _cm(te, c=c, t=t, tap=tap, bank=bank, db=db, dn=dn, sg=SS.n):
                        te.wait_ge(ctx.dma.h, dn)
                        if tap == 0:
                            te.wait_ge(SS.h, sg)
                        ST.inc(te.matmul(
                            bank[:], db[:],
                            A1[:, c * PST + 1 + tap + t * 512: c * PST + 1 + tap + t * 512 + 512],
                            start=(tap == 0), stop=(tap == 3), skip_group_check=True))
                    ctx.on("tensor", _cm)
                tn = ST.n

                def _sv(s, c=c, t=t, bank=bank, tn=tn, dst=dst,
                        dst_stride=dst_stride, dst_off=dst_off):
                    s.wait_ge(ST.h, tn)
                    SS.inc(s.activation(
                        dst[:, c * dst_stride + dst_off + t * 512:
                            c * dst_stride + dst_off + t * 512 + 512],
                        bank[:], AF.Silu))
                ctx.on("scalar", _sv)
        ctx.barrier()

    def _dump(tag, ap_src):
        if dbg is not None and tag in dbg:
            ctx.barrier()
            def _st(sy, ap_src=ap_src, apd=dbg[tag]):
                ctx.dmo.inc(sy.dma_start(out=apd, in_=ap_src), 16)
            ctx.on("sync", _st)

    proj_conv(WqL, 0, A2, LB, 0)
    proj_conv(WkL, 1, A3, LB, 0)
    proj_conv(WvL, 2, A4, VST, 64)
    _dump("qh", A2[:])
    _dump("vh", A4[:])

    def _ldwb(sy, guard=ST.n):
        sy.wait_ge(ST.h, guard)
        ctx.dma.inc(sy.dma_start(out=Wbb[:], in_=WbL), 16)
    ctx.on("sync", _ldwb)
    dnb = ctx.dma.n
    for t in range(NFT8):
        def _bm(te, t=t, dnb=dnb, vg=SW.n):
            te.wait_ge(SW.h, vg)
            te.wait_ge(ctx.dma.h, dnb)
            for k in range(NDC):
                inst = te.matmul(pl[0][0:2, :], Wbb[:, k * 2: k * 2 + 2],
                                 hs[:, k * LB + t * 512: k * LB + t * 512 + 512],
                                 start=(k == 0), stop=(k == NDC - 1), skip_group_check=True)
            ST.inc(inst)
        ctx.on("tensor", _bm)
        tn = ST.n

        def _bs(s, tn=tn):
            s.wait_ge(ST.h, tn)
            SS.inc(s.activation(bsb[:], pl[0][0:2, :], AF.Sigmoid))
        ctx.on("scalar", _bs)
        sn = SS.n
        for q4 in range(4):
            def _bt(te, q4=q4, sn=sn, vg=SW.n):
                te.wait_ge(SS.h, sn)
                te.wait_ge(SW.h, vg)
                ST.inc(te.transpose(pt2[:, 0:2], bsb[:, q4 * 128:(q4 + 1) * 128],
                                    ctx.ident[0:2, 0:2]))
            ctx.on("tensor", _bt)
            tn2 = ST.n

            def _bv(v, t=t, q4=q4, tn2=tn2):
                v.wait_ge(ST.h, tn2)
                i = t * 4 + q4
                SW.inc(v.tensor_copy(beta_tm[:, 2 * i:2 * i + 2], pt2[:, 0:2]))
            ctx.on("vector", _bv)
    ctx.barrier()
    _dump("beta", beta_tm[:])
    if stop_after == "pb":
        return

    def _msk2(g):
        g.memset(SLm, 1.0)
        g.affine_select(out=SLm, in_=SLm, compare_op=ALU.is_gt, fill=0.0,
                        base=0, pattern=[[-1, 128]], channel_multiplier=1)
        g.memset(SUm, 0.0)
        g.affine_select(out=SUm, in_=SUm, compare_op=ALU.is_ge, fill=1.0,
                        base=0, pattern=[[-1, 128]], channel_multiplier=1)
        g.memset(IUm, 0.0)
        g.affine_select(out=IUm, in_=IUm, compare_op=ALU.is_gt, fill=1.0,
                        base=0, pattern=[[-1, 128]], channel_multiplier=1)
        ctx.sem["gpsimd"].inc(g.tensor_copy(SLm, SLm))
    ctx.on("gpsimd", _msk2)
    ctx.barrier()

    for i in range(NTT):
        for h in range(2):
            def qsl(cc, i=i, h=h):
                return A2[:, (2 * h + cc) * LB + i * 128: (2 * h + cc) * LB + (i + 1) * 128]

            def ksl(cc, i=i, h=h):
                return A3[:, (2 * h + cc) * LB + i * 128: (2 * h + cc) * LB + (i + 1) * 128]

            def vsl(cc, i=i, h=h):
                return A4[:, (2 * h + cc) * VST + 64 + i * 128: (2 * h + cc) * VST + 64 + (i + 1) * 128]

            def Sh(cc, h=h):
                return Ssb[:, (2 * h + cc) * 256: (2 * h + cc) * 256 + 256]

            def Sb(cc, h=h):
                return Sb16[:, (2 * h + cc) * 256: (2 * h + cc) * 256 + 256]

            def _tr1(te, i=i, h=h, vg=SW.n):
                te.wait_ge(SW.h, vg)
                for cc in range(2):
                    te.transpose(ptb[:, cc * 128:(cc + 1) * 128], ksl(cc), ctx.identb[:])
                for cc in range(2):
                    te.transpose(ptb[:, 256 + cc * 128: 256 + (cc + 1) * 128], qsl(cc), ctx.identb[:])
                for cc in range(2):
                    inst = te.transpose(ptb[:, 512 + cc * 128: 512 + (cc + 1) * 128], vsl(cc), ctx.identb[:])
                ST.inc(inst)
            ctx.on("tensor", _tr1)
            tn = ST.n

            def _ev1(v, i=i, h=h, tn=tn):
                v.wait_ge(ST.h, tn)
                v.tensor_copy(khtm, ptb[:, 0:256])
                v.tensor_copy(qhtm, ptb[:, 256:512])
                SW.inc(v.tensor_scalar(vtm, ptb[:, 512:768],
                                       beta_tm[:, 2 * i + h:2 * i + h + 1], None, ALU.mult))
            ctx.on("vector", _ev1)
            vn = SW.n

            def _sq2(s, vn=vn):
                s.wait_ge(SW.h, vn)
                SS.inc(s.activation(sqs[:], khtm[:], AF.Square))
            ctx.on("scalar", _sq2)
            sn = SS.n

            def _rk(v, i=i, h=h, sn=sn):
                v.wait_ge(SS.h, sn)
                vchain(ctx, v, v.reduce_sum(rows[:, 8:9], sqs[:], axis=AX))
                vchain(ctx, v, v.tensor_scalar(rows[:, 8:9], rows[:, 8:9], ctx.c(1e-6), None, ALU.add))
                SW.inc(v.engine_nop())
            ctx.on("vector", _rk)
            vn = SW.n

            def _rks(s, vn=vn):
                s.wait_ge(SW.h, vn)
                SS.inc(s.activation(rows[:, 9:10], rows[:, 8:9], AF.Sqrt))
            ctx.on("scalar", _rks)
            sn = SS.n

            def _rkn(v, i=i, h=h, sn=sn):
                v.wait_ge(SS.h, sn)
                vchain(ctx, v, v.reciprocal(rows[:, 10:11], rows[:, 9:10]))
                vchain(ctx, v, v.tensor_tensor(rows[:, 11:12], rows[:, 10:11], rows[:, 10:11], ALU.mult))
                vchain(ctx, v, v.tensor_tensor(rows[:, 11:12], rows[:, 11:12], rows[:, 8:9], ALU.mult))
                vchain(ctx, v, v.tensor_scalar(rows[:, 11:12], rows[:, 11:12], ctx.c(-0.5), ctx.c(1.5), ALU.mult, ALU.add))
                vchain(ctx, v, v.tensor_tensor(rkq[:, 0:1], rows[:, 10:11], rows[:, 11:12], ALU.mult))
                vchain(ctx, v, v.tensor_tensor(rkq[:, 2:3], rkq[:, 0:1], beta_tm[:, 2 * i + h:2 * i + h + 1], ALU.mult))
                vchain(ctx, v, v.tensor_scalar(rkq[:, 3:4], rkq[:, 2:3], ctx.c(-1.0), None, ALU.mult))
                vchain(ctx, v, v.tensor_scalar(kbtm[:], khtm[:], rkq[:, 2:3], None, ALU.mult))
                SW.inc(v.tensor_scalar(khtm[:], khtm[:], rkq[:, 0:1], None, ALU.mult))
            ctx.on("vector", _rkn)
            vn = SW.n

            def _sq3(s, vn=vn):
                s.wait_ge(SW.h, vn)
                SS.inc(s.activation(sqs[:], qhtm[:], AF.Square))
            ctx.on("scalar", _sq3)
            sn = SS.n

            def _rq(v, sn=sn):
                v.wait_ge(SS.h, sn)
                vchain(ctx, v, v.reduce_sum(rows[:, 8:9], sqs[:], axis=AX))
                vchain(ctx, v, v.tensor_scalar(rows[:, 8:9], rows[:, 8:9], ctx.c(1e-6), None, ALU.add))
                SW.inc(v.engine_nop())
            ctx.on("vector", _rq)
            vn = SW.n

            def _rqs(s, vn=vn):
                s.wait_ge(SW.h, vn)
                SS.inc(s.activation(rows[:, 9:10], rows[:, 8:9], AF.Sqrt))
            ctx.on("scalar", _rqs)
            sn = SS.n

            def _rqn(v, sn=sn):
                v.wait_ge(SS.h, sn)
                vchain(ctx, v, v.reciprocal(rows[:, 10:11], rows[:, 9:10]))
                vchain(ctx, v, v.tensor_tensor(rows[:, 11:12], rows[:, 10:11], rows[:, 10:11], ALU.mult))
                vchain(ctx, v, v.tensor_tensor(rows[:, 11:12], rows[:, 11:12], rows[:, 8:9], ALU.mult))
                vchain(ctx, v, v.tensor_scalar(rows[:, 11:12], rows[:, 11:12], ctx.c(-0.5), ctx.c(1.5), ALU.mult, ALU.add))
                SW.inc(v.tensor_tensor(rkq[:, 1:2], rows[:, 10:11], rows[:, 11:12], ALU.mult))
            ctx.on("vector", _rqn)
            vn = SW.n

            def _trr(te, vn=vn):
                te.wait_ge(SW.h, vn)
                te.transpose(pt2[0:1, 0:128], rkq[:, 0:1], ctx.ident[:])
                ST.inc(te.transpose(pt2[0:1, 128:256], rkq[:, 2:3], ctx.ident[:]))
            ctx.on("tensor", _trr)
            tn = ST.n

            def _evr(v, tn=tn):
                v.wait_ge(ST.h, tn)
                SW.inc(v.tensor_copy(rrow[:], pt2[0:1, 0:256]))
            ctx.on("vector", _evr)
            vn = SW.n

            def _bcr(te, vn=vn):
                te.wait_ge(SW.h, vn)
                te.matmul(pt2[:, 0:128], ctx.onesc[:], rrow[:, 0:128], start=True, stop=True)
                ST.inc(te.matmul(pt2[:, 128:256], ctx.onesc[:], rrow[:, 128:256], start=True, stop=True))
            ctx.on("tensor", _bcr)
            tn = ST.n

            def _mkm(v, tn=tn):
                v.wait_ge(ST.h, tn)
                v.tensor_tensor(RKm[:], SLm[:], pt2[:, 0:128], ALU.mult)
                SW.inc(v.tensor_tensor(RKm2[:], SUm[:], pt2[:, 128:256], ALU.mult))
            ctx.on("vector", _mkm)
            vn = SW.n

            def _gm(te, i=i, h=h, vn=vn):
                te.wait_ge(SW.h, vn)
                for cc in range(2):
                    inst = te.matmul(pA[:, 0:128], ksl(cc), ksl(cc),
                                     start=(cc == 0), stop=(cc == 1), skip_group_check=True)
                ST.inc(inst)
            ctx.on("tensor", _gm)
            tn = ST.n

            def _mka(v, tn=tn):
                v.wait_ge(ST.h, tn)
                vchain(ctx, v, v.tensor_scalar(Atile[:], pA[:, 0:128], rkq[:, 3:4], None, ALU.mult))
                vchain(ctx, v, v.tensor_tensor(Atile[:], Atile[:], RKm[:], ALU.mult))
                vchain(ctx, v, v.tensor_scalar(rows[:, 8:9], rkq[:, 0:1], ctx.c(-1.0), None, ALU.mult))
                vchain(ctx, v, v.tensor_scalar(Qtile[:], pA[:, 0:128], rows[:, 8:9], None, ALU.mult))
                vchain(ctx, v, v.tensor_tensor(Qtile[:], Qtile[:], RKm2[:], ALU.mult))
                vchain(ctx, v, v.tensor_copy(Ptile[:], Atile[:]))
                SW.inc(v.tensor_tensor(Ctile[:], Qtile[:], ctx.identb[:], ALU.add))
            ctx.on("vector", _mka)
            vn = SW.n

            for it in range(6):
                def _dbl1(te, it=it, vg=SW.n):
                    te.wait_ge(SW.h, vg)
                    te.matmul(pB[:, 0:128], Qtile, Ptile, start=True, stop=True, skip_group_check=True)
                    if it < 5:
                        ST.inc(te.matmul(pC[:, 0:128], Ptile, Qtile, start=True, stop=True, skip_group_check=True))
                    else:
                        ST.inc(te.matmul(pC[:, 0:128], Ptile, Qtile, start=True, stop=True, skip_group_check=True))
                ctx.on("tensor", _dbl1)
                tn = ST.n

                def _dev1(v, it=it, tn=tn):
                    v.wait_ge(ST.h, tn)
                    vchain(ctx, v, v.tensor_copy(Ptile, pB[:, 0:128]))
                    SW.inc(v.tensor_copy(Qtile, pC[:, 0:128]))
                ctx.on("vector", _dev1)
                vn = SW.n

                def _dbl2(te, vn=vn):
                    te.wait_ge(SW.h, vn)
                    ST.inc(te.matmul(pD[:, 0:128], Ptile, Ctile, start=True, stop=True, skip_group_check=True))
                ctx.on("tensor", _dbl2)
                tn = ST.n

                def _dev2(v, tn=tn):
                    v.wait_ge(ST.h, tn)
                    SW.inc(v.tensor_tensor(Ctile, Ctile, pD[:, 0:128], ALU.add))
                ctx.on("vector", _dev2)
            vn = SW.n

            def _uw(te, vn=vn):
                te.wait_ge(SW.h, vn)
                te.matmul(pA[:, 0:256], Ctile[:], vtm[:], start=True, stop=True, skip_group_check=True)
                te.matmul(pB[:, 0:128], kbtm[:, 0:128], Ctile[:], start=True, stop=True, skip_group_check=True)
                ST.inc(te.matmul(pB[:, 128:256], kbtm[:, 128:256], Ctile[:], start=True, stop=True, skip_group_check=True))
            ctx.on("tensor", _uw)
            tn = ST.n

            def _uwev(v, tn=tn):
                v.wait_ge(ST.h, tn)
                v.tensor_copy(usb[:], pA[:, 0:256])
                SW.inc(v.tensor_scalar(wsb[:], pB[:, 0:256], ctx.c(-1.0), None, ALU.mult))
            ctx.on("vector", _uwev)
            vn = SW.n

            def _al(te, i=i, h=h, vn=vn):
                te.wait_ge(SW.h, vn)
                for cc in range(2):
                    inst = te.matmul(pC[:, 0:128], ksl(cc), qsl(cc),
                                     start=(cc == 0), stop=(cc == 1), skip_group_check=True)
                ST.inc(inst)
            ctx.on("tensor", _al)
            tn = ST.n

            def _alev(v, tn=tn):
                v.wait_ge(ST.h, tn)
                vchain(ctx, v, v.tensor_scalar(altl[:], pC[:, 0:128], rkq[:, 0:1], None, ALU.mult))
                SW.inc(v.tensor_tensor(altl[:], altl[:], IUm[:], ALU.mult))
            ctx.on("vector", _alev)
            vn = SW.n

            def _up(te, h=h, vn=vn):
                te.wait_ge(SW.h, vn)
                te.matmul(pD[:, 0:256], ctx.identb[:], usb[:], start=True, stop=False, skip_group_check=True)
                te.matmul(pD[:, 0:256], wsb[:, 0:128], Sb(0), start=False, stop=False, skip_group_check=True)
                ST.inc(te.matmul(pD[:, 0:256], wsb[:, 128:256], Sb(1), start=False, stop=True, skip_group_check=True))
            ctx.on("tensor", _up)
            tn = ST.n

            def _upev(v, tn=tn):
                v.wait_ge(ST.h, tn)
                SW.inc(v.tensor_copy(upsb[:], pD[:, 0:256]))
            ctx.on("vector", _upev)
            vn = SW.n

            def _om(te, i=i, h=h, vn=vn):
                te.wait_ge(SW.h, vn)
                te.matmul(pA[:, 0:256], qsl(0), Sb(0), start=True, stop=False, skip_group_check=True)
                te.matmul(pA[:, 0:256], qsl(1), Sb(1), start=False, stop=False, skip_group_check=True)
                ST.inc(te.matmul(pA[:, 0:256], altl[:], upsb[:], start=False, stop=True, skip_group_check=True))
            ctx.on("tensor", _om)
            tn = ST.n

            def _oev(v, i=i, h=h, tn=tn):
                v.wait_ge(ST.h, tn)
                SW.inc(v.tensor_scalar(
                    delta[:, i * 512 + h * 256: i * 512 + h * 256 + 256],
                    pA[:, 0:256], rkq[:, 1:2], None, ALU.mult))
            ctx.on("vector", _oev)
            vn = SW.n

            def _su(te, vn=vn):
                te.wait_ge(SW.h, vn)
                te.matmul(pB[:, 0:256], khtm[:, 0:128], upsb[:], start=True, stop=True, skip_group_check=True)
                ST.inc(te.matmul(pC[:, 0:256], khtm[:, 128:256], upsb[:], start=True, stop=True, skip_group_check=True))
            ctx.on("tensor", _su)
            tn = ST.n

            def _sev(v, h=h, tn=tn):
                v.wait_ge(ST.h, tn)
                vchain(ctx, v, v.tensor_tensor(Sh(0), Sh(0), pB[:, 0:256], ALU.add))
                vchain(ctx, v, v.tensor_tensor(Sh(1), Sh(1), pC[:, 0:256], ALU.add))
                v.tensor_copy(Sb(0), Sh(0))
                SW.inc(v.tensor_copy(Sb(1), Sh(1)))
            ctx.on("vector", _sev)
    ctx.barrier()

    _dump("delta", delta[:, 0:16384])
    if stop_after == "pc":
        return

    def _ldw2p(sy):
        ctx.dma.inc(sy.dma_start(out=WoS, in_=WoL), 16)
        ctx.dma.inc(sy.dma_start(out=w1hsb, in_=w1hsL), 16)
        ctx.dma.inc(sy.dma_start(out=w1sb, in_=w1st), 16)
    ctx.on("sync", _ldw2p)
    dnw = ctx.dma.n

    def _wosc(v, dnw=dnw):
        v.wait_ge(ctx.dma.h, dnw)
        for c in range(4):
            inst = v.tensor_scalar(WoS[:, c * 1024:(c + 1) * 1024],
                                   WoS[:, c * 1024:(c + 1) * 1024],
                                   onws[:, c:c + 1], None, ALU.mult)
        SW.inc(inst)
    ctx.on("vector", _wosc)
    ctx.barrier()

    def fir(dstack, ntaps, dst):
        for c in range(4):
            for grp in range(2):
                for tap in range(ntaps):
                    db = dbuf[tap % 2]

                    def _ld(sy, tap=tap, c=c, db=db, dstack=dstack, guard=ST.n):
                        sy.wait_ge(ST.h, guard)
                        ctx.dma.inc(sy.dma_start(out=db[:], in_=dstack[tap, c]), 16)
                    ctx.on("sync", _ld)
                    dn = ctx.dma.n

                    def _fm(te, c=c, grp=grp, tap=tap, db=db, dn=dn, ntaps=ntaps, sg=SS.n):
                        te.wait_ge(ctx.dma.h, dn)
                        if tap == 0:
                            te.wait_ge(SS.h, sg)
                        for f4 in range(4):
                            t = grp * 4 + f4
                            inst = te.matmul(
                                banks4[f4][:], db[:],
                                A4[:, c * VST + 64 + t * 512 - (ntaps - 1) + tap:
                                   c * VST + 64 + t * 512 - (ntaps - 1) + tap + 512],
                                start=(tap == 0), stop=(tap == ntaps - 1), skip_group_check=True)
                        ST.inc(inst)
                    ctx.on("tensor", _fm)
                tn = ST.n

                def _fe(s, c=c, grp=grp, tn=tn, dst=dst):
                    s.wait_ge(ST.h, tn)
                    for f4 in range(4):
                        t = grp * 4 + f4
                        inst = s.activation(
                            dst[:, c * LB + t * 512: c * LB + t * 512 + 512],
                            banks4[f4][:], AF.Copy)
                    SS.inc(inst)
                ctx.on("scalar", _fe)
        ctx.barrier()

    fir(dfirS, 5, A2)
    fir(dfirL, 64, A3)
    _dump("short", A2[:])
    _dump("long", A3[:])
    if stop_after == "pd":
        return

    def stats_tile(i, q4):
        def _trs(te, i=i, vg=SW.n, sg=SS.n):
            te.wait_ge(SW.h, vg)
            te.wait_ge(SS.h, sg)
            for cc in range(4):
                te.transpose(ptb[:, cc * 128:(cc + 1) * 128],
                             A2[:, cc * LB + i * 128: cc * LB + (i + 1) * 128], ctx.identb[:])
            for cc in range(4):
                te.transpose(ptb[:, 512 + cc * 128: 512 + (cc + 1) * 128],
                             A3[:, cc * LB + i * 128: cc * LB + (i + 1) * 128], ctx.identb[:])
            for cc in range(4):
                inst = te.transpose(ptb[:, 1024 + cc * 128: 1024 + (cc + 1) * 128],
                                    A4[:, cc * VST + 64 + i * 128: cc * VST + 64 + (i + 1) * 128],
                                    ctx.identb[:])
            ST.inc(inst)
        ctx.on("tensor", _trs)
        tn = ST.n

        for h in range(2):
            srcs = [lambda h=h, i=i: ptb[:, h * 256: h * 256 + 256],
                    lambda h=h, i=i: ptb[:, 512 + h * 256: 512 + h * 256 + 256],
                    lambda h=h, i=i: delta[:, i * 512 + h * 256: i * 512 + h * 256 + 256],
                    lambda h=h, i=i: ptb[:, 1024 + h * 256: 1024 + h * 256 + 256]]
            for si, srcf in enumerate(srcs):
                def _st_s(s, srcf=srcf, tn=tn, vg=SW.n):
                    s.wait_ge(ST.h, tn)
                    s.wait_ge(SW.h, vg)
                    SS.inc(s.activation(sqs[:], srcf(), AF.Square))
                ctx.on("scalar", _st_s)
                sn = SS.n

                def _st_v(v, si=si, srcf=srcf, sn=sn, tn=tn):
                    v.wait_ge(ST.h, tn)
                    vchain(ctx, v, v.reduce_sum(rows[:, 8:9], srcf(), axis=AX))
                    v.wait_ge(SS.h, sn)
                    vchain(ctx, v, v.reduce_sum(rows[:, 9:10], sqs[:], axis=AX))
                    vchain(ctx, v, v.reduce_sum(rows[:, 10:11], srcf(), axis=AX,
                                                apply_absolute_value=True))
                    vchain(ctx, v, v.tensor_scalar(stt[:, si * 4: si * 4 + 1],
                                                   rows[:, 8:9], ctx.c(1.0 / 256.0), None, ALU.mult))
                    vchain(ctx, v, v.tensor_scalar(rows[:, 11:12], rows[:, 9:10],
                                                   ctx.c(1.0 / 256.0), None, ALU.mult))
                    vchain(ctx, v, v.tensor_tensor(rows[:, 8:9], stt[:, si * 4: si * 4 + 1],
                                                   stt[:, si * 4: si * 4 + 1], ALU.mult))
                    vchain(ctx, v, v.tensor_tensor(stt[:, si * 4 + 1: si * 4 + 2],
                                                   rows[:, 11:12], rows[:, 8:9], ALU.subtract))
                    vchain(ctx, v, v.tensor_scalar(stt[:, si * 4 + 2: si * 4 + 3],
                                                   rows[:, 10:11], ctx.c(1.0 / 256.0), None, ALU.mult))
                    SW.inc(v.engine_nop())
                ctx.on("vector", _st_v)
                vn = SW.n

                def _st_l2(s, si=si, vn=vn):
                    s.wait_ge(SW.h, vn)
                    SS.inc(s.activation(stt[:, si * 4 + 3: si * 4 + 4], rows[:, 9:10], AF.Sqrt))
                ctx.on("scalar", _st_l2)

            sn = SS.n

            def _stt(te, sn=sn, vg=SW.n):
                te.wait_ge(SS.h, sn)
                te.wait_ge(SW.h, vg)
                ST.inc(te.transpose(pt2[0:16, 0:128], stt[:], ctx.ident[:]))
            ctx.on("tensor", _stt)
            tn2 = ST.n

            def _ste(v, q4=q4, h=h, tn2=tn2):
                v.wait_ge(ST.h, tn2)
                SW.inc(v.tensor_copy(statsf[0:16, h * 512 + q4 * 128: h * 512 + (q4 + 1) * 128],
                                     pt2[0:16, 0:128]))
            ctx.on("vector", _ste)

    for t in range(NFT8):
        for q4 in range(4):
            hs_tile(t * 4 + q4, hsf, q4 * 128, 512, xtbuf=xtf, nbuf=1)
            stats_tile(t * 4 + q4, q4)
        ctx.barrier()
        for oc in range(NDC):
            def _shm(te, t=t, oc=oc, vg=SW.n, sg=SS.n):
                te.wait_ge(SW.h, vg)
                te.wait_ge(SS.h, sg)
                for k in range(NDC):
                    inst = te.matmul(pA[:], w1hsb[:, k * 1024 + oc * 128: k * 1024 + oc * 128 + 128],
                                     hsf[:, k * 512: k * 512 + 512],
                                     start=(k == 0), stop=(k == NDC - 1), skip_group_check=True)
                ST.inc(inst)
            ctx.on("tensor", _shm)
            tn = ST.n

            def _shev(s, tn=tn):
                s.wait_ge(ST.h, tn)
                SS.inc(s.activation(s_sh, pA[:], AF.Copy))
            ctx.on("scalar", _shev)
            sn = SS.n

            for h in range(2):
                def _hsm(te, t=t, oc=oc, h=h, sn=sn, vg=SW.n):
                    te.wait_ge(SS.h, sn)
                    te.wait_ge(SW.h, vg)
                    ST.inc(te.matmul(pB[:], w1sb[0:16, oc * 128:(oc + 1) * 128],
                                     statsf[0:16, h * 512: h * 512 + 512],
                                     start=True, stop=True, skip_group_check=True))
                ctx.on("tensor", _hsm)
                tn = ST.n

                def _h1(v, tn=tn):
                    v.wait_ge(ST.h, tn)
                    SW.inc(v.tensor_tensor(scr[:, 0:512], s_sh, pB[:], ALU.add))
                ctx.on("vector", _h1)
                vn = SW.n

                def _h1g(s, oc=oc, vn=vn):
                    s.wait_ge(SW.h, vn)
                    SS.inc(s.activation(h1t, scr[:, 0:512], AF.Gelu,
                                        bias=b1sb[:, oc:oc + 1]))
                ctx.on("scalar", _h1g)
                sn2 = SS.n

                def _lgm(te, t=t, oc=oc, h=h, sn2=sn2):
                    te.wait_ge(SS.h, sn2)
                    inst = te.matmul(pl[h][:], w2sb[:, oc * 4:(oc + 1) * 4], h1t,
                                     start=(oc == 0), stop=(oc == NDC - 1), skip_group_check=True)
                    if oc == NDC - 1:
                        ST.inc(inst)
                ctx.on("tensor", _lgm)
        tn = ST.n

        def _lge(v, tn=tn):
            v.wait_ge(ST.h, tn)
            vchain(ctx, v, v.tensor_scalar(lgsb[:, 0:512], pl[0][:], b2sb[0:4, 0:1], None, ALU.add))
            SW.inc(v.tensor_scalar(lgsb[:, 512:1024], pl[1][:], b2sb[0:4, 0:1], None, ALU.add))
        ctx.on("vector", _lge)
        vn = SW.n

        for h in range(2):
            for q4 in range(4):
                def _plt(te, h=h, q4=q4, vn=vn, vg=SW.n):
                    te.wait_ge(SW.h, max(vn, vg))
                    ST.inc(te.transpose(pt2[:, 0:4],
                                        lgsb[:, h * 512 + q4 * 128: h * 512 + (q4 + 1) * 128],
                                        ctx.ident[0:4, 0:4]))
                ctx.on("tensor", _plt)
                tn2 = ST.n

                def _sfx(v, t=t, h=h, q4=q4, tn2=tn2):
                    v.wait_ge(ST.h, tn2)
                    i = t * 4 + q4
                    sl = slice(i * 8 + h * 4, i * 8 + h * 4 + 4)
                    vchain(ctx, v, v.tensor_scalar(probs[:, sl], pt2[:, 0:4],
                                                   rtau[:, h:h + 1], None, ALU.mult))
                    vchain(ctx, v, v.reduce_max(rows[:, 8:9], probs[:, sl], axis=AX))
                    vchain(ctx, v, v.tensor_scalar(probs[:, sl], probs[:, sl],
                                                   rows[:, 8:9], None, ALU.subtract))
                    SW.inc(v.engine_nop())
                ctx.on("vector", _sfx)
                vn2 = SW.n

                def _sfe(s, t=t, h=h, q4=q4, vn2=vn2):
                    s.wait_ge(SW.h, vn2)
                    i = t * 4 + q4
                    sl = slice(i * 8 + h * 4, i * 8 + h * 4 + 4)
                    SS.inc(s.activation(probs[:, sl], probs[:, sl], AF.Exp))
                ctx.on("scalar", _sfe)
                sn2 = SS.n

                def _sfn(v, t=t, h=h, q4=q4, sn2=sn2):
                    v.wait_ge(SS.h, sn2)
                    i = t * 4 + q4
                    sl = slice(i * 8 + h * 4, i * 8 + h * 4 + 4)
                    vchain(ctx, v, v.reduce_sum(rows[:, 8:9], probs[:, sl], axis=AX))
                    vchain(ctx, v, v.reciprocal(rows[:, 9:10], rows[:, 8:9]))
                    vchain(ctx, v, v.tensor_scalar(probs[:, sl], probs[:, sl],
                                                   rows[:, 9:10], None, ALU.mult))
                    SW.inc(v.tensor_scalar(probs[:, sl], probs[:, sl],
                                           ctx.c(0.8), ctx.c(0.05), ALU.mult, ALU.add))
                ctx.on("vector", _sfn)
    ctx.barrier()
    _dump("probs", probs[:])
    if stop_after == "pf":
        return

    for i in range(NTT):
        def _trb(te, i=i, vg=SW.n, sg=SS.n):
            te.wait_ge(SW.h, vg)
            te.wait_ge(SS.h, sg)
            for cc in range(4):
                te.transpose(ptb[:, cc * 128:(cc + 1) * 128],
                             A2[:, cc * LB + i * 128: cc * LB + (i + 1) * 128], ctx.identb[:])
            for cc in range(4):
                te.transpose(ptb[:, 512 + cc * 128: 512 + (cc + 1) * 128],
                             A3[:, cc * LB + i * 128: cc * LB + (i + 1) * 128], ctx.identb[:])
            for cc in range(4):
                inst = te.transpose(ptb[:, 1024 + cc * 128: 1024 + (cc + 1) * 128],
                                    A4[:, cc * VST + 64 + i * 128: cc * VST + 64 + (i + 1) * 128],
                                    ctx.identb[:])
            ST.inc(inst)
        ctx.on("tensor", _trb)
        tn = ST.n

        def _bl(v, i=i, tn=tn):
            v.wait_ge(ST.h, tn)
            for h in range(2):
                sl = slice(h * 256, h * 256 + 256)
                pb_ = lambda p, i=i, h=h: probs[:, i * 8 + h * 4 + p: i * 8 + h * 4 + p + 1]
                vchain(ctx, v, v.tensor_scalar(otm[:, sl], ptb[:, h * 256: h * 256 + 256], pb_(0), None, ALU.mult))
                vchain(ctx, v, v.scalar_tensor_tensor(otm[:, sl], ptb[:, 512 + h * 256: 512 + h * 256 + 256], pb_(1), otm[:, sl], ALU.mult, ALU.add))
                vchain(ctx, v, v.scalar_tensor_tensor(otm[:, sl], delta[:, i * 512 + h * 256: i * 512 + h * 256 + 256], pb_(2), otm[:, sl], ALU.mult, ALU.add))
                vchain(ctx, v, v.scalar_tensor_tensor(otm[:, sl], ptb[:, 1024 + h * 256: 1024 + h * 256 + 256], pb_(3), otm[:, sl], ALU.mult, ALU.add))
            SW.inc(v.engine_nop())
        ctx.on("vector", _bl)
        vn = SW.n

        def _rs_s(s, vn=vn):
            s.wait_ge(SW.h, vn)
            SS.inc(s.activation(sqf2[:, 0:512], otm, AF.Square))
        ctx.on("scalar", _rs_s)
        sn = SS.n

        def _rs_v(v, sn=sn):
            v.wait_ge(SS.h, sn)
            for h in range(2):
                sl = slice(h * 256, h * 256 + 256)
                vchain(ctx, v, v.reduce_sum(rows[:, 8:9], sqf2[:, sl], axis=AX))
                vchain(ctx, v, v.tensor_scalar(rows[:, 10 + h: 11 + h], rows[:, 8:9],
                                               ctx.c(1.0 / 256.0), ctx.c(1e-5), ALU.mult, ALU.add))
            SW.inc(v.engine_nop())
        ctx.on("vector", _rs_v)
        vn = SW.n

        def _rs_q(s, vn=vn):
            s.wait_ge(SW.h, vn)
            SS.inc(s.activation(rows[:, 8:10], rows[:, 10:12], AF.Sqrt))
        ctx.on("scalar", _rs_q)
        sn = SS.n

        def _rs_a(v, sn=sn):
            v.wait_ge(SS.h, sn)
            for h in range(2):
                vchain(ctx, v, v.reciprocal(rows[:, 4 + h: 5 + h], rows[:, 8 + h: 9 + h]))
                vchain(ctx, v, v.tensor_tensor(rows[:, 6:7], rows[:, 4 + h: 5 + h], rows[:, 4 + h: 5 + h], ALU.mult))
                vchain(ctx, v, v.tensor_tensor(rows[:, 6:7], rows[:, 6:7], rows[:, 10 + h: 11 + h], ALU.mult))
                vchain(ctx, v, v.tensor_scalar(rows[:, 6:7], rows[:, 6:7], ctx.c(-0.5), ctx.c(1.5), ALU.mult, ALU.add))
                vchain(ctx, v, v.tensor_tensor(rows[:, 4 + h: 5 + h], rows[:, 4 + h: 5 + h], rows[:, 6:7], ALU.mult))
                vchain(ctx, v, v.tensor_scalar(otm[:, h * 256:(h + 1) * 256],
                                               otm[:, h * 256:(h + 1) * 256],
                                               rows[:, 4 + h: 5 + h], None, ALU.mult))
            SW.inc(v.engine_nop())
        ctx.on("vector", _rs_a)
        vn = SW.n

        def _tro(te, vn=vn):
            te.wait_ge(SW.h, vn)
            for cc in range(4):
                inst = te.transpose(pt2[:, 512 + cc * 128: 512 + (cc + 1) * 128],
                                    otm[:, cc * 128:(cc + 1) * 128], ctx.ident[:])
            ST.inc(inst)
        ctx.on("tensor", _tro)
        tn = ST.n

        def _evo(v, tn=tn):
            v.wait_ge(ST.h, tn)
            SW.inc(v.tensor_copy(ofm[:], pt2[:, 512:1024]))
        ctx.on("vector", _evo)
        vn = SW.n

        for half in range(2):
            bank = [pA, pB][half]

            def _wo(te, half=half, bank=bank, vn=vn, vg=SW.n):
                te.wait_ge(SW.h, max(vn, vg))
                for cc in range(4):
                    inst = te.matmul(bank[:], ofm[:, cc * 128:(cc + 1) * 128],
                                     WoS[:, cc * 1024 + half * 512: cc * 1024 + half * 512 + 512],
                                     start=(cc == 0), stop=(cc == 3), skip_group_check=True)
                ST.inc(inst)
            ctx.on("tensor", _wo)
        tn = ST.n

        for half in range(2):
            def _pst(v, half=half, tn=tn, dg=ctx.dma.n):
                v.wait_ge(ST.h, tn)
                v.wait_ge(ctx.dma.h, dg)
                SW.inc(v.tensor_copy(pstg, [pA, pB][half][:]))
            ctx.on("vector", _pst)
            vn = SW.n

            def _sto(sy, i=i, half=half, vn=vn):
                sy.wait_ge(SW.h, vn)
                ctx.dma.inc(sy.dma_start(
                    out=rs_in[i * 128:(i + 1) * 128, half * 512:(half + 1) * 512],
                    in_=pstg), 16)
            ctx.on("sync", _sto)
    ctx.barrier()

    def _rs(g, dg=ctx.dma.n):
        g.wait_ge(ctx.dma.h, dg)
        ctx.cc.inc(g.collective_compute("ReduceScatter", ALU.add,
                                        [[0, 1], [2, 3], [4, 5], [6, 7]],
                                        [rs_in.opt()], [rs_out.opt()]))
    ctx.on("gpsimd", _rs)
    ccn = ctx.cc.n

    for i in range(NT):
        buf = i % 2
        xv = xt[:, buf * D:(buf + 1) * D]

        def _lf(sy, i=i, xv=xv, ccn=ccn, guard=SW.n, dg=ctx.dmo.n):
            sy.wait_ge(ctx.cc.h, ccn)
            sy.wait_ge(SW.h, guard)
            sy.wait_ge(ctx.dmo.h, dg)
            ctx.dma.inc(sy.dma_start(out=xv, in_=xhalf[i * 128:(i + 1) * 128, :]), 16)
            ctx.dma.inc(sy.dma_start(out=lnst[:], in_=rs_out[i * 128:(i + 1) * 128, :]), 16)
        ctx.on("sync", _lf)
        dn = ctx.dma.n

        def _ad(v, dn=dn):
            v.wait_ge(ctx.dma.h, dn)
            SW.inc(v.tensor_tensor(lnst[:], lnst[:], xv, ALU.add))
        ctx.on("vector", _ad)
        vn = SW.n

        def _st(sy, i=i, vn=vn):
            sy.wait_ge(SW.h, vn)
            ctx.dmo.inc(sy.dma_start(out=xout[i * 128:(i + 1) * 128, :], in_=lnst[:]), 16)
        ctx.on("sync", _st)


L0A_DBG = {
    "qh": ([128, 4 * LB], BF16), "vh": ([128, 4 * VST], BF16),
    "beta": ([128, 2 * NTT], F32), "delta": ([128, 16384], BF16),
    "short": ([128, 4 * LB], BF16), "long": ([128, 4 * LB], BF16),
    "probs": ([128, 8 * NTT], F32),
}


def build_l0a(stop_after=None, dbg_tags=()):
    nc = bass.Bass("TRN2", target_bir_lowering=False, debug=False)
    xin = nc.dram_tensor("xin", [LB, D], F32, kind="ExternalInput").ap()
    xhalf = nc.dram_tensor("xhalf", [TPC, D], F32, kind="ExternalInput").ap()
    WqL = nc.dram_tensor("WqL", [128, 4096], BF16, kind="ExternalInput").ap()
    WkL = nc.dram_tensor("WkL", [128, 4096], BF16, kind="ExternalInput").ap()
    WvL = nc.dram_tensor("WvL", [128, 4096], BF16, kind="ExternalInput").ap()
    WbL = nc.dram_tensor("WbL", [128, 16], BF16, kind="ExternalInput").ap()
    dconv = nc.dram_tensor("dconv", [3, 4, 4, 128, 128], BF16, kind="ExternalInput").ap()
    dfirS = nc.dram_tensor("dfirS", [5, 4, 128, 128], BF16, kind="ExternalInput").ap()
    dfirL = nc.dram_tensor("dfirL", [64, 4, 128, 128], BF16, kind="ExternalInput").ap()
    WoL = nc.dram_tensor("WoL", [128, 4096], BF16, kind="ExternalInput").ap()
    onwc = nc.dram_tensor("onwc", [128, 4], F32, kind="ExternalInput").ap()
    w1hsL = nc.dram_tensor("w1hsL", [128, 8192], BF16, kind="ExternalInput").ap()
    w1st = nc.dram_tensor("w1st", [16, 1024], BF16, kind="ExternalInput").ap()
    w2L = nc.dram_tensor("w2L", [128, 32], BF16, kind="ExternalInput").ap()
    b1c = nc.dram_tensor("b1c", [128, 8], F32, kind="ExternalInput").ap()
    b2r = nc.dram_tensor("b2r", [4, 1], F32, kind="ExternalInput").ap()
    ltr = nc.dram_tensor("ltr", [1, 4], F32, kind="ExternalInput").ap()
    xout = nc.dram_tensor("xout", [TPC, D], F32, kind="ExternalOutput").ap()
    dbg = {t: nc.dram_tensor("dbg_" + t, list(L0A_DBG[t][0]), L0A_DBG[t][1],
                             kind="ExternalOutput").ap() for t in dbg_tags}
    rs_in = nc.dram_tensor("rs_in", [LB, D], F32, kind="Internal").ap()
    rs_out = nc.dram_tensor("rs_out", [TPC, D], F32, kind="Internal").ap()

    with ExitStack() as es:
        ctx = Ctx(nc, es)
        emit_l0a(ctx, xin, xhalf, xout, WqL, WkL, WvL, WbL, dconv, dfirS, dfirL,
                 WoL, onwc, w1hsL, w1st, w2L, b1c, b2r, ltr, rs_in, rs_out,
                 stop_after=stop_after, dbg=dbg)
        ctx.on("sync", lambda sy: sy.wait_ge(ctx.dmo.h, ctx.dmo.n))
    return nc


def run_l0a(x_full, g, stop_after=None, dbg_tags=()):
    nc = build_l0a(stop_after, dbg_tags)
    in_maps = []
    for core in range(NCORES):
        b, hp = core // 2, core % 2
        lt4 = g["l0_lt"]
        ltcore = np.zeros((1, 4), np.float32)
        ltcore[0, 0:2] = lt4[0, 2 * hp: 2 * hp + 2]
        in_maps.append({
            "xin": np.ascontiguousarray(x_full[b]),
            "xhalf": np.ascontiguousarray(x_full[b, hp * TPC:(hp + 1) * TPC]),
            "WqL": g[f"l0_WqL{hp}"], "WkL": g[f"l0_WkL{hp}"], "WvL": g[f"l0_WvL{hp}"],
            "WbL": g[f"l0_WbL{hp}"], "dconv": g[f"l0_dconv{hp}"],
            "dfirS": g[f"l0_dfirS{hp}"], "dfirL": g[f"l0_dfirL{hp}"],
            "WoL": g[f"l0_WoL{hp}"], "onwc": g[f"l0_onw{hp}"],
            "w1hsL": g["l0_w1hsL"], "w1st": g["l0_w1st"], "w2L": g["l0_w2L"],
            "b1c": g["l0_b1"], "b2r": g["l0_b2"], "ltr": ltcore,
        })
    res = run_bass_kernel_spmd(nc, in_maps, core_ids=list(range(NCORES)))
    out = np.empty((B, L, D), np.float32)
    for core in range(NCORES):
        b, hp = core // 2, core % 2
        out[b, hp * TPC:(hp + 1) * TPC] = res.results[core]["xout"]
    if dbg_tags:
        return out, res
    return out





def build_fused(blocks=('a', 'f', 't', 'm')):
    nc = bass.Bass("TRN2", target_bir_lowering=False, debug=False)
    dt = lambda n, s, t, k="ExternalInput": nc.dram_tensor(n, list(s), t, kind=k).ap()
    a_xin = dt("a_xin", [LB, D], F32)
    a_xhalf = dt("a_xhalf", [TPC, D], F32)
    a_WqL = dt("a_WqL", [128, 4096], BF16)
    a_WkL = dt("a_WkL", [128, 4096], BF16)
    a_WvL = dt("a_WvL", [128, 4096], BF16)
    a_WbL = dt("a_WbL", [128, 16], BF16)
    a_dconv = dt("a_dconv", [3, 4, 4, 128, 128], BF16)
    a_dfirS = dt("a_dfirS", [5, 4, 128, 128], BF16)
    a_dfirL = dt("a_dfirL", [64, 4, 128, 128], BF16)
    a_WoL = dt("a_WoL", [128, 4096], BF16)
    a_onwc = dt("a_onwc", [128, 4], F32)
    a_w1hsL = dt("a_w1hsL", [128, 8192], BF16)
    a_w1st = dt("a_w1st", [16, 1024], BF16)
    a_w2L = dt("a_w2L", [128, 32], BF16)
    a_b1c = dt("a_b1c", [128, 8], F32)
    a_b2r = dt("a_b2r", [4, 1], F32)
    a_ltr = dt("a_ltr", [1, 4], F32)
    rs_in = dt("rs_in", [LB, D], F32, "Internal")
    rs_out = dt("rs_out", [TPC, D], F32, "Internal")
    f_qwT = dt("f_qwT", [6, D, D], I32)
    f_screpT = dt("f_screpT", [6, D, D], F32)
    f_bias6 = dt("f_bias6", [6, 128, 8], F32)
    f_laT = dt("f_laT", [6, 128, 256], BF16)
    f_lbT = dt("f_lbT", [6, 32, D], BF16)
    f_gam = dt("f_gam", [3, 128, 8], F32)
    f_bet = dt("f_bet", [3, 128, 8], F32)
    t_WqL = dt("t_WqL", [128, NDC * 1024], BF16)
    t_WkL = dt("t_WkL", [128, NDC * 1024], BF16)
    t_WvL = dt("t_WvL", [128, NDC * 1024], BF16)
    t_WoL = dt("t_WoL", [128, NDC * 1024], BF16)
    t_projT = dt("t_projT", [64, 64], BF16)
    t_W1L = dt("t_W1L", [128, NDC * 4096], BF16)
    t_b1c = dt("t_b1c", [128, 32], F32)
    t_W2L = dt("t_W2L", [128, 32 * 1024], BF16)
    t_b2c = dt("t_b2c", [128, 8], F32)
    t_g1c = dt("t_g1c", [128, 8], F32)
    t_be1c = dt("t_be1c", [128, 8], F32)
    t_g2c = dt("t_g2c", [128, 8], F32)
    t_be2c = dt("t_be2c", [128, 8], F32)
    kv_in = dt("kv_in", [130, 512], F32, "Internal")
    kv_out = dt("kv_out", [130, 512], F32, "Internal")
    mx_in = dt("mx_in", [1, 1], F32, "Internal")
    mx_out = dt("mx_out", [1, 1], F32, "Internal")
    m_EX = dt("m_EX", [10, 128, 3 * 4096], BF16)
    m_gateTf = dt("m_gateTf", [128, 64], F32)
    m_lbrow = dt("m_lbrow", [8, 1], F32)
    cc_in = dt("cc_in", [1, 16], F32, "Internal")
    cc_out = dt("cc_out", [1, 16], F32, "Internal")
    s1 = dt("s1", [TPC, D], F32, "Internal")
    s2 = dt("s2", [TPC, D], F32, "Internal")
    s3 = dt("s3", [TPC, D], F32, "Internal")
    xout = dt("xout", [TPC, D], F32, "ExternalOutput")

    chain = []
    if 'a' in blocks:
        chain.append(('a', lambda c: emit_l0a(
            c, a_xin, a_xhalf, s1, a_WqL, a_WkL, a_WvL, a_WbL, a_dconv,
            a_dfirS, a_dfirL, a_WoL, a_onwc, a_w1hsL, a_w1st, a_w2L,
            a_b1c, a_b2r, a_ltr, rs_in, rs_out)))
    if 'f' in blocks:
        fin = s1 if 'a' in blocks else a_xhalf
        chain.append(('f', lambda c, fin=fin: emit_l0f(
            c, fin, s2, f_qwT, f_screpT, f_bias6, f_laT, f_lbT, f_gam, f_bet)))
    if 't' in blocks:
        tin = s2 if 'f' in blocks else a_xhalf
        chain.append(('t', lambda c, tin=tin: emit_l1a(
            c, tin, s3, t_WqL, t_WkL, t_WvL, t_WoL, t_projT, t_W1L, t_b1c,
            t_W2L, t_b2c, t_g1c, t_be1c, t_g2c, t_be2c, kv_in, kv_out,
            mx_in, mx_out)))
    if 'm' in blocks:
        min_ = s3 if 't' in blocks else a_xhalf
        chain.append(('m', lambda c, min_=min_: emit_l1f(
            c, min_, xout, m_EX, m_gateTf, m_lbrow, cc_in, cc_out)))
    outs = {'a': s1, 'f': s2, 't': s3, 'm': xout}

    with ExitStack() as es0:
        ctx = Ctx(nc, es0)
        for bi, (tag, fn) in enumerate(chain):
            with ExitStack() as eb:
                ctx.es = eb
                fn(ctx)
                if bi < len(chain) - 1 and SEAM_MODE == "reset":
                    ctx.seam_reset()
                else:
                    ctx.barrier()
        ctx.es = es0
        if chain[-1][0] != 'm':
            sfin = outs[chain[-1][0]]
            def _cp(sy, sfin=sfin):
                for i in range(4):
                    ctx.dmo.inc(sy.dma_start(out=xout[i * 512:(i + 1) * 512, :],
                                             in_=sfin[i * 512:(i + 1) * 512, :]), 16)
            ctx.on("sync", _cp)
        ctx.on("sync", lambda sy: sy.wait_ge(ctx.dmo.h, ctx.dmo.n))
    return nc


FUSED_BLOCKS = ('a', 'f', 't', 'm')
SEAM_MODE = 'plain'


def kernel(**inputs):
    g = _host_prep(inputs)
    nc = build_fused(FUSED_BLOCKS)
    in_maps = []
    for core in range(NCORES):
        b, hp = core // 2, core % 2
        lt4 = g["l0_lt"]
        ltcore = np.zeros((1, 4), np.float32)
        ltcore[0, 0:2] = lt4[0, 2 * hp: 2 * hp + 2]
        in_maps.append({
            "a_xin": np.ascontiguousarray(g["x"][b]),
            "a_xhalf": np.ascontiguousarray(g["x"][b, hp * TPC:(hp + 1) * TPC]),
            "a_WqL": g[f"l0_WqL{hp}"], "a_WkL": g[f"l0_WkL{hp}"],
            "a_WvL": g[f"l0_WvL{hp}"], "a_WbL": g[f"l0_WbL{hp}"],
            "a_dconv": g[f"l0_dconv{hp}"], "a_dfirS": g[f"l0_dfirS{hp}"],
            "a_dfirL": g[f"l0_dfirL{hp}"], "a_WoL": g[f"l0_WoL{hp}"],
            "a_onwc": g[f"l0_onw{hp}"], "a_w1hsL": g["l0_w1hsL"],
            "a_w1st": g["l0_w1st"], "a_w2L": g["l0_w2L"], "a_b1c": g["l0_b1"],
            "a_b2r": g["l0_b2"], "a_ltr": ltcore,
            "f_qwT": g["l0f_qwT"], "f_screpT": g["l0f_screpT"],
            "f_bias6": g["l0f_bias"], "f_laT": g["l0f_laT"], "f_lbT": g["l0f_lbT"],
            "f_gam": g["l0f_gam"], "f_bet": g["l0f_bet"],
            "t_WqL": g["l1_WqL"], "t_WkL": g["l1_WkL"], "t_WvL": g["l1_WvL"],
            "t_WoL": g["l1_WoL"], "t_projT": g["l1_projT"], "t_W1L": g["l1_W1L"],
            "t_b1c": g["l1_b1"], "t_W2L": g["l1_W2L"], "t_b2c": g["l1_b2"],
            "t_g1c": g["l1_ln1g"], "t_be1c": g["l1_ln1b"],
            "t_g2c": g["l1_ln2g"], "t_be2c": g["l1_ln2b"],
            "m_EX": g["m_EX"], "m_gateTf": g["m_gateTf"], "m_lbrow": g["m_lb"],
        })
    res = run_bass_kernel_spmd(nc, in_maps, core_ids=list(range(NCORES)))
    out = np.empty((B, L, D), np.float32)
    for core in range(NCORES):
        b, hp = core // 2, core % 2
        out[b, hp * TPC:(hp + 1) * TPC] = res.results[core]["xout"]
    return out




def emit_l1a(ctx, xin, xout, WqL, WkL, WvL, WoL, projT, W1L, b1c, W2L, b2c,
             g1c, be1c, g2c, be2c, kv_in, kv_out, mx_in, mx_out,
             stop_after=None, dbg=None):
    DN = 0.3535533905932738
    b1c_d, b2c_d, g1c_d, be1c_d, g2c_d, be2c_d = b1c, b2c, g1c, be1c, g2c, be2c
    RATIO = 0.125
    EPSK = 1e-6
    ctx.const_pool([1.0 / D, 1e-5, 1e-6, 0.0, -0.5, 1.5, 1.0, DN, RATIO,
                    RATIO * EPSK, 0.70710678118654752])
    ctx.emit_init()

    xt = ctx.sbuf("xt", [128, 2 * D], F32)
    scr = ctx.sbuf("scr", [128, D], F32)
    lnst = scr[:]
    rows = ctx.sbuf("rows", [128, 8], F32)
    sqf = ctx.sbuf("sqf", [128, D], BF16)
    rowstats = ctx.sbuf("rowstats", [128, 2 * NT], F32)
    hbA = ctx.sbuf("hbA", [128, NDC * TPC], BF16)
    bufA = ctx.sbuf("bufA", [128, NDC * TPC], BF16)
    bufB = ctx.sbuf("bufB", [128, NDC * TPC], BF16)
    bufC = ctx.sbuf("bufC", [128, NDC * TPC], BF16)
    bufD = ctx.sbuf("bufD", [128, NDC * TPC], BF16)
    Wb_ = [ctx.sbuf(f"Wb{i}", [128, NDC * 512], BF16) for i in range(2)]
    dq = ctx.sbuf("dq", [128, 16 * NT], F32)
    dk = ctx.sbuf("dk", [128, 16 * NT], F32)
    prj = ctx.sbuf("prj", [128, 64], BF16)
    mxr = ctx.sbuf("mxr", [128, 4], F32)
    mxrow = ctx.sbuf("mxrow", [1, 128], F32)
    kvloc = ctx.sbuf("kvloc", [128, 512], F32)
    kfs = ctx.sbuf("kfs", [1, 1024], F32)
    kvaug = ctx.sbuf("kvaug", [128, 8 * 65], BF16)
    t1 = scr[:, 0:512]
    t2 = scr[:, 512:1024]
    rcol = ctx.sbuf("rcol", [128, 4], F32)
    rws = ctx.sbuf("rws", [1, 3 * 512], F32)

    pt2 = ctx.psum("pt2", [128, 1024])
    pA = ctx.psum("pA", [128, 512])
    pB = ctx.psum("pB", [128, 512])
    pC = ctx.psum("pC", [128, 512])
    pD = ctx.psum("pD", [128, 512])
    prow = [ctx.psum(f"prow{i}", [1, 512]) for i in range(2)]

    b1c = ctx.sbuf("b1s", [128, 32], F32)
    b2c = ctx.sbuf("b2s", [128, 8], F32)
    g1c = ctx.sbuf("g1s", [128, 8], F32)
    be1c = ctx.sbuf("be1s", [128, 8], F32)
    g2c = ctx.sbuf("g2s", [128, 8], F32)
    be2c = ctx.sbuf("be2s", [128, 8], F32)

    def _ldpar(sy):
        for dst, srcap in ((b1c, b1c_d), (b2c, b2c_d), (g1c, g1c_d),
                           (be1c, be1c_d), (g2c, g2c_d), (be2c, be2c_d)):
            ctx.dma.inc(sy.dma_start(out=dst[:], in_=srcap), 16)
    ctx.on("sync", _ldpar)

    SW = ctx.sem["vector"]; SS = ctx.sem["scalar"]; ST = ctx.sem["tensor"]

    def ld(sy_ops):
        ctx.on("sync", sy_ops)
        return ctx.dma.n

    def _ldprj(sy):
        ctx.dma.inc(sy.dma_start(out=prj[0:64, :], in_=projT), 16)
        ctx.dma.inc(sy.dma_start(out=prj[64:128, :], in_=projT), 16)
    dn0 = ld(_ldprj)
    def _dump(tag, ap_src):
        if dbg is not None and tag in dbg:
            ctx.barrier()
            def _st(sy, ap_src=ap_src, apd=dbg[tag]):
                ctx.dmo.inc(sy.dma_start(out=apd, in_=ap_src), 16)
            ctx.on("sync", _st)

    def _stop(tag):
        return stop_after == tag

    emit_tm_ln_to_fm(ctx, xin, xt, lnst, rows, sqf, None, hbA, pt2,
                     hb_only=True, rowsave=rowstats)
    ctx.barrier()
    _dump("hb", hbA[:])
    if _stop("p1"):
        return

    def proj_fm(WL, dstbuf, scale_dn):
        WLv = WL.rearrange("p (k f) -> p k f", k=NDC)
        for half in range(2):
            wb = Wb_[half]

            def _ldw(sy, half=half, wb=wb, guard=ST.n):
                sy.wait_ge(ST.h, guard)
                ctx.dma.inc(sy.dma_start(
                    out=wb[:], in_=WLv[:, :, half * 512:(half + 1) * 512]), 16)
            ctx.on("sync", _ldw)
            dn = ctx.dma.n
            for t in range(NFT):
                for o4 in range(4):
                    oc = half * 4 + o4
                    bank = [pA, pB][(t * 4 + o4) % 2]

                    def _mm(te, t=t, o4=o4, bank=bank, dn=dn, wb=wb, vg=SW.n):
                        te.wait_ge(ctx.dma.h, dn)
                        te.wait_ge(SW.h, vg)
                        for k in range(NDC):
                            inst = te.matmul(
                                bank[:], wb[:, k * 512 + o4 * 128: k * 512 + o4 * 128 + 128],
                                hbA[:, k * TPC + t * 512: k * TPC + t * 512 + 512],
                                start=(k == 0), stop=(k == NDC - 1), skip_group_check=True)
                        ST.inc(inst)
                    ctx.on("tensor", _mm)
                    tn = ST.n

                    def _ev(v, t=t, oc=oc, bank=bank, tn=tn, dstbuf=dstbuf, scale_dn=scale_dn):
                        v.wait_ge(ST.h, tn)
                        sl = slice(oc * TPC + t * 512, oc * TPC + t * 512 + 512)
                        if scale_dn:
                            SW.inc(v.tensor_scalar(dstbuf[:, sl], bank[:], ctx.c(DN), None, ALU.mult))
                        else:
                            SW.inc(v.tensor_copy(dstbuf[:, sl], bank[:]))
                    ctx.on("vector", _ev)
        ctx.barrier()

    proj_fm(WqL, bufA, True)
    _dump("qs", bufA[:])
    if _stop("pq"):
        return
    proj_fm(WkL, bufB, True)

    WvLv = WvL.rearrange("p (k f) -> p k f", k=NDC)
    for half in range(2):
        wb = Wb_[half]

        def _ldwv(sy, half=half, wb=wb, guard=ST.n):
            sy.wait_ge(ST.h, guard)
            ctx.dma.inc(sy.dma_start(
                out=wb[:], in_=WvLv[:, :, half * 512:(half + 1) * 512]), 16)
        ctx.on("sync", _ldwv)
        dn = ctx.dma.n
        for col in range(NT):
            bank = [pA, pB][col % 2]

            def _mmv(te, col=col, bank=bank, dn=dn, wb=wb, vg=SW.n):
                te.wait_ge(ctx.dma.h, dn)
                te.wait_ge(SW.h, vg)
                for k in range(NDC):
                    inst = te.matmul(
                        bank[:], hbA[:, k * TPC + col * 128: k * TPC + col * 128 + 128],
                        wb[:, k * 512:(k + 1) * 512],
                        start=(k == 0), stop=(k == NDC - 1), skip_group_check=True)
                ST.inc(inst)
            ctx.on("tensor", _mmv)
            tn = ST.n

            def _evv(v, col=col, half=half, bank=bank, tn=tn):
                v.wait_ge(ST.h, tn)
                sl = slice(col * 1024 + half * 512, col * 1024 + half * 512 + 512)
                SW.inc(v.tensor_copy(bufC[:, sl], bank[:]))
            ctx.on("vector", _evv)
    ctx.barrier()

    ptb = pt2[:].bitcast(BF16)
    def diag_of(srcbuf, dst):
        for i in range(NT):
            def _tr(te, i=i, vg=SW.n, sg=SS.n):
                te.wait_ge(SW.h, vg)
                te.wait_ge(SS.h, sg)
                for c in range(NDC):
                    inst = te.transpose(ptb[:, c * 128:(c + 1) * 128],
                                        srcbuf[:, c * TPC + i * 128: c * TPC + (i + 1) * 128],
                                        ctx.identb[:])
                ST.inc(inst)
            ctx.on("tensor", _tr)
            tn = ST.n

            def _sq(s, tn=tn):
                s.wait_ge(ST.h, tn)
                SS.inc(s.activation(sqf[:], ptb[:, 0:1024], AF.Square,
                                    scale=ctx.c(0.70710678118654752)))
            ctx.on("scalar", _sq)
            sn = SS.n

            def _rd(v, i=i, sn=sn, dst=dst):
                v.wait_ge(SS.h, sn)
                for h in range(16):
                    inst = v.reduce_sum(dst[:, i * 16 + h: i * 16 + h + 1],
                                        sqf[:, h * 64:(h + 1) * 64], axis=AX)
                SW.inc(inst)
            ctx.on("vector", _rd)
    ctx.barrier()
    _dump("vtm", bufC[:])
    if _stop("pv"):
        return
    diag_of(bufA, dq)
    diag_of(bufB, dk)
    ctx.barrier()
    _dump("dq", dq[:])
    if _stop("pd"):
        return

    def _mz(v):
        SW.inc(v.memset(mxr[:, 0:1], -3e38))
    ctx.on("vector", _mz)
    for h in range(16):
        for col in range(NT):
            bank = [pA, pB][(h * NT + col) % 2]

            def _mmk(te, h=h, col=col, bank=bank, vg=SW.n):
                te.wait_ge(SW.h, vg)
                ST.inc(te.matmul(bank[:, 0:64],
                                 bufB[(h % 2) * 64:(h % 2) * 64 + 64,
                                      (h // 2) * TPC + col * 128: (h // 2) * TPC + (col + 1) * 128],
                                 prj[(h % 2) * 64:(h % 2) * 64 + 64, :],
                                 start=True, stop=True, skip_group_check=True))
            ctx.on("tensor", _mmk)
            tn = ST.n

            def _mx(v, bank=bank, tn=tn):
                v.wait_ge(ST.h, tn)
                vchain(ctx, v, v.reduce_max(mxr[:, 1:2], bank[:, 0:64], axis=AX))
                SW.inc(v.tensor_tensor(mxr[:, 0:1], mxr[:, 0:1], mxr[:, 1:2], ALU.max))
            ctx.on("vector", _mx)
    vn = SW.n

    def _mfin(te, vn=vn):
        te.wait_ge(SW.h, vn)
        ST.inc(te.transpose(pt2[0:1, 0:128], mxr[:, 0:1], ctx.ident[:]))
    ctx.on("tensor", _mfin)
    tn = ST.n

    def _mfin2(v, tn=tn):
        v.wait_ge(ST.h, tn)
        SW.inc(v.reduce_max(mxrow[0:1, 0:1], pt2[0:1, 0:128], axis=AX))
    ctx.on("vector", _mfin2)
    vn = SW.n
    dn = ld(lambda sy, vn=vn: (sy.wait_ge(SW.h, vn),
                               ctx.dma.inc(sy.dma_start(out=mx_in, in_=mxrow[0:1, 0:1]), 16)))

    def _ccm(g, dn=dn):
        g.wait_ge(ctx.dma.h, dn)
        ctx.cc.inc(g.collective_compute("AllReduce", ALU.max,
                                        [[0, 1, 2, 3, 4, 5, 6, 7]],
                                        [mx_in.opt()], [mx_out.opt()]))
    ctx.on("gpsimd", _ccm)
    ccn = ctx.cc.n
    dn = ld(lambda sy, ccn=ccn: (sy.wait_ge(ctx.cc.h, ccn),
                                 ctx.dma.inc(sy.dma_start(out=mxrow[0:1, 0:1], in_=mx_out), 16)))

    def _mbc(te, dn=dn):
        te.wait_ge(ctx.dma.h, dn)
        ST.inc(te.matmul(pt2[:, 0:1], ctx.onesc[:], mxrow[0:1, 0:1], start=True, stop=True))
    ctx.on("tensor", _mbc)
    tn = ST.n

    def _mbc2(v, tn=tn):
        v.wait_ge(ST.h, tn)
        SW.inc(v.tensor_copy(mxr[:, 2:3], pt2[:, 0:1]))
    ctx.on("vector", _mbc2)
    ctx.barrier()

    def feat(srcbuf, dstbuf, dcol, is_q):
        for h in range(16):
            for col in range(NT):
                bank = [pA, pB][(h * NT + col) % 2]

                def _mmf(te, h=h, col=col, bank=bank, srcbuf=srcbuf, vg=SW.n, sg=SS.n):
                    te.wait_ge(SW.h, vg)
                    te.wait_ge(SS.h, sg)
                    ST.inc(te.matmul(bank[:, 0:64],
                                     srcbuf[(h % 2) * 64:(h % 2) * 64 + 64,
                                            (h // 2) * TPC + col * 128: (h // 2) * TPC + (col + 1) * 128],
                                     prj[(h % 2) * 64:(h % 2) * 64 + 64, :],
                                     start=True, stop=True, skip_group_check=True))
                ctx.on("tensor", _mmf)
                tn = ST.n

                def _evf(v, h=h, col=col, bank=bank, tn=tn, dcol=dcol, is_q=is_q):
                    v.wait_ge(ST.h, tn)
                    if is_q:
                        vchain(ctx, v, v.reduce_max(mxr[:, 3:4], bank[:, 0:64], axis=AX))
                        mcol = mxr[:, 3:4]
                    else:
                        mcol = mxr[:, 2:3]
                    SW.inc(v.tensor_scalar(t1[:, 0:64], bank[:, 0:64],
                                           dcol[:, col * 16 + h: col * 16 + h + 1],
                                           mcol, ALU.subtract, ALU.subtract))
                ctx.on("vector", _evf)
                vn = SW.n

                def _exf(s, vn=vn):
                    s.wait_ge(SW.h, vn)
                    SS.inc(s.activation(t2[:, 0:64], t1[:, 0:64], AF.Exp))
                ctx.on("scalar", _exf)
                sn = SS.n

                def _scf(v, h=h, col=col, sn=sn, dstbuf=dstbuf):
                    v.wait_ge(SS.h, sn)
                    SW.inc(v.tensor_scalar(
                        dstbuf[:, col * 1024 + h * 64: col * 1024 + h * 64 + 64],
                        t2[:, 0:64], ctx.c(RATIO), ctx.c(RATIO * EPSK), ALU.mult, ALU.add))
                ctx.on("vector", _scf)

    _dump("mx", mxr[:, 2:3])
    if _stop("pm"):
        return
    feat(bufA, hbA, dq, True)
    ctx.barrier()
    _dump("qf", hbA[:])
    if _stop("pqf"):
        return
    feat(bufB, bufA, dk, False)
    ctx.barrier()
    _dump("kf", bufA[:])
    if _stop("pkf"):
        return
    qf_tm, kf_tm, v_tm = hbA, bufA, bufC
    o1fA = bufA[:].bitcast(F32)
    o1fB = bufB[:].bitcast(F32)

    def o1f_sl(c, a, b):
        base = o1fA if c < 4 else o1fB
        cc = c % 4
        return base[:, cc * TPC + a: cc * TPC + b]

    for hp in range(8):
        for par in range(2):
            h = hp * 2 + par
            for i in range(NT):
                def _mmkv(te, h=h, par=par, i=i, vg=SW.n):
                    if i == 0:
                        te.wait_ge(SW.h, vg)
                    inst = te.matmul(pA[par * 64:par * 64 + 64, 0:64],
                                     kf_tm[:, i * 1024 + h * 64: i * 1024 + (h + 1) * 64],
                                     v_tm[:, i * 1024 + h * 64: i * 1024 + (h + 1) * 64],
                                     start=(i == 0), stop=(i == NT - 1),
                                     skip_group_check=True)
                    if i == NT - 1:
                        ST.inc(inst)
                ctx.on("tensor", _mmkv)
        tn = ST.n

        def _evkv(v, hp=hp, tn=tn):
            v.wait_ge(ST.h, tn)
            SW.inc(v.tensor_copy(kvloc[:, hp * 64:(hp + 1) * 64], pA[:, 0:64]))
        ctx.on("vector", _evkv)
        vn = SW.n
        ctx.on("tensor", lambda te, vn=vn: te.wait_ge(SW.h, vn))

    for i in range(NT):
        def _mks(te, i=i):
            te.matmul(prow[0][:], ctx.onesk[:],
                      kf_tm[:, i * 1024: i * 1024 + 512],
                      start=(i == 0), stop=(i == NT - 1), skip_group_check=True)
            inst = te.matmul(prow[1][:], ctx.onesk[:],
                             kf_tm[:, i * 1024 + 512: i * 1024 + 1024],
                             start=(i == 0), stop=(i == NT - 1), skip_group_check=True)
            if i == NT - 1:
                ST.inc(inst)
        ctx.on("tensor", _mks)
    tn = ST.n

    def _evks(v, tn=tn):
        v.wait_ge(ST.h, tn)
        v.tensor_copy(kfs[:, 0:512], prow[0][:])
        SW.inc(v.tensor_copy(kfs[:, 512:1024], prow[1][:]))
    ctx.on("vector", _evks)
    vn = SW.n

    def _stkv(sy, vn=vn):
        sy.wait_ge(SW.h, vn)
        ctx.dma.inc(sy.dma_start(out=kv_in[0:128, :], in_=kvloc[:]), 16)
        ctx.dma.inc(sy.dma_start(out=kv_in[128:129, :], in_=kfs[:, 0:512]), 16)
        ctx.dma.inc(sy.dma_start(out=kv_in[129:130, :], in_=kfs[:, 512:1024]), 16)
    ctx.on("sync", _stkv)
    dn = ctx.dma.n

    def _cckv(g, dn=dn):
        g.wait_ge(ctx.dma.h, dn)
        ctx.cc.inc(g.collective_compute("AllReduce", ALU.add,
                                        [[0, 1], [2, 3], [4, 5], [6, 7]],
                                        [kv_in.opt()], [kv_out.opt()]))
    ctx.on("gpsimd", _cckv)
    ccn = ctx.cc.n

    def _ldkv(sy, ccn=ccn):
        sy.wait_ge(ctx.cc.h, ccn)
        ctx.dma.inc(sy.dma_start(out=kvloc[:], in_=kv_out[0:128, :]), 16)
        ctx.dma.inc(sy.dma_start(out=kfs[:, 0:512], in_=kv_out[128:129, :]), 16)
        ctx.dma.inc(sy.dma_start(out=kfs[:, 512:1024], in_=kv_out[129:130, :]), 16)
    ctx.on("sync", _ldkv)
    dn = ctx.dma.n

    def _kva(v, dn=dn):
        v.wait_ge(ctx.dma.h, dn)
        src8 = kvloc[:].rearrange("p (hp c) -> p hp c", hp=8)
        dst8 = kvaug[:].rearrange("p (hp c) -> p hp c", hp=8)[:, :, 0:64]
        SW.inc(v.tensor_copy(dst8, src8))
    ctx.on("vector", _kva)
    for hp in range(8):
        def _trk(te, hp=hp, vg=SW.n, dn=dn):
            te.wait_ge(ctx.dma.h, dn)
            te.wait_ge(SW.h, vg)
            ST.inc(te.transpose(pB[:, 0:1],
                                kfs[0:1, hp * 128:(hp + 1) * 128],
                                ctx.ident[0:1, 0:1]))
        ctx.on("tensor", _trk)
        tn = ST.n

        def _evk2(v, hp=hp, tn=tn):
            v.wait_ge(ST.h, tn)
            SW.inc(v.tensor_copy(kvaug[:, hp * 65 + 64: hp * 65 + 65], pB[:, 0:1]))
        ctx.on("vector", _evk2)
        vn = SW.n
        ctx.on("tensor", lambda te, vn=vn: te.wait_ge(SW.h, vn))
    ctx.barrier()
    _dump("kvaug", kvaug[:])
    if _stop("pkv"):
        return

    qf_fm = bufB
    for i in range(NT):
        def _trq(te, i=i, vg=SW.n):
            te.wait_ge(SW.h, vg)
            srcq = qf_tm[:, i * 1024:(i + 1) * 1024]
            for c in range(NDC):
                inst = te.transpose(ptb[:, c * 128:(c + 1) * 128],
                                    srcq[:, c * 128:(c + 1) * 128], ctx.identb[:])
            ST.inc(inst)
        ctx.on("tensor", _trq)
        tn = ST.n

        def _evq(v, i=i, tn=tn):
            v.wait_ge(ST.h, tn)
            dst = qf_fm[:].rearrange("p (c t) -> p c t", c=NDC)[:, :, i * 128:(i + 1) * 128]
            SW.inc(v.tensor_copy(dst, ptb[:, 0:1024].rearrange("p (c t) -> p c t", c=NDC)))
        ctx.on("vector", _evq)
    ctx.barrier()

    attn_tm = bufC
    for h in range(16):
        for col in range(NT):
            bank = [pA, pB][(h * NT + col) % 2]

            def _mmn(te, h=h, col=col, bank=bank, vg=SW.n):
                te.wait_ge(SW.h, vg)
                ST.inc(te.matmul(
                    bank[:, 0:65],
                    qf_fm[(h % 2) * 64:(h % 2) * 64 + 64,
                          (h // 2) * TPC + col * 128: (h // 2) * TPC + (col + 1) * 128],
                    kvaug[(h % 2) * 64:(h % 2) * 64 + 64,
                          (h // 2) * 65: (h // 2) * 65 + 65],
                    start=True, stop=True, skip_group_check=True))
            ctx.on("tensor", _mmn)
            tn = ST.n

            def _evn(v, h=h, col=col, bank=bank, tn=tn):
                v.wait_ge(ST.h, tn)
                vchain(ctx, v, v.reciprocal(rcol[:, 0:1], bank[:, 64:65]))
                SW.inc(v.tensor_scalar(
                    attn_tm[:, col * 1024 + h * 64: col * 1024 + h * 64 + 64],
                    bank[:, 0:64], rcol[:, 0:1], None, ALU.mult))
            ctx.on("vector", _evn)
    ctx.barrier()
    _dump("attn", attn_tm[:])
    if _stop("pat"):
        return

    attn_fm = bufD
    for i in range(NT):
        def _tra(te, i=i, vg=SW.n):
            te.wait_ge(SW.h, vg)
            srca = attn_tm[:, i * 1024:(i + 1) * 1024]
            for c in range(NDC):
                inst = te.transpose(ptb[:, c * 128:(c + 1) * 128],
                                    srca[:, c * 128:(c + 1) * 128], ctx.identb[:])
            ST.inc(inst)
        ctx.on("tensor", _tra)
        tn = ST.n

        def _eva(v, i=i, tn=tn):
            v.wait_ge(ST.h, tn)
            dst = attn_fm[:].rearrange("p (c t) -> p c t", c=NDC)[:, :, i * 128:(i + 1) * 128]
            SW.inc(v.tensor_copy(dst, ptb[:, 0:1024].rearrange("p (c t) -> p c t", c=NDC)))
        ctx.on("vector", _eva)
    ctx.barrier()

    o1b = hbA
    WoLv = WoL.rearrange("p (k f) -> p k f", k=NDC)

    def _ldwo(sy, guard=ST.n):
        sy.wait_ge(ST.h, guard)
        ctx.dma.inc(sy.dma_start(out=Wb_[0][:], in_=WoLv[:, :, 0:512]), 16)
        ctx.dma.inc(sy.dma_start(out=Wb_[1][:], in_=WoLv[:, :, 512:1024]), 16)
    dn = ld(_ldwo)
    for i in range(NT):
        buf = i % 2
        xv = xt[:, buf * D:(buf + 1) * D]

        def _ldx(sy, i=i, xv=xv, guard=ctx.sem["vector"].n):
            sy.wait_ge(SW.h, guard)
            ctx.dma.inc(sy.dma_start(out=xv, in_=xin[i * 128:(i + 1) * 128, :]), 16)
        ctx.on("sync", _ldx)
        dxn = ctx.dma.n

        for half in range(2):
            bank = [pA, pB][half]

            def _mmo(te, i=i, half=half, bank=bank, dn=dn, vg=SW.n):
                te.wait_ge(ctx.dma.h, dn)
                te.wait_ge(SW.h, vg)
                for k in range(NDC):
                    inst = te.matmul(
                        bank[:], attn_fm[:, k * TPC + i * 128: k * TPC + (i + 1) * 128],
                        Wb_[half][:, k * 512:(k + 1) * 512],
                        start=(k == 0), stop=(k == NDC - 1), skip_group_check=True)
                ST.inc(inst)
            ctx.on("tensor", _mmo)
        tn = ST.n

        def _z(v, i=i, tn=tn, dxn=dxn, xv=xv):
            v.wait_ge(ST.h, tn)
            v.wait_ge(ctx.dma.h, dxn)
            vchain(ctx, v, v.tensor_scalar(lnst[:], xv, rowstats[:, 2 * i:2 * i + 1],
                                           rowstats[:, 2 * i + 1:2 * i + 2],
                                           ALU.subtract, ALU.mult))
            vchain(ctx, v, v.tensor_tensor(lnst[:, 0:512], lnst[:, 0:512], pA[:], ALU.add))
            SW.inc(v.tensor_tensor(lnst[:, 512:1024], lnst[:, 512:1024], pB[:], ALU.add))
        ctx.on("vector", _z)
        vn = SW.n

        def _st1(s, vn=vn):
            s.wait_ge(SW.h, vn)
            SS.inc(s.activation(sqf[:], lnst[:], AF.Square))
        ctx.on("scalar", _st1)
        sqn = SS.n

        def _r1(v, sqn=sqn):
            vchain(ctx, v, v.reduce_sum(rows[:, 0:1], lnst[:], axis=AX))
            v.wait_ge(SS.h, sqn)
            vchain(ctx, v, v.reduce_sum(rows[:, 1:2], sqf[:], axis=AX))
            vchain(ctx, v, v.tensor_scalar(rows[:, 2:3], rows[:, 0:1], ctx.c(1.0 / D), None, ALU.mult))
            vchain(ctx, v, v.tensor_scalar(rows[:, 3:4], rows[:, 1:2], ctx.c(1.0 / D), None, ALU.mult))
            vchain(ctx, v, v.tensor_tensor(rows[:, 4:5], rows[:, 2:3], rows[:, 2:3], ALU.mult))
            vchain(ctx, v, v.tensor_tensor(rows[:, 3:4], rows[:, 3:4], rows[:, 4:5], ALU.subtract))
            SW.inc(v.tensor_scalar(rows[:, 7:8], rows[:, 3:4], ctx.c(1e-6), None, ALU.add))
        ctx.on("vector", _r1)
        vn = SW.n

        def _s1(s, vn=vn):
            s.wait_ge(SW.h, vn)
            SS.inc(s.activation(rows[:, 5:6], rows[:, 7:8], AF.Sqrt))
        ctx.on("scalar", _s1)
        sn = SS.n

        def _a1(v, sn=sn):
            v.wait_ge(SS.h, sn)
            vchain(ctx, v, v.reciprocal(rows[:, 6:7], rows[:, 5:6]))
            vchain(ctx, v, v.tensor_tensor(rows[:, 4:5], rows[:, 6:7], rows[:, 6:7], ALU.mult))
            vchain(ctx, v, v.tensor_tensor(rows[:, 4:5], rows[:, 4:5], rows[:, 7:8], ALU.mult))
            vchain(ctx, v, v.tensor_scalar(rows[:, 4:5], rows[:, 4:5], ctx.c(-0.5), ctx.c(1.5), ALU.mult, ALU.add))
            vchain(ctx, v, v.tensor_tensor(rows[:, 6:7], rows[:, 6:7], rows[:, 4:5], ALU.mult))
            SW.inc(v.tensor_scalar(lnst[:], lnst[:], rows[:, 2:3], rows[:, 6:7],
                                   ALU.subtract, ALU.mult))
        ctx.on("vector", _a1)
        vn = SW.n

        def _tr1(te, vn=vn):
            te.wait_ge(SW.h, vn)
            for c in range(NDC):
                inst = te.transpose(pt2[:, c * 128:(c + 1) * 128],
                                    lnst[:, c * 128:(c + 1) * 128], ctx.ident[:])
            ST.inc(inst)
        ctx.on("tensor", _tr1)
        tn = ST.n

        def _ev1(v, i=i, tn=tn):
            v.wait_ge(ST.h, tn)
            for c in range(NDC):
                v.tensor_scalar(o1f_sl(c, i * 128, (i + 1) * 128),
                                pt2[:, c * 128:(c + 1) * 128],
                                g1c[:, c:c + 1], be1c[:, c:c + 1], ALU.mult, ALU.add)
            for c in range(NDC):
                inst = v.tensor_scalar(o1b[:, c * TPC + i * 128: c * TPC + (i + 1) * 128],
                                       pt2[:, c * 128:(c + 1) * 128],
                                       g1c[:, c:c + 1], be1c[:, c:c + 1], ALU.mult, ALU.add)
            SW.inc(inst)
        ctx.on("vector", _ev1)
    ctx.barrier()
    _dump("o1b", o1b[:])
    if _stop("po1"):
        return

    htile = bufD
    for t in range(NFT):
        for mc in range(32):
            wb = Wb_[mc % 2]

            def _ldw1(sy, mc=mc, wb=wb, guard=ST.n):
                sy.wait_ge(ST.h, guard)
                ctx.dma.inc(sy.dma_start(
                    out=wb[:, 0:1024],
                    in_=W1L[:].rearrange("p (k f) -> p k f", k=NDC)[:, :, mc * 128:(mc + 1) * 128]), 16)
            ctx.on("sync", _ldw1)
            dwn = ctx.dma.n
            bank = [pA, pB][mc % 2]

            def _mm1(te, t=t, mc=mc, wb=wb, bank=bank, dwn=dwn, vg=SW.n, sg=SS.n):
                te.wait_ge(ctx.dma.h, dwn)
                te.wait_ge(SW.h, vg)
                te.wait_ge(SS.h, sg)
                for k in range(NDC):
                    inst = te.matmul(bank[:],
                                     wb[:, k * 128:(k + 1) * 128],
                                     o1b[:, k * TPC + t * 512: k * TPC + t * 512 + 512],
                                     start=(k == 0), stop=(k == NDC - 1), skip_group_check=True)
                ST.inc(inst)
            ctx.on("tensor", _mm1)
            tn = ST.n

            def _el_v(v, mc=mc, bank=bank, tn=tn):
                v.wait_ge(ST.h, tn)
                v.tensor_scalar(t1[:], bank[:], b1c[:, mc:mc + 1], ctx.c(0.0), ALU.add, ALU.min)
                SW.inc(v.tensor_scalar(t2[:], bank[:], b1c[:, mc:mc + 1], ctx.c(0.0), ALU.add, ALU.max))
            ctx.on("vector", _el_v)
            vn = SW.n

            def _el_s(s, vn=vn):
                s.wait_ge(SW.h, vn)
                SS.inc(s.activation(t1[:], t1[:], AF.Exp))
            ctx.on("scalar", _el_s)
            sn = SS.n

            def _el_h(v, mc=mc, sn=sn):
                v.wait_ge(SS.h, sn)
                SW.inc(v.scalar_tensor_tensor(
                    htile[:, mc * 512:(mc + 1) * 512], t1[:], ctx.c(1.0), t2[:],
                    ALU.subtract, ALU.add))
            ctx.on("vector", _el_h)

        for grp in range(8):
            wb = Wb_[grp % 2]

            def _ldw2(sy, grp=grp, wb=wb, guard=ST.n):
                sy.wait_ge(ST.h, guard)
                ctx.dma.inc(sy.dma_start(
                    out=wb[:], in_=W2L[:, grp * 4096:(grp + 1) * 4096]), 16)
            ctx.on("sync", _ldw2)
            dwn = ctx.dma.n
            for oc in range(NDC):
                bank = [pC, pD][oc % 2]

                def _mm2(te, grp=grp, oc=oc, wb=wb, bank=bank, dwn=dwn, vg=SW.n):
                    te.wait_ge(ctx.dma.h, dwn)
                    te.wait_ge(SW.h, vg)
                    for kk in range(4):
                        inst = te.matmul(
                            bank[:], wb[:, kk * 1024 + oc * 128: kk * 1024 + oc * 128 + 128],
                            htile[:, (grp * 4 + kk) * 512:(grp * 4 + kk) * 512 + 512],
                            start=(kk == 0), stop=(kk == 3), skip_group_check=True)
                    ST.inc(inst)
                ctx.on("tensor", _mm2)
                tn = ST.n

                def _acc2(v, t=t, grp=grp, oc=oc, bank=bank, tn=tn):
                    v.wait_ge(ST.h, tn)
                    reg = o1f_sl(oc, t * 512, t * 512 + 512)
                    if grp == 0:
                        SW.inc(v.scalar_tensor_tensor(reg, bank[:], b2c[:, oc:oc + 1],
                                                      reg, ALU.add, ALU.add))
                    else:
                        SW.inc(v.tensor_tensor(reg, reg, bank[:], ALU.add))
                ctx.on("vector", _acc2)
    ctx.barrier()

    _dump("z2a", o1fA[:, 0:8192])
    _dump("z2b", o1fB[:, 0:8192])
    if _stop("pz2"):
        return
    zb = o1b
    for t in range(NFT):
        for k in range(NDC):
            def _cz(s, t=t, k=k):
                sl = slice(k * TPC + t * 512, k * TPC + t * 512 + 512)
                SS.inc(s.activation(zb[:, sl], o1f_sl(k, t * 512, t * 512 + 512), AF.Copy))
            ctx.on("scalar", _cz)
    ctx.barrier()
    emit_fm_ln_apply(ctx, o1f_sl, zb, g2c, be2c, rws, t1, prow, pt2, eps=1e-6)
    ctx.barrier()

    emit_fm_to_tm_add_sl(ctx, xin, xout, o1f_sl, xt, lnst, pt2)


def emit_fm_ln_apply(ctx, zf_sl, zb, gc, bc, rws, tmp, prow, pb, eps):
    SW = ctx.sem["vector"]; SS = ctx.sem["scalar"]; ST = ctx.sem["tensor"]
    sqc = tmp.bitcast(BF16)
    for t in range(NFT):
        for k in range(NDC):
            def _sq(s, t=t, k=k, guard=ST.n, vguard=SW.n):
                s.wait_ge(ST.h, guard)
                if k == 0:
                    s.wait_ge(SW.h, vguard)
                sl = slice(k * TPC + t * 512, k * TPC + t * 512 + 512)
                SS.inc(s.activation(sqc[:, 0:512], zb[:, sl], AF.Square))
            ctx.on("scalar", _sq)
            sn = SS.n

            def _mm(te, t=t, k=k, sn=sn):
                te.wait_ge(SS.h, sn)
                sl = slice(k * TPC + t * 512, k * TPC + t * 512 + 512)
                te.matmul(prow[0][:], ctx.onesk[:], zb[:, sl],
                          start=(k == 0), stop=(k == NDC - 1), skip_group_check=True)
                inst = te.matmul(prow[1][:], ctx.onesk[:], sqc[:, 0:512],
                                 start=(k == 0), stop=(k == NDC - 1), skip_group_check=True)
                ST.inc(inst)
            ctx.on("tensor", _mm)
        tn = ST.n

        def _rows_v(v, tn=tn):
            v.wait_ge(ST.h, tn)
            s0 = tmp[0:1, 0:512]
            mrow = rws[:, 1024:1536]
            vrow = rws[:, 0:512]
            vchain(ctx, v, v.tensor_scalar(mrow, prow[0][:], ctx.c(1.0 / D)[0:1, :], None, ALU.mult))
            vchain(ctx, v, v.tensor_scalar(vrow, prow[1][:], ctx.c(1.0 / D)[0:1, :], None, ALU.mult))
            vchain(ctx, v, v.tensor_tensor(s0, mrow, mrow, ALU.mult))
            vchain(ctx, v, v.tensor_tensor(vrow, vrow, s0, ALU.subtract))
            SW.inc(v.tensor_scalar(vrow, vrow, ctx.c(eps)[0:1, :], None, ALU.add))
        ctx.on("vector", _rows_v)
        vn = SW.n

        def _rows_s(s, vn=vn):
            s.wait_ge(SW.h, vn)
            SS.inc(s.activation(tmp[0:1, 0:512], rws[:, 0:512], AF.Sqrt))
        ctx.on("scalar", _rows_s)
        sn = SS.n

        def _rows_v2(v, sn=sn):
            v.wait_ge(SS.h, sn)
            r1 = rws[:, 512:1024]
            r2 = rws[:, 1024:1536]
            tt = tmp[0:1, 0:512]
            vchain(ctx, v, v.reciprocal(r1, tt))
            vchain(ctx, v, v.tensor_tensor(tt, r1, r1, ALU.mult))
            vchain(ctx, v, v.tensor_tensor(tt, tt, rws[:, 0:512], ALU.mult))
            vchain(ctx, v, v.tensor_scalar(tt, tt, ctx.c(-0.5)[0:1, :], ctx.c(1.5)[0:1, :], ALU.mult, ALU.add))
            vchain(ctx, v, v.tensor_tensor(r1, r1, tt, ALU.mult))
            SW.inc(v.tensor_tensor(r2, r2, r1, ALU.mult))
        ctx.on("vector", _rows_v2)
        vn = SW.n

        def _bc(te, vn=vn):
            te.wait_ge(SW.h, vn)
            te.matmul(pb[:, 0:512], ctx.onesc[:], rws[:, 512:1024], start=True, stop=True)
            inst = te.matmul(pb[:, 512:1024], ctx.onesc[:], rws[:, 1024:1536],
                             start=True, stop=True)
            ST.inc(inst)
        ctx.on("tensor", _bc)
        tn = ST.n

        for k in range(NDC):
            def _ap(v, t=t, k=k, tn=tn):
                v.wait_ge(ST.h, tn)
                sl = slice(k * TPC + t * 512, k * TPC + t * 512 + 512)
                reg = zf_sl(k, t * 512, t * 512 + 512)
                v.tensor_tensor(tmp[:, 0:512], reg, pb[:, 0:512], ALU.mult)
                v.tensor_tensor(tmp[:, 0:512], tmp[:, 0:512], pb[:, 512:1024], ALU.subtract)
                SW.inc(v.tensor_scalar(reg, tmp[:, 0:512],
                                       gc[:, k:k + 1], bc[:, k:k + 1], ALU.mult, ALU.add))
            ctx.on("vector", _ap)


L1A_DBG_SHAPES = {
    "hb": ([128, NDC * TPC], BF16), "qs": ([128, NDC * TPC], BF16),
    "vtm": ([128, NDC * TPC], BF16), "dq": ([128, 16 * NT], F32),
    "mx": ([128, 1], F32), "qf": ([128, NDC * TPC], BF16),
    "kf": ([128, NDC * TPC], BF16), "kvaug": ([128, 8 * 65], BF16),
    "attn": ([128, NDC * TPC], BF16), "o1b": ([128, NDC * TPC], BF16),
    "z2a": ([128, 8192], F32), "z2b": ([128, 8192], F32),
}


def build_l1a(stop_after=None, dbg_tags=()):
    nc = bass.Bass("TRN2", target_bir_lowering=False, debug=False)
    xin = nc.dram_tensor("xin", [TPC, D], F32, kind="ExternalInput").ap()
    WqL = nc.dram_tensor("WqL", [128, NDC * 1024], BF16, kind="ExternalInput").ap()
    WkL = nc.dram_tensor("WkL", [128, NDC * 1024], BF16, kind="ExternalInput").ap()
    WvL = nc.dram_tensor("WvL", [128, NDC * 1024], BF16, kind="ExternalInput").ap()
    WoL = nc.dram_tensor("WoL", [128, NDC * 1024], BF16, kind="ExternalInput").ap()
    projT = nc.dram_tensor("projT", [64, 64], BF16, kind="ExternalInput").ap()
    W1L = nc.dram_tensor("W1L", [128, NDC * 4096], BF16, kind="ExternalInput").ap()
    b1c = nc.dram_tensor("b1c", [128, 32], F32, kind="ExternalInput").ap()
    W2L = nc.dram_tensor("W2L", [128, 32 * 1024], BF16, kind="ExternalInput").ap()
    b2c = nc.dram_tensor("b2c", [128, 8], F32, kind="ExternalInput").ap()
    g1c = nc.dram_tensor("g1c", [128, 8], F32, kind="ExternalInput").ap()
    be1c = nc.dram_tensor("be1c", [128, 8], F32, kind="ExternalInput").ap()
    g2c = nc.dram_tensor("g2c", [128, 8], F32, kind="ExternalInput").ap()
    be2c = nc.dram_tensor("be2c", [128, 8], F32, kind="ExternalInput").ap()
    xout = nc.dram_tensor("xout", [TPC, D], F32, kind="ExternalOutput").ap()
    dbg = {t: nc.dram_tensor("dbg_" + t, list(L1A_DBG_SHAPES[t][0]),
                             L1A_DBG_SHAPES[t][1], kind="ExternalOutput").ap()
           for t in dbg_tags}
    kv_in = nc.dram_tensor("kv_in", [130, 512], F32, kind="Internal").ap()
    kv_out = nc.dram_tensor("kv_out", [130, 512], F32, kind="Internal").ap()
    mx_in = nc.dram_tensor("mx_in", [1, 1], F32, kind="Internal").ap()
    mx_out = nc.dram_tensor("mx_out", [1, 1], F32, kind="Internal").ap()
    with ExitStack() as es:
        ctx = Ctx(nc, es)
        emit_l1a(ctx, xin, xout, WqL, WkL, WvL, WoL, projT, W1L, b1c, W2L, b2c,
                 g1c, be1c, g2c, be2c, kv_in, kv_out, mx_in, mx_out,
                 stop_after=stop_after, dbg=dbg)
        ctx.on("sync", lambda sy: sy.wait_ge(ctx.dmo.h, ctx.dmo.n))
    return nc


def run_l1a(x2_full, g, stop_after=None, dbg_tags=()):
    nc = build_l1a(stop_after, dbg_tags)
    in_maps = []
    for core in range(NCORES):
        b, j = core // 2, core % 2
        in_maps.append({
            "xin": np.ascontiguousarray(x2_full[b, j * TPC:(j + 1) * TPC]),
            "WqL": g["l1_WqL"], "WkL": g["l1_WkL"], "WvL": g["l1_WvL"],
            "WoL": g["l1_WoL"], "projT": g["l1_projT"], "W1L": g["l1_W1L"],
            "b1c": g["l1_b1"], "W2L": g["l1_W2L"], "b2c": g["l1_b2"],
            "g1c": g["l1_ln1g"], "be1c": g["l1_ln1b"],
            "g2c": g["l1_ln2g"], "be2c": g["l1_ln2b"],
        })
    res = run_bass_kernel_spmd(nc, in_maps, core_ids=list(range(NCORES)))
    out = np.empty((B, L, D), np.float32)
    for core in range(NCORES):
        b, j = core // 2, core % 2
        out[b, j * TPC:(j + 1) * TPC] = res.results[core]["xout"]
    if dbg_tags:
        return out, res
    return out





def build_fused(blocks=('a', 'f', 't', 'm')):
    nc = bass.Bass("TRN2", target_bir_lowering=False, debug=False)
    dt = lambda n, s, t, k="ExternalInput": nc.dram_tensor(n, list(s), t, kind=k).ap()
    a_xin = dt("a_xin", [LB, D], F32)
    a_xhalf = dt("a_xhalf", [TPC, D], F32)
    a_WqL = dt("a_WqL", [128, 4096], BF16)
    a_WkL = dt("a_WkL", [128, 4096], BF16)
    a_WvL = dt("a_WvL", [128, 4096], BF16)
    a_WbL = dt("a_WbL", [128, 16], BF16)
    a_dconv = dt("a_dconv", [3, 4, 4, 128, 128], BF16)
    a_dfirS = dt("a_dfirS", [5, 4, 128, 128], BF16)
    a_dfirL = dt("a_dfirL", [64, 4, 128, 128], BF16)
    a_WoL = dt("a_WoL", [128, 4096], BF16)
    a_onwc = dt("a_onwc", [128, 4], F32)
    a_w1hsL = dt("a_w1hsL", [128, 8192], BF16)
    a_w1st = dt("a_w1st", [16, 1024], BF16)
    a_w2L = dt("a_w2L", [128, 32], BF16)
    a_b1c = dt("a_b1c", [128, 8], F32)
    a_b2r = dt("a_b2r", [4, 1], F32)
    a_ltr = dt("a_ltr", [1, 4], F32)
    rs_in = dt("rs_in", [LB, D], F32, "Internal")
    rs_out = dt("rs_out", [TPC, D], F32, "Internal")
    f_qwT = dt("f_qwT", [6, D, D], I32)
    f_screpT = dt("f_screpT", [6, D, D], F32)
    f_bias6 = dt("f_bias6", [6, 128, 8], F32)
    f_laT = dt("f_laT", [6, 128, 256], BF16)
    f_lbT = dt("f_lbT", [6, 32, D], BF16)
    f_gam = dt("f_gam", [3, 128, 8], F32)
    f_bet = dt("f_bet", [3, 128, 8], F32)
    t_WqL = dt("t_WqL", [128, NDC * 1024], BF16)
    t_WkL = dt("t_WkL", [128, NDC * 1024], BF16)
    t_WvL = dt("t_WvL", [128, NDC * 1024], BF16)
    t_WoL = dt("t_WoL", [128, NDC * 1024], BF16)
    t_projT = dt("t_projT", [64, 64], BF16)
    t_W1L = dt("t_W1L", [128, NDC * 4096], BF16)
    t_b1c = dt("t_b1c", [128, 32], F32)
    t_W2L = dt("t_W2L", [128, 32 * 1024], BF16)
    t_b2c = dt("t_b2c", [128, 8], F32)
    t_g1c = dt("t_g1c", [128, 8], F32)
    t_be1c = dt("t_be1c", [128, 8], F32)
    t_g2c = dt("t_g2c", [128, 8], F32)
    t_be2c = dt("t_be2c", [128, 8], F32)
    kv_in = dt("kv_in", [130, 512], F32, "Internal")
    kv_out = dt("kv_out", [130, 512], F32, "Internal")
    mx_in = dt("mx_in", [1, 1], F32, "Internal")
    mx_out = dt("mx_out", [1, 1], F32, "Internal")
    m_EX = dt("m_EX", [10, 128, 3 * 4096], BF16)
    m_gateTf = dt("m_gateTf", [128, 64], F32)
    m_lbrow = dt("m_lbrow", [8, 1], F32)
    cc_in = dt("cc_in", [1, 16], F32, "Internal")
    cc_out = dt("cc_out", [1, 16], F32, "Internal")
    s1 = dt("s1", [TPC, D], F32, "Internal")
    s2 = dt("s2", [TPC, D], F32, "Internal")
    s3 = dt("s3", [TPC, D], F32, "Internal")
    xout = dt("xout", [TPC, D], F32, "ExternalOutput")

    chain = []
    if 'a' in blocks:
        chain.append(('a', lambda c: emit_l0a(
            c, a_xin, a_xhalf, s1, a_WqL, a_WkL, a_WvL, a_WbL, a_dconv,
            a_dfirS, a_dfirL, a_WoL, a_onwc, a_w1hsL, a_w1st, a_w2L,
            a_b1c, a_b2r, a_ltr, rs_in, rs_out)))
    if 'f' in blocks:
        fin = s1 if 'a' in blocks else a_xhalf
        chain.append(('f', lambda c, fin=fin: emit_l0f(
            c, fin, s2, f_qwT, f_screpT, f_bias6, f_laT, f_lbT, f_gam, f_bet)))
    if 't' in blocks:
        tin = s2 if 'f' in blocks else a_xhalf
        chain.append(('t', lambda c, tin=tin: emit_l1a(
            c, tin, s3, t_WqL, t_WkL, t_WvL, t_WoL, t_projT, t_W1L, t_b1c,
            t_W2L, t_b2c, t_g1c, t_be1c, t_g2c, t_be2c, kv_in, kv_out,
            mx_in, mx_out)))
    if 'm' in blocks:
        min_ = s3 if 't' in blocks else a_xhalf
        chain.append(('m', lambda c, min_=min_: emit_l1f(
            c, min_, xout, m_EX, m_gateTf, m_lbrow, cc_in, cc_out)))
    outs = {'a': s1, 'f': s2, 't': s3, 'm': xout}

    with ExitStack() as es0:
        ctx = Ctx(nc, es0)
        for bi, (tag, fn) in enumerate(chain):
            with ExitStack() as eb:
                ctx.es = eb
                fn(ctx)
                if bi < len(chain) - 1 and SEAM_MODE == "reset":
                    ctx.seam_reset()
                else:
                    ctx.barrier()
        ctx.es = es0
        if chain[-1][0] != 'm':
            sfin = outs[chain[-1][0]]
            def _cp(sy, sfin=sfin):
                for i in range(4):
                    ctx.dmo.inc(sy.dma_start(out=xout[i * 512:(i + 1) * 512, :],
                                             in_=sfin[i * 512:(i + 1) * 512, :]), 16)
            ctx.on("sync", _cp)
        ctx.on("sync", lambda sy: sy.wait_ge(ctx.dmo.h, ctx.dmo.n))
    return nc


FUSED_BLOCKS = ('a', 'f', 't', 'm')
SEAM_MODE = 'plain'


def kernel(**inputs):
    g = _host_prep(inputs)
    nc = build_fused(FUSED_BLOCKS)
    in_maps = []
    for core in range(NCORES):
        b, hp = core // 2, core % 2
        lt4 = g["l0_lt"]
        ltcore = np.zeros((1, 4), np.float32)
        ltcore[0, 0:2] = lt4[0, 2 * hp: 2 * hp + 2]
        in_maps.append({
            "a_xin": np.ascontiguousarray(g["x"][b]),
            "a_xhalf": np.ascontiguousarray(g["x"][b, hp * TPC:(hp + 1) * TPC]),
            "a_WqL": g[f"l0_WqL{hp}"], "a_WkL": g[f"l0_WkL{hp}"],
            "a_WvL": g[f"l0_WvL{hp}"], "a_WbL": g[f"l0_WbL{hp}"],
            "a_dconv": g[f"l0_dconv{hp}"], "a_dfirS": g[f"l0_dfirS{hp}"],
            "a_dfirL": g[f"l0_dfirL{hp}"], "a_WoL": g[f"l0_WoL{hp}"],
            "a_onwc": g[f"l0_onw{hp}"], "a_w1hsL": g["l0_w1hsL"],
            "a_w1st": g["l0_w1st"], "a_w2L": g["l0_w2L"], "a_b1c": g["l0_b1"],
            "a_b2r": g["l0_b2"], "a_ltr": ltcore,
            "f_qwT": g["l0f_qwT"], "f_screpT": g["l0f_screpT"],
            "f_bias6": g["l0f_bias"], "f_laT": g["l0f_laT"], "f_lbT": g["l0f_lbT"],
            "f_gam": g["l0f_gam"], "f_bet": g["l0f_bet"],
            "t_WqL": g["l1_WqL"], "t_WkL": g["l1_WkL"], "t_WvL": g["l1_WvL"],
            "t_WoL": g["l1_WoL"], "t_projT": g["l1_projT"], "t_W1L": g["l1_W1L"],
            "t_b1c": g["l1_b1"], "t_W2L": g["l1_W2L"], "t_b2c": g["l1_b2"],
            "t_g1c": g["l1_ln1g"], "t_be1c": g["l1_ln1b"],
            "t_g2c": g["l1_ln2g"], "t_be2c": g["l1_ln2b"],
            "m_EX": g["m_EX"], "m_gateTf": g["m_gateTf"], "m_lbrow": g["m_lb"],
        })
    res = run_bass_kernel_spmd(nc, in_maps, core_ids=list(range(NCORES)))
    out = np.empty((B, L, D), np.float32)
    for core in range(NCORES):
        b, hp = core // 2, core % 2
        out[b, hp * TPC:(hp + 1) * TPC] = res.results[core]["xout"]
    return out



def emit_fm_to_tm_add_sl(ctx, xin, xout, sl_fn, xt, ost, pt2, ntile=NT):
    for i in range(ntile):
        buf = i % 2
        xv = xt[:, buf * D:(buf + 1) * D]

        def _load(sy, i=i, xv=xv):
            sy.wait_ge(ctx.sem["vector"].h, ctx.sem["vector"].n)
            ctx.dma.inc(sy.dma_start(out=xv, in_=xin[i * 128:(i + 1) * 128, :]), 16)
        ctx.on("sync", _load)
        dn = ctx.dma.n

        def _tr(te, i=i):
            te.wait_ge(ctx.sem["vector"].h, ctx.sem["vector"].n)
            for c in range(NDC):
                inst = te.transpose(pt2[:, c * 128:(c + 1) * 128],
                                    sl_fn(c, i * 128, (i + 1) * 128), ctx.ident[:])
            ctx.sem["tensor"].inc(inst)
        ctx.on("tensor", _tr)
        tn = ctx.sem["tensor"].n

        def _add(v, tn=tn, dn=dn, xv=xv):
            v.wait_ge(ctx.sem["tensor"].h, tn)
            v.wait_ge(ctx.dma.h, dn)
            ctx.sem["vector"].inc(v.tensor_tensor(ost[:], pt2[:], xv, ALU.add))
        ctx.on("vector", _add)
        vn = ctx.sem["vector"].n

        def _store(sy, i=i, vn=vn):
            sy.wait_ge(ctx.sem["vector"].h, vn)
            ctx.dmo.inc(sy.dma_start(out=xout[i * 128:(i + 1) * 128, :], in_=ost[:]), 16)
        ctx.on("sync", _store)
        dmo_n = ctx.dmo.n

        def _guard(v, dmo_n=dmo_n):
            v.wait_ge(ctx.dmo.h, dmo_n)
        ctx.on("vector", _guard)


# version 08 (final)
# speedup vs baseline: 1.1379x; 1.1379x over previous
import numpy as np
import ml_dtypes
from contextlib import ExitStack

import concourse.bass as bass
import concourse.mybir as mybir
from concourse.bass_utils import run_bass_kernel_spmd

F32 = mybir.dt.float32
BF16 = mybir.dt.bfloat16
I32 = mybir.dt.int32
AF = mybir.ActivationFunctionType
ALU = mybir.AluOpType
AX = mybir.AxisListType.X
BF = ml_dtypes.bfloat16

NCORES = 8
B, L, D = 4, 4096, 1024
TPC = 2048
NT = TPC // 128
NFT = TPC // 512
NDC = D // 128


class Sem:
    def __init__(self, es, nc, name):
        self.h = es.enter_context(nc.semaphore(name))
        self.n = 0

    def inc(self, inst, k=1):
        inst.then_inc(self.h, k)
        self.n += k


class Ctx:
    ENGINES = ("sync", "vector", "scalar", "tensor", "gpsimd")

    def __init__(self, nc, es):
        self.nc = nc
        self.es = es
        self.block = es.enter_context(nc.Block())
        self.sem = {e: Sem(es, nc, f"s_{e}") for e in self.ENGINES}
        self.dma = Sem(es, nc, "s_dma")
        self.dmo = Sem(es, nc, "s_dmo")
        self.cc = Sem(es, nc, "s_cc")
        self.epoch = Sem(es, nc, "s_epoch")
        self._consts = {}
        self.ident = None
        self.identb = None
        self.onesk = None
        self.onesc = None
        self.onescb = None

    def on(self, eng, fn):
        getattr(self.block, eng)(fn)

    _uid = 0

    def sbuf(self, name, shape, dtype):
        Ctx._uid += 1
        return self.es.enter_context(
            self.nc.sbuf_tensor(f"{name}_{Ctx._uid}", shape, dtype))

    def psum(self, name, shape, dtype=F32):
        Ctx._uid += 1
        return self.es.enter_context(
            self.nc.psum_tensor(f"{name}_{Ctx._uid}", shape, dtype))

    def const_pool(self, values):
        vals = sorted(set(float(v) for v in values))
        tile = self.sbuf("constpool", [128, max(len(vals), 1)], F32)
        for i, v in enumerate(vals):
            self._consts[v] = tile[:, i:i + 1]

    def c(self, v):
        return self._consts[float(v)]

    def emit_init(self):
        self.ident = self.sbuf("c_ident", [128, 128], F32)
        self.identb = self.sbuf("c_identb", [128, 128], BF16)
        self.onesk = self.sbuf("c_onesk", [128, 1], BF16)
        self.oneskf = self.sbuf("c_oneskf", [128, 1], F32)
        self.onesc = self.sbuf("c_onesc", [1, 128], F32)
        self.onescb = self.sbuf("c_onescb", [1, 128], BF16)

        def _f(g):
            for v, ap in self._consts.items():
                g.memset(ap, v)
            g.memset(self.onesk[:], 1.0)
            g.memset(self.oneskf[:], 1.0)
            g.memset(self.onesc[:], 1.0)
            g.memset(self.onescb[:], 1.0)
            for t in (self.ident, self.identb):
                g.memset(t[:], 0.0)
                inst = g.affine_select(out=t[:], in_=t[:], compare_op=ALU.not_equal,
                                       fill=1.0, base=0, pattern=[[-1, 128]],
                                       channel_multiplier=1)
            inst = g.tensor_copy(self.onescb[:], self.onescb[:])
            self.sem["gpsimd"].inc(inst)
        self.on("gpsimd", _f)
        self.barrier()

    def barrier(self):
        sems = [self.sem[e] for e in self.ENGINES] + [self.dma, self.dmo, self.cc]
        counts = [(s, s.n) for s in sems if s.n]
        for e in self.ENGINES:
            def _f(eng, counts=counts):
                for s, n in counts:
                    eng.wait_ge(s.h, n)
            self.on(e, _f)

    def seam_reset(self):
        self.barrier()
        sems = [self.sem[e] for e in self.ENGINES] + [self.dma, self.dmo, self.cc]
        def _clr(g, sems=sems):
            inst = None
            for s in sems:
                inst = g.sem_clear(s.h)
                s.n = 0
            self.epoch.inc(inst if inst is not None else g.engine_nop())
        self.on("gpsimd", _clr)
        en = self.epoch.n
        for e in self.ENGINES:
            self.on(e, lambda eng, en=en: eng.wait_ge(self.epoch.h, en))



def vchain(ctx, v, inst):
    ctx.sem["vector"].inc(inst)
    v.wait_ge(ctx.sem["vector"].h, ctx.sem["vector"].n)



def _host_prep(inputs):
    g = {}
    f32 = lambda a: np.ascontiguousarray(np.asarray(a), dtype=np.float32)
    b16 = lambda a: np.ascontiguousarray(np.asarray(a).astype(BF))

    g["x"] = f32(inputs["L0_attn_hidden_states"])

    def w_lay0(a):
        Dd, O = a.shape
        return np.ascontiguousarray(
            a.reshape(Dd // 128, 128, O).transpose(1, 0, 2).reshape(128, (Dd // 128) * O))

    def diag_stack(w_cols, taps):
        out = np.zeros((taps, 4, 128, 128), dtype=BF)
        idx = np.arange(128)
        for tap in range(taps):
            for c in range(4):
                out[tap, c, idx, idx] = w_cols[c * 128:(c + 1) * 128, tap].astype(BF)
        return out

    Wq0 = np.asarray(inputs["L0_attn_Wq"], np.float32)
    Wk0 = np.asarray(inputs["L0_attn_Wk"], np.float32)
    Wv0 = np.asarray(inputs["L0_attn_Wv"], np.float32)
    Wb0 = np.asarray(inputs["L0_attn_Wb"], np.float32)
    Wo0 = np.asarray(inputs["L0_attn_Wo"], np.float32)
    cq = f32(inputs["L0_attn_conv_q_w"])
    ck = f32(inputs["L0_attn_conv_k_w"])
    cv = f32(inputs["L0_attn_conv_v_w"])
    fs = f32(inputs["L0_attn_fir_short_w"]).reshape(1024, 5)
    fl = f32(inputs["L0_attn_fir_long_w"]).reshape(1024, 64)
    onw = f32(inputs["L0_attn_o_norm_w"])
    for hp in range(2):
        sl = slice(hp * 512, (hp + 1) * 512)
        g[f"l0_WqL{hp}"] = b16(w_lay0(Wq0[:, sl]))
        g[f"l0_WkL{hp}"] = b16(w_lay0(Wk0[:, sl]))
        g[f"l0_WvL{hp}"] = b16(w_lay0(Wv0[:, sl]))
        g[f"l0_WbL{hp}"] = b16(w_lay0(Wb0[:, 2 * hp:2 * hp + 2]))
        g[f"l0_dconv{hp}"] = np.concatenate([
            diag_stack(cq[sl], 4)[None], diag_stack(ck[sl], 4)[None],
            diag_stack(cv[sl], 4)[None]], axis=0)
        g[f"l0_dfirS{hp}"] = diag_stack(fs[sl], 5)
        g[f"l0_dfirL{hp}"] = diag_stack(fl[sl], 64)
        g[f"l0_WoL{hp}"] = b16(w_lay0(Wo0[sl, :]))
        ch = (np.arange(512) % 256)
        g[f"l0_onw{hp}"] = np.ascontiguousarray(
            onw[ch].reshape(4, 128).T.astype(np.float32))
    g["l0_w1hsL"] = b16(w_lay0(np.asarray(inputs["L0_attn_mlp_w1"], np.float32)[:D]))
    g["l0_w1st"] = b16(inputs["L0_attn_mlp_w1"][D:])
    g["l0_b1"] = np.ascontiguousarray(
        f32(inputs["L0_attn_mlp_b1"]).reshape(8, 128).T)
    g["l0_w2L"] = b16(w_lay0(np.asarray(inputs["L0_attn_mlp_w2"], np.float32)))
    g["l0_b2"] = f32(inputs["L0_attn_mlp_b2"]).reshape(4, 1)
    g["l0_lt"] = f32(inputs["L0_attn_log_temp"]).reshape(1, 4)

    qw = np.asarray(inputs["L0_ffn_qweights"])
    g["l0f_qwT"] = np.ascontiguousarray(np.transpose(qw, (0, 2, 1)))
    sc = f32(inputs["L0_ffn_scales"])
    g["l0f_screpT"] = np.ascontiguousarray(
        np.transpose(np.repeat(sc, 16, axis=2), (0, 2, 1)))
    g["l0f_bias"] = np.ascontiguousarray(
        f32(inputs["L0_ffn_biases"]).reshape(6, 8, 128).transpose(0, 2, 1))
    laT = np.transpose(np.asarray(inputs["L0_ffn_lora_a"]), (0, 2, 1))
    g["l0f_laT"] = b16(laT.reshape(6, 8, 128, 32).transpose(0, 2, 1, 3).reshape(6, 128, 256))
    g["l0f_lbT"] = b16(np.transpose(np.asarray(inputs["L0_ffn_lora_b"]), (0, 2, 1)))
    g["l0f_gam"] = np.ascontiguousarray(
        f32(inputs["L0_ffn_ln_gamma"]).reshape(3, 8, 128).transpose(0, 2, 1))
    g["l0f_bet"] = np.ascontiguousarray(
        f32(inputs["L0_ffn_ln_beta"]).reshape(3, 8, 128).transpose(0, 2, 1))

    def w_lay(a):
        Dd, O = a.shape
        return np.ascontiguousarray(
            a.reshape(Dd // 128, 128, O).transpose(1, 0, 2).reshape(128, (Dd // 128) * O))
    g["l1_WqL"] = b16(w_lay(np.asarray(inputs["L1_attn_Wq"], np.float32).reshape(D, 1024)))
    g["l1_WkL"] = b16(w_lay(np.asarray(inputs["L1_attn_Wk"], np.float32).reshape(D, 1024)))
    g["l1_WvL"] = b16(w_lay(np.asarray(inputs["L1_attn_Wv"], np.float32).reshape(D, 1024)))
    g["l1_WoL"] = b16(w_lay(np.asarray(inputs["L1_attn_Wo"], np.float32).reshape(1024, D)))
    g["l1_projT"] = b16(np.asarray(inputs["L1_attn_proj"]).T)
    g["l1_W1L"] = b16(w_lay(np.asarray(inputs["L1_attn_W1"], np.float32)))
    g["l1_b1"] = np.ascontiguousarray(f32(inputs["L1_attn_b1"]).reshape(32, 128).T)
    g["l1_W2L"] = b16(w_lay(np.asarray(inputs["L1_attn_W2"], np.float32)))
    g["l1_b2"] = np.ascontiguousarray(f32(inputs["L1_attn_b2"]).reshape(8, 128).T)
    for nm, key in (("ln1_g", "l1_ln1g"), ("ln1_b", "l1_ln1b"),
                    ("ln2_g", "l1_ln2g"), ("ln2_b", "l1_ln2b")):
        g[key] = np.ascontiguousarray(f32(inputs["L1_attn_" + nm]).reshape(8, 128).T)

    gateT = f32(inputs["L1_ffn_gate_w"]).T
    g["m_gateTf"] = np.ascontiguousarray(
        gateT.reshape(8, 128, 8).transpose(1, 0, 2).reshape(128, 64))
    g["m_lb"] = f32(inputs["L1_ffn_lb_bias"]).reshape(8, 1)
    egT = np.transpose(np.asarray(inputs["L1_ffn_expert_gate_w"]), (0, 2, 1))
    euT = np.transpose(np.asarray(inputs["L1_ffn_expert_up_w"]), (0, 2, 1))
    edT = np.transpose(np.asarray(inputs["L1_ffn_expert_down_w"]), (0, 2, 1))
    sgT = np.asarray(inputs["L1_ffn_shared_gate_w"]).T
    suT = np.asarray(inputs["L1_ffn_shared_up_w"]).T
    sdT = np.asarray(inputs["L1_ffn_shared_down_w"]).T
    def up_lay(a):
        return a.reshape(8, 128, 512).transpose(1, 0, 2).reshape(128, 4096)
    def dn_lay(a):
        return a.reshape(4, 128, 1024).transpose(1, 0, 2).reshape(128, 4096)
    ex = np.zeros((10, 128, 3 * 4096), dtype=BF)
    for e in range(8):
        ex[e, :, 0:4096] = up_lay(egT[e]).astype(BF)
        ex[e, :, 4096:8192] = up_lay(euT[e]).astype(BF)
        ex[e, :, 8192:12288] = dn_lay(edT[e]).astype(BF)
    for h in range(2):
        ex[8 + h, :, 0:4096] = up_lay(sgT[:, h * 512:(h + 1) * 512]).astype(BF)
        ex[8 + h, :, 4096:8192] = up_lay(suT[:, h * 512:(h + 1) * 512]).astype(BF)
        ex[8 + h, :, 8192:12288] = dn_lay(sdT[h * 512:(h + 1) * 512, :]).astype(BF)
    g["m_EX"] = ex
    return g



def emit_tm_ln_to_fm(ctx, xin, xt, lnst, rows, sqf, hf, hb, pt2, ntile=NT,
                     tpc=TPC, eps=1e-5, hb_only=False, stageF=None, hook=None,
                     rowsave=None):
    for i in range(ntile):
        buf = i % 2
        xv = xt[:, buf * D:(buf + 1) * D]

        def _load(sy, i=i, xv=xv):
            sy.wait_ge(ctx.sem["tensor"].h, ctx.sem["tensor"].n)
            ctx.dma.inc(sy.dma_start(out=xv, in_=xin[i * 128:(i + 1) * 128, :]), 16)
        ctx.on("sync", _load)
        dn = ctx.dma.n

        def _stats_s(s, xv=xv, dn=dn, vn_prev=ctx.sem["vector"].n):
            s.wait_ge(ctx.dma.h, dn)
            s.wait_ge(ctx.sem["vector"].h, vn_prev)
            ctx.sem["scalar"].inc(s.activation(sqf[:], xv, AF.Square))
        ctx.on("scalar", _stats_s)
        sqn = ctx.sem["scalar"].n

        def _rowmath_v(v, xv=xv, dn=dn, sn_prev=ctx.sem["scalar"].n, sqn=sqn):
            v.wait_ge(ctx.dma.h, dn)
            v.wait_ge(ctx.sem["scalar"].h, sn_prev)
            vchain(ctx, v, v.reduce_sum(rows[:, 0:1], xv, axis=AX))
            v.wait_ge(ctx.sem["scalar"].h, sqn)
            vchain(ctx, v, v.reduce_sum(rows[:, 1:2], sqf[:], axis=AX))
            vchain(ctx, v, v.tensor_scalar(rows[:, 2:3], rows[:, 0:1], ctx.c(1.0 / D), None, ALU.mult))
            vchain(ctx, v, v.tensor_scalar(rows[:, 3:4], rows[:, 1:2], ctx.c(1.0 / D), None, ALU.mult))
            vchain(ctx, v, v.tensor_tensor(rows[:, 4:5], rows[:, 2:3], rows[:, 2:3], ALU.mult))
            vchain(ctx, v, v.tensor_tensor(rows[:, 3:4], rows[:, 3:4], rows[:, 4:5], ALU.subtract))
            ctx.sem["vector"].inc(
                v.tensor_scalar(rows[:, 7:8], rows[:, 3:4], ctx.c(eps), None, ALU.add))
        ctx.on("vector", _rowmath_v)
        vn = ctx.sem["vector"].n

        def _rowmath_s(s, vn=vn):
            s.wait_ge(ctx.sem["vector"].h, vn)
            ctx.sem["scalar"].inc(
                s.activation(rows[:, 5:6], rows[:, 7:8], AF.Sqrt))
        ctx.on("scalar", _rowmath_s)
        sn = ctx.sem["scalar"].n

        def _apply_v(v, xv=xv, sn=sn, _i=i):
            v.wait_ge(ctx.sem["scalar"].h, sn)
            vchain(ctx, v, v.reciprocal(rows[:, 6:7], rows[:, 5:6]))
            vchain(ctx, v, v.tensor_tensor(rows[:, 4:5], rows[:, 6:7], rows[:, 6:7], ALU.mult))
            vchain(ctx, v, v.tensor_tensor(rows[:, 4:5], rows[:, 4:5], rows[:, 7:8], ALU.mult))
            vchain(ctx, v, v.tensor_scalar(rows[:, 4:5], rows[:, 4:5], ctx.c(-0.5), ctx.c(1.5),
                            ALU.mult, ALU.add))
            vchain(ctx, v, v.tensor_tensor(rows[:, 6:7], rows[:, 6:7], rows[:, 4:5], ALU.mult))
            if rowsave is not None:
                v.tensor_copy(rowsave[:, 2 * _i:2 * _i + 1], rows[:, 2:3])
                v.tensor_copy(rowsave[:, 2 * _i + 1:2 * _i + 2], rows[:, 6:7])
            ctx.sem["vector"].inc(
                v.tensor_scalar(lnst[:], xv, rows[:, 2:3], rows[:, 6:7],
                                ALU.subtract, ALU.mult))
        ctx.on("vector", _apply_v)
        vn = ctx.sem["vector"].n

        def _tr(te, vn=vn):
            te.wait_ge(ctx.sem["vector"].h, vn)
            te.wait_ge(ctx.sem["vector"].h, vn)
            for c in range(NDC):
                inst = te.transpose(pt2[:, c * 128:(c + 1) * 128],
                                    lnst[:, c * 128:(c + 1) * 128], ctx.ident[:])
            ctx.sem["tensor"].inc(inst)
        ctx.on("tensor", _tr)
        tn = ctx.sem["tensor"].n

        def _evac(v, i=i, tn=tn):
            v.wait_ge(ctx.sem["tensor"].h, tn)
            src = pt2[:].rearrange("p (c t) -> p c t", c=NDC)
            if not hb_only:
                dstf = hf[:].rearrange("p (c t) -> p c t", c=NDC)[:, :, i * 128:(i + 1) * 128]
                v.tensor_copy(dstf, src)
            if stageF is not None:
                v.tensor_copy(stageF[:], pt2[:])
            dstb = hb[:].rearrange("p (c t) -> p c t", c=NDC)[:, :, i * 128:(i + 1) * 128]
            ctx.sem["vector"].inc(v.tensor_copy(dstb, src))
        ctx.on("vector", _evac)
        if hook is not None:
            hook(i)


def emit_fm_to_tm_add(ctx, xin, xout, hf, xt, ost, pt2, ntile=NT, tpc=TPC):
    for i in range(ntile):
        buf = i % 2
        xv = xt[:, buf * D:(buf + 1) * D]

        def _load(sy, i=i, xv=xv):
            sy.wait_ge(ctx.sem["vector"].h, ctx.sem["vector"].n)
            ctx.dma.inc(sy.dma_start(out=xv, in_=xin[i * 128:(i + 1) * 128, :]), 16)
        ctx.on("sync", _load)
        dn = ctx.dma.n

        def _tr(te, i=i):
            te.wait_ge(ctx.sem["vector"].h, ctx.sem["vector"].n)
            hfv = hf[:].rearrange("p (c t) -> p c t", c=NDC)
            for c in range(NDC):
                inst = te.transpose(pt2[:, c * 128:(c + 1) * 128],
                                    hfv[:, c, i * 128:(i + 1) * 128], ctx.ident[:])
            ctx.sem["tensor"].inc(inst)
        ctx.on("tensor", _tr)
        tn = ctx.sem["tensor"].n

        def _add(v, tn=tn, dn=dn, xv=xv):
            v.wait_ge(ctx.sem["tensor"].h, tn)
            v.wait_ge(ctx.dma.h, dn)
            ctx.sem["vector"].inc(v.tensor_tensor(ost[:], pt2[:], xv, ALU.add))
        ctx.on("vector", _add)
        vn = ctx.sem["vector"].n

        def _store(sy, i=i, vn=vn):
            sy.wait_ge(ctx.sem["vector"].h, vn)
            ctx.dmo.inc(sy.dma_start(out=xout[i * 128:(i + 1) * 128, :], in_=ost[:]), 16)
        ctx.on("sync", _store)
        dmo_n = ctx.dmo.n

        def _guard(v, dmo_n=dmo_n):
            v.wait_ge(ctx.dmo.h, dmo_n)
        ctx.on("vector", _guard)



def emit_l0f(ctx, xin, xout, qwT, screpT, bias6, laT, lbT, gam, bet,
             stop_after=None, dbg=None):
    ctx.const_pool([1.0 / D, 1e-5, 0.0, -0.5, 1.5])
    ctx.emit_init()

    xt = ctx.sbuf("xt", [128, 2 * D], F32)
    lnst = ctx.sbuf("lnst", [128, D], F32)
    hf = ctx.sbuf("hf", [128, NDC * TPC], F32)
    hb = ctx.sbuf("hb", [128, NDC * TPC], BF16)
    yb = ctx.sbuf("yb", [128, NDC * TPC], BF16)
    W = ctx.sbuf("W", [128, NDC * D], BF16)
    wstage = ctx.sbuf("wstage", [128, D], I32)
    wf = ctx.sbuf("wf", [128, D], F32)
    wsc = ctx.sbuf("wsc", [128, D], F32)
    la = ctx.sbuf("la", [128, NDC * 32], BF16)
    lbt = ctx.sbuf("lbt", [32, D], BF16)
    rb = ctx.sbuf("rb", [32, TPC], BF16)
    bcol = ctx.sbuf("bcol", [128, 8], F32)
    gcol = ctx.sbuf("gcol", [128, 8], F32)
    bbcol = ctx.sbuf("bbcol", [128, 8], F32)
    rows = ctx.sbuf("rows", [128, 8], F32)
    rws = ctx.sbuf("rws", [1, 6 * 512], F32)
    sq = ctx.sbuf("sq", [128, 512], BF16)
    sqf = ctx.sbuf("sqf", [128, D], F32)
    tmp = ctx.sbuf("tmpf", [128, 512], F32)

    pm = [ctx.psum(f"pm{i}", [128, 512]) for i in range(2)]
    pb = ctx.psum("pb", [128, 1024])
    prow = [ctx.psum(f"prow{i}", [1, 512]) for i in range(2)]
    pt2 = ctx.psum("pt2", [128, 1024])

    def _dump(tag, tile, width):
        if dbg is not None and tag in dbg:
            ctx.barrier()
            def _st(sy, tile=tile, ap=dbg[tag], width=width):
                ctx.dmo.inc(sy.dma_start(out=ap, in_=tile[:, 0:width]), 16)
            ctx.on("sync", _st)

    emit_tm_ln_to_fm(ctx, xin, xt, lnst, rows, sqf, hf, hb, pt2)
    ctx.barrier()
    _dump("p1_hf", hf, NDC * TPC)
    _dump("p1_hb", hb, NDC * TPC)
    if stop_after == "p1":
        return

    for blk in range(3):
        emit_qlora(ctx, blk * 2, hb, yb, qwT, screpT, bias6, laT, lbT,
                   W, wstage, wf, wsc, la, lbt, rb, bcol, pm, "bias_only")
        ctx.barrier()
        if blk == 0:
            _dump("q0_yb", yb, NDC * TPC)
            _dump("q0_W", W, NDC * D)
            _dump("q0_rb", rb, TPC)
        if stop_after == f"q{blk}a":
            return
        emit_fm_ln_gelu(ctx, blk, yb, gam, bet, gcol, bbcol, rws, sq, tmp, prow, pb)
        ctx.barrier()
        if blk == 0:
            _dump("ln0_yb", yb, NDC * TPC)
        if stop_after == f"q{blk}ln":
            return
        emit_qlora(ctx, blk * 2 + 1, yb, None, qwT, screpT, bias6, laT, lbT,
                   W, wstage, wf, wsc, la, lbt, rb, bcol, pm, "residual",
                   hf=hf, hb=hb)
        ctx.barrier()
        if blk == 0:
            _dump("r0_hf", hf, NDC * TPC)
        if stop_after == f"q{blk}b":
            return

    emit_fm_to_tm_add(ctx, xin, xout, hf, xt, lnst, pt2)


def emit_qlora(ctx, m, src, dst, qwT, screpT, bias6, laT, lbT, W, wstage, wf,
               wsc, la, lbt, rb, bcol, pm, evac, hf=None, hb=None):
    def _small(sy, m=m):
        ctx.dma.inc(sy.dma_start(out=la[:], in_=laT[m]), 16)
        ctx.dma.inc(sy.dma_start(out=lbt[:], in_=lbT[m]), 16)
        ctx.dma.inc(sy.dma_start(out=bcol[:], in_=bias6[m]), 16)
    ctx.on("sync", _small)

    for c in range(NDC):
        def _ld(sy, m=m, c=c):
            sy.wait_ge(ctx.sem["vector"].h, ctx.sem["vector"].n)
            ctx.dma.inc(sy.dma_start(out=wstage[:], in_=qwT[m, c * 128:(c + 1) * 128, :]), 16)
            ctx.dma.inc(sy.dma_start(out=wsc[:], in_=screpT[m, c * 128:(c + 1) * 128, :]), 16)
        ctx.on("sync", _ld)
        dn = ctx.dma.n

        def _deq_s(s, dn=dn):
            s.wait_ge(ctx.dma.h, dn)
            ctx.sem["scalar"].inc(
                s.activation(wf[:], wstage[:], AF.Copy, bias=-1.0, scale=1.0 / 7.5))
        ctx.on("scalar", _deq_s)
        sn = ctx.sem["scalar"].n

        def _deq_v(v, c=c, sn=sn, dn=dn):
            v.wait_ge(ctx.sem["scalar"].h, sn)
            v.wait_ge(ctx.dma.h, dn)
            ctx.sem["vector"].inc(
                v.tensor_tensor(W[:, c * D:(c + 1) * D], wf[:], wsc[:], ALU.mult))
        ctx.on("vector", _deq_v)
    ctx.barrier()

    for t in range(NFT):
        def _r(te, t=t):
            for k in range(NDC):
                inst = te.matmul(pm[0][0:32, :], la[:, k * 32:(k + 1) * 32],
                                 src[:, k * TPC + t * 512: k * TPC + t * 512 + 512],
                                 start=(k == 0), stop=(k == NDC - 1))
            ctx.sem["tensor"].inc(inst)
        ctx.on("tensor", _r)
        tn = ctx.sem["tensor"].n

        def _rev(v, t=t, tn=tn):
            v.wait_ge(ctx.sem["tensor"].h, tn)
            ctx.sem["vector"].inc(v.tensor_copy(rb[:, t * 512:(t + 1) * 512], pm[0][0:32, :]))
        ctx.on("vector", _rev)
        vn = ctx.sem["vector"].n
        ctx.on("tensor", lambda te, vn=vn: te.wait_ge(ctx.sem["vector"].h, vn))

    bank_guard = [None, None]
    for t in range(NFT):
        for oc in range(NDC):
            bank = (t * NDC + oc) % 2

            def _mm(te, t=t, oc=oc, bank=bank, guard=bank_guard[bank]):
                if guard is not None:
                    te.wait_ge(guard[0].h, guard[1])
                for k in range(NDC):
                    te.matmul(pm[bank][:], W[:, k * D + oc * 128: k * D + oc * 128 + 128],
                              src[:, k * TPC + t * 512: k * TPC + t * 512 + 512],
                              start=(k == 0), stop=False)
                inst = te.matmul(pm[bank][:], lbt[:, oc * 128:(oc + 1) * 128],
                                 rb[:, t * 512:(t + 1) * 512], start=False, stop=True)
                ctx.sem["tensor"].inc(inst)
            ctx.on("tensor", _mm)
            tn = ctx.sem["tensor"].n

            if evac == "bias_only":
                def _ev(v, t=t, oc=oc, bank=bank, tn=tn):
                    v.wait_ge(ctx.sem["tensor"].h, tn)
                    ctx.sem["vector"].inc(v.tensor_scalar(
                        dst[:, oc * TPC + t * 512: oc * TPC + t * 512 + 512],
                        pm[bank][:], bcol[:, oc:oc + 1], None, ALU.add))
                ctx.on("vector", _ev)
                bank_guard[bank] = (ctx.sem["vector"], ctx.sem["vector"].n)
            else:
                def _ev(v, t=t, oc=oc, bank=bank, tn=tn):
                    v.wait_ge(ctx.sem["tensor"].h, tn)
                    reg = hf[:, oc * TPC + t * 512: oc * TPC + t * 512 + 512]
                    ctx.sem["vector"].inc(v.scalar_tensor_tensor(
                        reg, pm[bank][:], bcol[:, oc:oc + 1], reg, ALU.add, ALU.add))
                ctx.on("vector", _ev)
                vn = ctx.sem["vector"].n
                bank_guard[bank] = (ctx.sem["vector"], vn)

                def _cast(s, t=t, oc=oc, vn=vn):
                    s.wait_ge(ctx.sem["vector"].h, vn)
                    sl = slice(oc * TPC + t * 512, oc * TPC + t * 512 + 512)
                    ctx.sem["scalar"].inc(s.activation(hb[:, sl], hf[:, sl], AF.Copy))
                ctx.on("scalar", _cast)


def emit_fm_ln_gelu(ctx, blk, yb, gam, bet, gcol, bbcol, rws, sq, tmp, prow, pb,
                    eps=1e-5):
    def _ldgb(sy, blk=blk):
        ctx.dma.inc(sy.dma_start(out=gcol[:], in_=gam[blk]), 16)
        ctx.dma.inc(sy.dma_start(out=bbcol[:], in_=bet[blk]), 16)
    ctx.on("sync", _ldgb)
    gbn = ctx.dma.n

    for t in range(NFT):
        for k in range(NDC):
            def _sqk(s, t=t, k=k, prev_tn=ctx.sem["tensor"].n):
                s.wait_ge(ctx.sem["tensor"].h, prev_tn)
                ctx.sem["scalar"].inc(s.activation(
                    sq[:], yb[:, k * TPC + t * 512: k * TPC + t * 512 + 512], AF.Square))
            ctx.on("scalar", _sqk)
            sn = ctx.sem["scalar"].n

            def _mmk(te, t=t, k=k, sn=sn):
                te.wait_ge(ctx.sem["scalar"].h, sn)
                te.matmul(prow[0][:], ctx.onesk[:],
                          yb[:, k * TPC + t * 512: k * TPC + t * 512 + 512],
                          start=(k == 0), stop=(k == NDC - 1), skip_group_check=True)
                inst = te.matmul(prow[1][:], ctx.onesk[:], sq[:],
                                 start=(k == 0), stop=(k == NDC - 1), skip_group_check=True)
                ctx.sem["tensor"].inc(inst)
            ctx.on("tensor", _mmk)
        tn = ctx.sem["tensor"].n

        def _rows_v(v, tn=tn, gbn=gbn):
            v.wait_ge(ctx.sem["tensor"].h, tn)
            v.wait_ge(ctx.dma.h, gbn)
            s0 = rws[:, 0:512]
            mrow = rws[:, 2048:2560]
            vrow = rws[:, 2560:3072]
            vchain(ctx, v, v.tensor_scalar(mrow, prow[0][:], ctx.c(1.0 / D)[0:1, :], None, ALU.mult))
            vchain(ctx, v, v.tensor_scalar(vrow, prow[1][:], ctx.c(1.0 / D)[0:1, :], None, ALU.mult))
            vchain(ctx, v, v.tensor_tensor(s0, mrow, mrow, ALU.mult))
            vchain(ctx, v, v.tensor_tensor(vrow, vrow, s0, ALU.subtract))
            ctx.sem["vector"].inc(v.tensor_scalar(
                rws[:, 512:1024], vrow, ctx.c(eps)[0:1, :], None, ALU.add))
        ctx.on("vector", _rows_v)
        vn = ctx.sem["vector"].n

        def _rows_s(s, vn=vn):
            s.wait_ge(ctx.sem["vector"].h, vn)
            ctx.sem["scalar"].inc(s.activation(
                rws[:, 0:512], rws[:, 512:1024], AF.Sqrt))
        ctx.on("scalar", _rows_s)
        sn = ctx.sem["scalar"].n

        def _rows_v2(v, sn=sn):
            v.wait_ge(ctx.sem["scalar"].h, sn)
            r1 = rws[:, 1024:1536]
            r2 = rws[:, 1536:2048]
            t = rws[:, 0:512]
            vchain(ctx, v, v.reciprocal(r1, t))
            vchain(ctx, v, v.tensor_tensor(t, r1, r1, ALU.mult))
            vchain(ctx, v, v.tensor_tensor(t, t, rws[:, 512:1024], ALU.mult))
            vchain(ctx, v, v.tensor_scalar(t, t, ctx.c(-0.5)[0:1, :], ctx.c(1.5)[0:1, :],
                            ALU.mult, ALU.add))
            vchain(ctx, v, v.tensor_tensor(r1, r1, t, ALU.mult))
            ctx.sem["vector"].inc(v.tensor_tensor(r2, rws[:, 2048:2560], r1, ALU.mult))
        ctx.on("vector", _rows_v2)
        vn = ctx.sem["vector"].n

        def _bc(te, vn=vn):
            te.wait_ge(ctx.sem["vector"].h, vn)
            te.matmul(pb[:, 0:512], ctx.onesc[:], rws[:, 1024:1536], start=True, stop=True)
            inst = te.matmul(pb[:, 512:1024], ctx.onesc[:], rws[:, 1536:2048],
                             start=True, stop=True)
            ctx.sem["tensor"].inc(inst)
        ctx.on("tensor", _bc)
        tn = ctx.sem["tensor"].n

        for k in range(NDC):
            def _ap_v(v, t=t, k=k, tn=tn):
                v.wait_ge(ctx.sem["tensor"].h, tn)
                sl = slice(k * TPC + t * 512, k * TPC + t * 512 + 512)
                v.tensor_tensor(tmp[:], yb[:, sl], pb[:, 0:512], ALU.mult)
                v.tensor_tensor(tmp[:], tmp[:], pb[:, 512:1024], ALU.subtract)
                ctx.sem["vector"].inc(v.tensor_scalar(
                    tmp[:], tmp[:], gcol[:, k:k + 1], bbcol[:, k:k + 1], ALU.mult, ALU.add))
            ctx.on("vector", _ap_v)
            vn = ctx.sem["vector"].n

            def _ap_s(s, t=t, k=k, vn=vn):
                s.wait_ge(ctx.sem["vector"].h, vn)
                sl = slice(k * TPC + t * 512, k * TPC + t * 512 + 512)
                ctx.sem["scalar"].inc(s.activation(yb[:, sl], tmp[:], AF.Gelu))
            ctx.on("scalar", _ap_s)
            sn = ctx.sem["scalar"].n
            ctx.on("vector", lambda v, sn=sn: v.wait_ge(ctx.sem["scalar"].h, sn))


def build_l0f(stop_after=None, dbg_tags=()):
    nc = bass.Bass("TRN2", target_bir_lowering=False, debug=False)
    xin = nc.dram_tensor("xin", [TPC, D], F32, kind="ExternalInput").ap()
    qwT = nc.dram_tensor("qwT", [6, D, D], I32, kind="ExternalInput").ap()
    screpT = nc.dram_tensor("screpT", [6, D, D], F32, kind="ExternalInput").ap()
    bias6 = nc.dram_tensor("bias6", [6, 128, 8], F32, kind="ExternalInput").ap()
    laT = nc.dram_tensor("laT", [6, 128, 256], BF16, kind="ExternalInput").ap()
    lbT = nc.dram_tensor("lbT", [6, 32, D], BF16, kind="ExternalInput").ap()
    gam = nc.dram_tensor("gam", [3, 128, 8], F32, kind="ExternalInput").ap()
    bet = nc.dram_tensor("bet", [3, 128, 8], F32, kind="ExternalInput").ap()
    xout = nc.dram_tensor("xout", [TPC, D], F32, kind="ExternalOutput").ap()
    shapes = {"p1_hb": [128, NDC * TPC], "p1_hf": [128, NDC * TPC], "q0_yb": [128, NDC * TPC],
              "q0_W": [128, NDC * D], "q0_rb": [32, TPC],
              "ln0_yb": [128, NDC * TPC], "r0_hf": [128, NDC * TPC]}
    dtypes = {"p1_hb": BF16, "p1_hf": F32, "q0_yb": BF16, "q0_W": BF16, "q0_rb": BF16,
              "ln0_yb": BF16, "r0_hf": F32}
    dbg = {t: nc.dram_tensor(t, shapes[t], dtypes[t], kind="ExternalOutput").ap()
           for t in dbg_tags}
    with ExitStack() as es:
        ctx = Ctx(nc, es)
        emit_l0f(ctx, xin, xout, qwT, screpT, bias6, laT, lbT, gam, bet,
                 stop_after=stop_after, dbg=dbg)
        ctx.on("sync", lambda sy: sy.wait_ge(ctx.dmo.h, ctx.dmo.n))
    return nc


def run_l0f(x1_full, g, stop_after=None, dbg_tags=()):
    nc = build_l0f(stop_after, dbg_tags)
    in_maps = []
    for core in range(NCORES):
        b, j = core // 2, core % 2
        in_maps.append({
            "xin": np.ascontiguousarray(x1_full[b, j * TPC:(j + 1) * TPC]),
            "qwT": g["l0f_qwT"], "screpT": g["l0f_screpT"], "bias6": g["l0f_bias"],
            "laT": g["l0f_laT"], "lbT": g["l0f_lbT"],
            "gam": g["l0f_gam"], "bet": g["l0f_bet"],
        })
    res = run_bass_kernel_spmd(nc, in_maps, core_ids=list(range(NCORES)))
    out = np.empty((B, L, D), np.float32)
    for core in range(NCORES):
        b, j = core // 2, core % 2
        out[b, j * TPC:(j + 1) * TPC] = res.results[core]["xout"]
    if dbg_tags:
        return out, res
    return out






def emit_l1f(ctx, xin, xout, EX, gateTf, lbrow, cc_in, cc_out):
    ctx.const_pool([1.0 / D, 1e-5, 0.0, -0.5, 1.5, -1e9, 1e-8, 1.0])
    ctx.emit_init()

    xt = ctx.sbuf("xt", [128, 2 * D], F32)
    lnst = ctx.sbuf("lnst", [128, D], F32)
    rows = ctx.sbuf("rows", [128, 8], F32)
    sqf = ctx.sbuf("sqf", [128, D], F32)
    hb = ctx.sbuf("hb", [128, NDC * TPC], BF16)
    acc = ctx.sbuf("acc", [128, NT * D], F32)
    stageF = ctx.sbuf("stageF", [128, D], F32)
    gtw = ctx.sbuf("gtw", [128, 64], F32)
    lgf = ctx.sbuf("lgf", [8, TPC], F32)
    lbr = ctx.sbuf("lbr", [8, 1], F32)
    sct = ctx.sbuf("sct", [128, NT * 8], F32)
    scl = ctx.sbuf("scl", [128, NT * 8], F32)
    msk = ctx.sbuf("msk", [128, NT * 8], F32)
    mwt = ctx.sbuf("mwt", [128, NT * 8], F32)
    srow = ctx.sbuf("srow", [128, 16], F32)
    wrow = ctx.sbuf("wrow", [1, 16], F32)
    Wx = [ctx.sbuf(f"Wx{i}", [128, 3 * 4096], BF16) for i in range(2)]
    ig = ctx.sbuf("ig", [128, 4 * TPC], BF16)
    t1 = ctx.sbuf("t1", [128, 512], F32)
    scrt2 = ctx.sbuf("scrt2", [128, 512], F32)

    pt2 = ctx.psum("pt2", [128, 1024])
    pg = ctx.psum("pg", [128, 512])
    pu = ctx.psum("pu", [128, 512])
    pd = [ctx.psum(f"pd{i}", [128, 512]) for i in range(2)]
    psm = ctx.psum("psm", [128, 512])

    def _ldr(sy):
        ctx.dma.inc(sy.dma_start(out=gtw[:], in_=gateTf), 16)
        ctx.dma.inc(sy.dma_start(out=lbr[:], in_=lbrow), 16)
    ctx.on("sync", _ldr)
    rn = ctx.dma.n

    def hook(i):
        vn = ctx.sem["vector"].n

        def _gmm(te, vn=vn):
            te.wait_ge(ctx.sem["vector"].h, vn)
            te.wait_ge(ctx.dma.h, rn)
            for c in range(NDC):
                inst = te.matmul(psm[0:8, 0:128], gtw[:, c * 8:(c + 1) * 8],
                                 stageF[:, c * 128:(c + 1) * 128],
                                 start=(c == 0), stop=(c == NDC - 1))
            ctx.sem["tensor"].inc(inst)
        ctx.on("tensor", _gmm)
        tn = ctx.sem["tensor"].n

        def _gev(v, i=i, tn=tn):
            v.wait_ge(ctx.sem["tensor"].h, tn)
            ctx.sem["vector"].inc(
                v.tensor_copy(lgf[:, i * 128:(i + 1) * 128], psm[0:8, 0:128]))
        ctx.on("vector", _gev)
        vn2 = ctx.sem["vector"].n
        ctx.on("tensor", lambda te, vn2=vn2: te.wait_ge(ctx.sem["vector"].h, vn2))

    emit_tm_ln_to_fm(ctx, xin, xt, lnst, rows, sqf, None, hb, pt2,
                     hb_only=True, stageF=stageF, hook=hook)
    ctx.barrier()

    def _lb(v):
        ctx.sem["vector"].inc(v.tensor_scalar(lgf[:], lgf[:], lbr[:], None, ALU.add))
    ctx.on("vector", _lb)
    ctx.barrier()
    for i in range(NT):
        def _trl(te, i=i):
            te.wait_ge(ctx.sem["scalar"].h, ctx.sem["scalar"].n)
            ctx.sem["tensor"].inc(te.transpose(
                pt2[:, 0:8], lgf[:, i * 128:(i + 1) * 128], ctx.ident[0:8, 0:8]))
        ctx.on("tensor", _trl)
        tn = ctx.sem["tensor"].n

        def _sig(s, i=i, tn=tn):
            s.wait_ge(ctx.sem["tensor"].h, tn)
            ctx.sem["scalar"].inc(s.activation(
                sct[:, i * 8:(i + 1) * 8], pt2[:, 0:8], AF.Sigmoid))
        ctx.on("scalar", _sig)
    ctx.barrier()

    for i in range(NT):
        def _mask(v, i=i):
            sl = slice(i * 8, (i + 1) * 8)
            sc = sct[:, sl]
            vchain(ctx, v, v.reduce_max(srow[:, 0:1], sc, axis=AX))
            vchain(ctx, v, v.tensor_scalar(msk[:, sl], sc, srow[:, 0:1], None, ALU.is_equal))
            vchain(ctx, v, v.scalar_tensor_tensor(mwt[:, sl], msk[:, sl], ctx.c(-1e9), sc,
                                                  ALU.mult, ALU.add))
            vchain(ctx, v, v.reduce_max(srow[:, 1:2], mwt[:, sl], axis=AX))
            vchain(ctx, v, v.tensor_scalar(mwt[:, sl], mwt[:, sl], srow[:, 1:2], None, ALU.is_equal))
            vchain(ctx, v, v.tensor_tensor(srow[:, 2:3], srow[:, 0:1], srow[:, 1:2], ALU.add))
            vchain(ctx, v, v.tensor_scalar(srow[:, 2:3], srow[:, 2:3], ctx.c(1e-8), None, ALU.add))
            vchain(ctx, v, v.reciprocal(srow[:, 3:4], srow[:, 2:3]))
            vchain(ctx, v, v.tensor_tensor(srow[:, 4:5], srow[:, 0:1], srow[:, 3:4], ALU.mult))
            vchain(ctx, v, v.tensor_tensor(srow[:, 5:6], srow[:, 1:2], srow[:, 3:4], ALU.mult))
            vchain(ctx, v, v.tensor_scalar(srow[:, 6:7], srow[:, 4:5], ctx.c(0.0), None, ALU.max))
            vchain(ctx, v, v.tensor_scalar(mwt[:, sl], mwt[:, sl], srow[:, 5:6], None, ALU.mult))
            vchain(ctx, v, v.scalar_tensor_tensor(mwt[:, sl], msk[:, sl], srow[:, 4:5],
                                                  mwt[:, sl], ALU.mult, ALU.add))
            ctx.sem["vector"].inc(v.tensor_scalar(
                msk[:, sl], mwt[:, sl], ctx.c(0.0), None, ALU.is_gt))
        ctx.on("vector", _mask)
    ctx.barrier()

    for i in range(NT):
        def _sums(te, i=i, vn=ctx.sem["vector"].n):
            if i == 0:
                te.wait_ge(ctx.sem["vector"].h, vn)
            te.matmul(psm[0:1, 0:8], ctx.oneskf[:], msk[:, i * 8:(i + 1) * 8],
                      start=(i == 0), stop=(i == NT - 1), skip_group_check=True)
            inst = te.matmul(psm[0:1, 8:16], ctx.oneskf[:], mwt[:, i * 8:(i + 1) * 8],
                             start=(i == 0), stop=(i == NT - 1), skip_group_check=True)
            if i == NT - 1:
                ctx.sem["tensor"].inc(inst)
        ctx.on("tensor", _sums)
    tn = ctx.sem["tensor"].n

    def _wev(v, tn=tn):
        v.wait_ge(ctx.sem["tensor"].h, tn)
        ctx.sem["vector"].inc(v.tensor_copy(wrow[:], psm[0:1, 0:16]))
    ctx.on("vector", _wev)
    vn = ctx.sem["vector"].n

    def _st_cc(sy, vn=vn):
        sy.wait_ge(ctx.sem["vector"].h, vn)
        ctx.dma.inc(sy.dma_start(out=cc_in, in_=wrow[:]), 16)
    ctx.on("sync", _st_cc)
    dn = ctx.dma.n

    def _cc(g, dn=dn):
        g.wait_ge(ctx.dma.h, dn)
        ctx.cc.inc(g.collective_compute("AllReduce", ALU.add,
                                        [[0, 1, 2, 3, 4, 5, 6, 7]],
                                        [cc_in.opt()], [cc_out.opt()]))
    ctx.on("gpsimd", _cc)
    ccn = ctx.cc.n

    def _ld_cc(sy, ccn=ccn):
        sy.wait_ge(ctx.cc.h, ccn)
        ctx.dma.inc(sy.dma_start(out=wrow[:], in_=cc_out), 16)
    ctx.on("sync", _ld_cc)
    dn = ctx.dma.n

    def _wm(v, dn=dn):
        v.wait_ge(ctx.dma.h, dn)
        vchain(ctx, v, v.tensor_scalar(wrow[:, 0:8], wrow[:, 0:8],
                                       ctx.c(1.0)[0:1, :], None, ALU.max))
        vchain(ctx, v, v.reciprocal(wrow[:, 0:8], wrow[:, 0:8]))
        ctx.sem["vector"].inc(
            v.tensor_tensor(wrow[:, 0:8], wrow[:, 8:16], wrow[:, 0:8], ALU.mult))
    ctx.on("vector", _wm)
    vn = ctx.sem["vector"].n

    def _bcw(te, vn=vn):
        te.wait_ge(ctx.sem["vector"].h, vn)
        ctx.sem["tensor"].inc(te.matmul(psm[:, 0:8], ctx.onesc[:], wrow[:, 0:8],
                                        start=True, stop=True))
    ctx.on("tensor", _bcw)
    tn = ctx.sem["tensor"].n

    for i in range(NT):
        def _scl(v, i=i, tn=tn):
            v.wait_ge(ctx.sem["tensor"].h, tn)
            sl = slice(i * 8, (i + 1) * 8)
            ctx.sem["vector"].inc(
                v.tensor_tensor(scl[:, sl], msk[:, sl], psm[:, 0:8], ALU.mult))
        ctx.on("vector", _scl)

    def _z(g):
        ctx.sem["gpsimd"].inc(g.memset(acc[:], 0.0))
    ctx.on("gpsimd", _z)
    ctx.barrier()

    wbuf_guard = [None, None]
    for e in range(10):
        wbuf = Wx[e % 2]

        def _ldw(sy, e=e, wbuf=wbuf, guard=wbuf_guard[e % 2]):
            if guard is not None:
                sy.wait_ge(guard[0].h, guard[1])
            ctx.dma.inc(sy.dma_start(out=wbuf[:], in_=EX[e]), 16)
        ctx.on("sync", _ldw)
        dn = ctx.dma.n
        Wg, Wu, Wd = wbuf[:, 0:4096], wbuf[:, 4096:8192], wbuf[:, 8192:12288]

        gu_guard = [None, None]
        for t in range(NFT):
            for oc in range(4):
                par = (t * 4 + oc) % 2
                pgx, pux = [(pg, pu), (pd[0], pd[1])][par]
                t1x = [t1, scrt2][par]

                def _gu(te, t=t, oc=oc, dn=dn, Wg=Wg, Wu=Wu, pgx=pgx, pux=pux,
                        bg=gu_guard[par]):
                    te.wait_ge(ctx.dma.h, dn)
                    if bg is not None:
                        te.wait_ge(ctx.sem["vector"].h, bg)
                    for k in range(NDC):
                        te.matmul(pgx[:], Wg[:, k * 512 + oc * 128: k * 512 + oc * 128 + 128],
                                  hb[:, k * TPC + t * 512: k * TPC + t * 512 + 512],
                                  start=(k == 0), stop=(k == NDC - 1), skip_group_check=True)
                    for k in range(NDC):
                        inst = te.matmul(pux[:], Wu[:, k * 512 + oc * 128: k * 512 + oc * 128 + 128],
                                         hb[:, k * TPC + t * 512: k * TPC + t * 512 + 512],
                                         start=(k == 0), stop=(k == NDC - 1), skip_group_check=True)
                    ctx.sem["tensor"].inc(inst)
                ctx.on("tensor", _gu)
                tn = ctx.sem["tensor"].n

                def _silu(s, tn=tn, pgx=pgx, t1x=t1x):
                    s.wait_ge(ctx.sem["tensor"].h, tn)
                    ctx.sem["scalar"].inc(s.activation(t1x[:], pgx[:], AF.Silu))
                ctx.on("scalar", _silu)
                sn = ctx.sem["scalar"].n

                def _ig(v, t=t, oc=oc, sn=sn, tn=tn, pux=pux, t1x=t1x):
                    v.wait_ge(ctx.sem["scalar"].h, sn)
                    v.wait_ge(ctx.sem["tensor"].h, tn)
                    sl = slice(oc * TPC + t * 512, oc * TPC + t * 512 + 512)
                    ctx.sem["vector"].inc(v.tensor_tensor(ig[:, sl], t1x[:], pux[:], ALU.mult))
                ctx.on("vector", _ig)
                gu_guard[par] = ctx.sem["vector"].n

            ig_vn = ctx.sem["vector"].n
            for tcol in range(4):
                for half in range(2):
                    def _dn(te, e=e, t=t, tcol=tcol, half=half, dn=dn, Wd=Wd,
                            ig_vn=ig_vn, vg=ctx.sem["vector"].n):
                        te.wait_ge(ctx.sem["vector"].h, max(ig_vn, vg))
                        for c in range(4):
                            inst = te.matmul(
                                pd[half][:],
                                ig[:, c * TPC + t * 512 + tcol * 128:
                                   c * TPC + t * 512 + tcol * 128 + 128],
                                Wd[:, c * 1024 + half * 512: c * 1024 + half * 512 + 512],
                                start=(c == 0), stop=(c == 3), skip_group_check=True)
                        ctx.sem["tensor"].inc(inst)
                    ctx.on("tensor", _dn)
                    tn = ctx.sem["tensor"].n

                    def _acc(v, e=e, t=t, tcol=tcol, half=half, tn=tn):
                        v.wait_ge(ctx.sem["tensor"].h, tn)
                        ti = t * 4 + tcol
                        reg = acc[:, ti * D + half * 512: ti * D + half * 512 + 512]
                        if e < 8:
                            ctx.sem["vector"].inc(v.scalar_tensor_tensor(
                                reg, pd[half][:], scl[:, ti * 8 + e: ti * 8 + e + 1],
                                reg, ALU.mult, ALU.add))
                        else:
                            ctx.sem["vector"].inc(
                                v.tensor_tensor(reg, reg, pd[half][:], ALU.add))
                    ctx.on("vector", _acc)
            gu_guard[0] = gu_guard[1] = ctx.sem["vector"].n
        wbuf_guard[e % 2] = (ctx.sem["tensor"], ctx.sem["tensor"].n)
    ctx.barrier()

    for i in range(NT):
        buf = i % 2
        xv = xt[:, buf * D:(buf + 1) * D]

        def _load(sy, i=i, xv=xv, guard=ctx.dmo.n):
            sy.wait_ge(ctx.dmo.h, guard)
            ctx.dma.inc(sy.dma_start(out=xv, in_=xin[i * 128:(i + 1) * 128, :]), 16)
        ctx.on("sync", _load)
        dn = ctx.dma.n

        def _add(v, i=i, dn=dn, xv=xv):
            v.wait_ge(ctx.dma.h, dn)
            ctx.sem["vector"].inc(v.tensor_tensor(
                acc[:, i * D:(i + 1) * D], acc[:, i * D:(i + 1) * D], xv, ALU.add))
        ctx.on("vector", _add)
        vn = ctx.sem["vector"].n

        def _store(sy, i=i, vn=vn):
            sy.wait_ge(ctx.sem["vector"].h, vn)
            ctx.dmo.inc(sy.dma_start(out=xout[i * 128:(i + 1) * 128, :],
                                     in_=acc[:, i * D:(i + 1) * D]), 16)
        ctx.on("sync", _store)


def build_l1f():
    nc = bass.Bass("TRN2", target_bir_lowering=False, debug=False)
    xin = nc.dram_tensor("xin", [TPC, D], F32, kind="ExternalInput").ap()
    EX = nc.dram_tensor("EX", [10, 128, 3 * 4096], BF16, kind="ExternalInput").ap()
    gateTf = nc.dram_tensor("gateTf", [128, 64], F32, kind="ExternalInput").ap()
    lbrow = nc.dram_tensor("lbrow", [8, 1], F32, kind="ExternalInput").ap()
    xout = nc.dram_tensor("xout", [TPC, D], F32, kind="ExternalOutput").ap()
    cc_in = nc.dram_tensor("cc_in", [1, 16], F32, kind="Internal").ap()
    cc_out = nc.dram_tensor("cc_out", [1, 16], F32, kind="Internal").ap()
    with ExitStack() as es:
        ctx = Ctx(nc, es)
        emit_l1f(ctx, xin, xout, EX, gateTf, lbrow, cc_in, cc_out)
        ctx.on("sync", lambda sy: sy.wait_ge(ctx.dmo.h, ctx.dmo.n))
    return nc


def run_l1f(x3_full, g):
    nc = build_l1f()
    in_maps = []
    for core in range(NCORES):
        b, j = core // 2, core % 2
        in_maps.append({
            "xin": np.ascontiguousarray(x3_full[b, j * TPC:(j + 1) * TPC]),
            "EX": g["m_EX"], "gateTf": g["m_gateTf"], "lbrow": g["m_lb"],
        })
    res = run_bass_kernel_spmd(nc, in_maps, core_ids=list(range(NCORES)))
    out = np.empty((B, L, D), np.float32)
    for core in range(NCORES):
        b, j = core // 2, core % 2
        out[b, j * TPC:(j + 1) * TPC] = res.results[core]["xout"]
    return out





LB = 4096
NTT = 32
NFT8 = 8
VST = 4160
PST = 4100


def emit_l0a(ctx, xin, xhalf, xout, WqL, WkL, WvL, WbL, dconv, dfirS, dfirL,
             WoL, onwc, w1hsL, w1st, w2L, b1c, b2r, ltr, rs_in, rs_out,
             stop_after=None, dbg=None):
    SW = ctx.sem["vector"]; SS = ctx.sem["scalar"]; ST = ctx.sem["tensor"]
    ctx.const_pool([1.0 / D, 1.0 / 256.0, 1e-5, 1e-6, 0.0, -0.5, 1.5, 1.0,
                    0.05, 0.8, -1.0])
    ctx.emit_init()

    scr = ctx.sbuf("scr", [128, D], F32)
    lnst = scr[:]
    rows = ctx.sbuf("rows", [128, 12], F32)
    rowstats = ctx.sbuf("rowstats", [128, 2 * NTT], F32)
    hs = ctx.sbuf("hs", [128, NDC * LB], BF16)
    A1 = ctx.sbuf("A1", [128, 4 * PST], BF16)
    A2 = ctx.sbuf("A2", [128, 4 * LB], BF16)
    A3 = ctx.sbuf("A3", [128, 4 * LB], BF16)
    A4 = ctx.sbuf("A4", [128, 4 * VST], BF16)
    delta = A1[:]
    beta_tm = ctx.sbuf("beta_tm", [128, 2 * NTT], F32)
    dbuf = [ctx.sbuf(f"dbuf{i}", [128, 4 * 128], BF16) for i in range(2)]
    Wbb = ctx.sbuf("Wbb", [128, 16], BF16)
    rkq = ctx.sbuf("rkq", [128, 8], F32)
    w2sb = ctx.sbuf("w2sb", [128, 32], BF16)
    b1sb = ctx.sbuf("b1sb", [128, 8], F32)
    b2sb = ctx.sbuf("b2sb", [4, 1], F32)
    ltsb = ctx.sbuf("ltsb", [1, 4], F32)
    rtau = ctx.sbuf("rtau", [128, 2], F32)
    onws = ctx.sbuf("onws", [128, 4], F32)
    bsb = scr[0:2, 0:512]
    Ssb = ctx.sbuf("Ssb", [128, 1024], F32)
    Sb16 = ctx.sbuf("Sb16", [128, 1024], BF16)
    A2F = A2[:].bitcast(F32)
    A1F = A1[:].bitcast(F32)
    xt = A2F[:, 0:2048]
    xtg = A1F[:, 0:2048]
    Wp = A4[:, 64:4160]
    hsF = hs[:].bitcast(F32)
    w1hsb = hs[:, 0:8192]
    WoS = hs[:, 8192:12288]
    hsf = hs[:, 12288:16384]
    s_sh = hsF[:, 8192:8704]
    lgsb = hsF[0:4, 8704:9728]
    rrow = hsF[0:1, 8704:8960]
    pstg = hsF[:, 9728:10240]
    h1t = hs[:, 20480:20992]
    statsf = hs[0:16, 21504:22528]
    sqf2 = hs[:, 22528:23552]
    otm = hsF[:, 11776:12288]
    xtf = hsF[:, 12288:13312]
    sqs = hs[:, 26624:26880]
    stt = hsF[:, 13440:13456]
    SLm = hs[:, 26912:27040]
    SUm = hs[:, 27040:27168]
    IUm = hs[:, 27168:27296]
    Atile = hs[:, 27296:27424]
    Ptile = hs[:, 27424:27552]
    Qtile = hs[:, 27552:27680]
    Ctile = hs[:, 27680:27808]
    RKm = hs[:, 27808:27936]
    RKm2 = hs[:, 27936:28064]
    altl = hs[:, 28064:28192]
    khtm = hs[:, 28192:28448]
    qhtm = hs[:, 28448:28704]
    kbtm = hs[:, 28704:28960]
    vtm = hs[:, 28960:29216]
    usb = hs[:, 29216:29472]
    upsb = hs[:, 29472:29728]
    wsb = hs[:, 29728:29984]
    ofm = hs[:, 29984:30496]
    w1sb = hs[0:16, 30496:31520]
    probs = hsF[:, 15760:16016]

    pt2 = ctx.psum("pt2", [128, 1024])
    ptb = pt2[:].bitcast(BF16)
    pA = ctx.psum("pA", [128, 512])
    pB = ctx.psum("pB", [128, 512])
    pC = ctx.psum("pC", [128, 512])
    pD = ctx.psum("pD", [128, 512])
    pl = [ctx.psum(f"pl{i}", [4, 512]) for i in range(2)]
    banks4 = [pA, pB, pC, pD]

    def _msk(g):
        g.memset(Ssb[:], 0.0)
        g.memset(Sb16[:], 0.0)
        for c in range(4):
            g.memset(A4[:, c * VST: c * VST + 64], 0.0)
            g.memset(A1[:, c * PST: c * PST + 4], 0.0)
        ctx.sem["gpsimd"].inc(g.tensor_copy(Sb16[:, 0:16], Sb16[:, 0:16]))
    ctx.on("gpsimd", _msk)

    def _ldp(sy):
        for dst, srcap in ((w1sb, w1st), (w2sb, w2L), (b1sb, b1c), (b2sb, b2r),
                           (ltsb, ltr), (onws, onwc), (WoS, WoL), (w1hsb, w1hsL)):
            ctx.dma.inc(sy.dma_start(out=dst[:], in_=srcap), 16)
    ctx.on("sync", _ldp)
    ctx.barrier()

    def _tau_s(s):
        SS.inc(s.activation(ltsb[:, 2:4], ltsb[:, 0:2], AF.Exp))
    ctx.on("scalar", _tau_s)
    sn = SS.n

    def _tau_v(v, sn=sn):
        v.wait_ge(SS.h, sn)
        SW.inc(v.reciprocal(ltsb[:, 0:2], ltsb[:, 2:4]))
    ctx.on("vector", _tau_v)
    vn = SW.n

    def _tau_bc(te, vn=vn):
        te.wait_ge(SW.h, vn)
        te.matmul(pt2[:, 0:1], ctx.onesc[:], ltsb[:, 0:1], start=True, stop=True)
        inst = te.matmul(pt2[:, 1:2], ctx.onesc[:], ltsb[:, 1:2], start=True, stop=True)
        ST.inc(inst)
    ctx.on("tensor", _tau_bc)
    tn = ST.n

    def _tau_ev(v, tn=tn):
        v.wait_ge(ST.h, tn)
        SW.inc(v.tensor_copy(rtau[:], pt2[:, 0:2]))
    ctx.on("vector", _tau_ev)
    vn = SW.n

    ctx.barrier()

    for i in range(NTT):
        buf = i % 2
        xv = xt[:, buf * D:(buf + 1) * D]

        def _load(sy, i=i, xv=xv, guard=SW.n):
            sy.wait_ge(SW.h, guard)
            ctx.dma.inc(sy.dma_start(out=xv, in_=xin[i * 128:(i + 1) * 128, :]), 16)
        ctx.on("sync", _load)
        dn = ctx.dma.n

        def _sq(s, xv=xv, dn=dn, vg=SW.n):
            s.wait_ge(ctx.dma.h, dn)
            s.wait_ge(SW.h, vg)
            SS.inc(s.activation(sqf2, xv, AF.Square))
        ctx.on("scalar", _sq)
        sn = SS.n

        def _rm(v, i=i, xv=xv, dn=dn, sn=sn):
            v.wait_ge(ctx.dma.h, dn)
            vchain(ctx, v, v.reduce_sum(rows[:, 0:1], xv, axis=AX))
            v.wait_ge(SS.h, sn)
            vchain(ctx, v, v.reduce_sum(rows[:, 1:2], sqf2, axis=AX))
            vchain(ctx, v, v.tensor_scalar(rows[:, 2:3], rows[:, 0:1], ctx.c(1.0 / D), None, ALU.mult))
            vchain(ctx, v, v.tensor_scalar(rows[:, 3:4], rows[:, 1:2], ctx.c(1.0 / D), None, ALU.mult))
            vchain(ctx, v, v.tensor_tensor(rows[:, 4:5], rows[:, 2:3], rows[:, 2:3], ALU.mult))
            vchain(ctx, v, v.tensor_tensor(rows[:, 3:4], rows[:, 3:4], rows[:, 4:5], ALU.subtract))
            SW.inc(v.tensor_scalar(rows[:, 7:8], rows[:, 3:4], ctx.c(1e-5), None, ALU.add))
        ctx.on("vector", _rm)
        vn = SW.n

        def _sr(s, vn=vn):
            s.wait_ge(SW.h, vn)
            SS.inc(s.activation(rows[:, 5:6], rows[:, 7:8], AF.Sqrt))
        ctx.on("scalar", _sr)
        sn = SS.n

        def _nw(v, i=i, sn=sn):
            v.wait_ge(SS.h, sn)
            vchain(ctx, v, v.reciprocal(rows[:, 6:7], rows[:, 5:6]))
            vchain(ctx, v, v.tensor_tensor(rows[:, 4:5], rows[:, 6:7], rows[:, 6:7], ALU.mult))
            vchain(ctx, v, v.tensor_tensor(rows[:, 4:5], rows[:, 4:5], rows[:, 7:8], ALU.mult))
            vchain(ctx, v, v.tensor_scalar(rows[:, 4:5], rows[:, 4:5], ctx.c(-0.5), ctx.c(1.5), ALU.mult, ALU.add))
            vchain(ctx, v, v.tensor_tensor(rows[:, 6:7], rows[:, 6:7], rows[:, 4:5], ALU.mult))
            v.tensor_copy(rowstats[:, 2 * i:2 * i + 1], rows[:, 2:3])
            SW.inc(v.tensor_copy(rowstats[:, 2 * i + 1:2 * i + 2], rows[:, 6:7]))
        ctx.on("vector", _nw)
    ctx.barrier()

    def hs_tile(i, dst, dst_off, dst_stride, xtbuf=None, nbuf=2):
        xtbuf = xt if xtbuf is None else xtbuf
        buf = i % nbuf
        xv = xtbuf[:, buf * D:(buf + 1) * D]

        def _load(sy, i=i, xv=xv, guard=ST.n):
            sy.wait_ge(ST.h, guard)
            ctx.dma.inc(sy.dma_start(out=xv, in_=xin[i * 128:(i + 1) * 128, :]), 16)
        ctx.on("sync", _load)
        dn = ctx.dma.n

        def _ap(v, i=i, xv=xv, dn=dn):
            v.wait_ge(ctx.dma.h, dn)
            SW.inc(v.tensor_scalar(lnst[:], xv, rowstats[:, 2 * i:2 * i + 1],
                                   rowstats[:, 2 * i + 1:2 * i + 2],
                                   ALU.subtract, ALU.mult))
        ctx.on("vector", _ap)
        vn = SW.n

        def _tr(te, vn=vn):
            te.wait_ge(SW.h, vn)
            for c in range(NDC):
                inst = te.transpose(pt2[:, c * 128:(c + 1) * 128],
                                    lnst[:, c * 128:(c + 1) * 128], ctx.ident[:])
            ST.inc(inst)
        ctx.on("tensor", _tr)
        tn = ST.n

        def _ev(v, tn=tn, dst=dst, dst_off=dst_off, dst_stride=dst_stride):
            v.wait_ge(ST.h, tn)
            for c in range(NDC):
                inst = v.tensor_copy(
                    dst[:, dst_off + c * dst_stride: dst_off + c * dst_stride + 128],
                    pt2[:, c * 128:(c + 1) * 128])
            SW.inc(inst)
        ctx.on("vector", _ev)

    for i in range(NTT):
        hs_tile(i, hs, i * 128, LB)
    ctx.barrier()

    def proj_conv(WL, pidx, dst, dst_stride, dst_off):
        def _ldw(sy, WL=WL, guard=ST.n):
            sy.wait_ge(ST.h, guard)
            ctx.dma.inc(sy.dma_start(out=Wp, in_=WL), 16)
        ctx.on("sync", _ldw)
        dnw = ctx.dma.n
        for t in range(NFT8):
            for oc in range(4):
                bank = banks4[(t * 4 + oc) % 4]

                def _mm(te, t=t, oc=oc, bank=bank, dnw=dnw, vg=SW.n):
                    te.wait_ge(SW.h, vg)
                    te.wait_ge(ctx.dma.h, dnw)
                    for k in range(NDC):
                        inst = te.matmul(
                            bank[:], Wp[:, k * 512 + oc * 128: k * 512 + oc * 128 + 128],
                            hs[:, k * LB + t * 512: k * LB + t * 512 + 512],
                            start=(k == 0), stop=(k == NDC - 1), skip_group_check=True)
                    ST.inc(inst)
                ctx.on("tensor", _mm)
                tn = ST.n

                def _ev(v, t=t, oc=oc, bank=bank, tn=tn):
                    v.wait_ge(ST.h, tn)
                    SW.inc(v.tensor_copy(
                        A1[:, oc * PST + 4 + t * 512: oc * PST + 4 + t * 512 + 512],
                        bank[:]))
                ctx.on("vector", _ev)
        ctx.barrier()
        for c in range(4):
            db = dbuf[c % 2]

            def _ldc(sy, pidx=pidx, c=c, db=db, guard=ST.n):
                sy.wait_ge(ST.h, guard)
                ctx.dma.inc(sy.dma_start(
                    out=db[:, 0:512],
                    in_=dconv[pidx, 0:4, c].rearrange("k p f -> p k f")), 16)
            ctx.on("sync", _ldc)
            dnc = ctx.dma.n
            for t in range(NFT8):
                bank = banks4[t % 4]
                for tap in range(4):
                    def _cm(te, c=c, t=t, tap=tap, bank=bank, db=db, dnc=dnc, sg=SS.n):
                        te.wait_ge(ctx.dma.h, dnc)
                        if tap == 0:
                            te.wait_ge(SS.h, sg)
                        ST.inc(te.matmul(
                            bank[:], db[:, tap * 128:(tap + 1) * 128],
                            A1[:, c * PST + 1 + tap + t * 512: c * PST + 1 + tap + t * 512 + 512],
                            start=(tap == 0), stop=(tap == 3), skip_group_check=True))
                    ctx.on("tensor", _cm)
                tn = ST.n

                def _sv(s, c=c, t=t, bank=bank, tn=tn, dst=dst,
                        dst_stride=dst_stride, dst_off=dst_off):
                    s.wait_ge(ST.h, tn)
                    SS.inc(s.activation(
                        dst[:, c * dst_stride + dst_off + t * 512:
                            c * dst_stride + dst_off + t * 512 + 512],
                        bank[:], AF.Silu))
                ctx.on("scalar", _sv)
        ctx.barrier()

    def _dump(tag, ap_src):
        if dbg is not None and tag in dbg:
            ctx.barrier()
            def _st(sy, ap_src=ap_src, apd=dbg[tag]):
                ctx.dmo.inc(sy.dma_start(out=apd, in_=ap_src), 16)
            ctx.on("sync", _st)

    proj_conv(WqL, 0, A2, LB, 0)
    proj_conv(WkL, 1, A3, LB, 0)
    proj_conv(WvL, 2, A4, VST, 64)
    _dump("qh", A2[:])
    _dump("vh", A4[:])

    def _ldwb(sy, guard=ST.n):
        sy.wait_ge(ST.h, guard)
        ctx.dma.inc(sy.dma_start(out=Wbb[:], in_=WbL), 16)
    ctx.on("sync", _ldwb)
    dnb = ctx.dma.n
    for t in range(NFT8):
        def _bm(te, t=t, dnb=dnb, vg=SW.n):
            te.wait_ge(SW.h, vg)
            te.wait_ge(ctx.dma.h, dnb)
            for k in range(NDC):
                inst = te.matmul(pl[0][0:2, :], Wbb[:, k * 2: k * 2 + 2],
                                 hs[:, k * LB + t * 512: k * LB + t * 512 + 512],
                                 start=(k == 0), stop=(k == NDC - 1), skip_group_check=True)
            ST.inc(inst)
        ctx.on("tensor", _bm)
        tn = ST.n

        def _bs(s, tn=tn):
            s.wait_ge(ST.h, tn)
            SS.inc(s.activation(bsb[:], pl[0][0:2, :], AF.Sigmoid))
        ctx.on("scalar", _bs)
        sn = SS.n
        for q4 in range(4):
            def _bt(te, q4=q4, sn=sn, vg=SW.n):
                te.wait_ge(SS.h, sn)
                te.wait_ge(SW.h, vg)
                ST.inc(te.transpose(pt2[:, 0:2], bsb[:, q4 * 128:(q4 + 1) * 128],
                                    ctx.ident[0:2, 0:2]))
            ctx.on("tensor", _bt)
            tn2 = ST.n

            def _bv(v, t=t, q4=q4, tn2=tn2):
                v.wait_ge(ST.h, tn2)
                i = t * 4 + q4
                SW.inc(v.tensor_copy(beta_tm[:, 2 * i:2 * i + 2], pt2[:, 0:2]))
            ctx.on("vector", _bv)
    ctx.barrier()
    _dump("beta", beta_tm[:])
    if stop_after == "pb":
        return

    def _msk2(g):
        g.memset(SLm, 1.0)
        g.affine_select(out=SLm, in_=SLm, compare_op=ALU.is_gt, fill=0.0,
                        base=0, pattern=[[-1, 128]], channel_multiplier=1)
        g.memset(SUm, 0.0)
        g.affine_select(out=SUm, in_=SUm, compare_op=ALU.is_ge, fill=1.0,
                        base=0, pattern=[[-1, 128]], channel_multiplier=1)
        g.memset(IUm, 0.0)
        g.affine_select(out=IUm, in_=IUm, compare_op=ALU.is_gt, fill=1.0,
                        base=0, pattern=[[-1, 128]], channel_multiplier=1)
        ctx.sem["gpsimd"].inc(g.tensor_copy(SLm, SLm))
    ctx.on("gpsimd", _msk2)
    ctx.barrier()

    for i in range(NTT):
        for h in range(2):
            def qsl(cc, i=i, h=h):
                return A2[:, (2 * h + cc) * LB + i * 128: (2 * h + cc) * LB + (i + 1) * 128]

            def ksl(cc, i=i, h=h):
                return A3[:, (2 * h + cc) * LB + i * 128: (2 * h + cc) * LB + (i + 1) * 128]

            def vsl(cc, i=i, h=h):
                return A4[:, (2 * h + cc) * VST + 64 + i * 128: (2 * h + cc) * VST + 64 + (i + 1) * 128]

            def Sh(cc, h=h):
                return Ssb[:, (2 * h + cc) * 256: (2 * h + cc) * 256 + 256]

            def Sb(cc, h=h):
                return Sb16[:, (2 * h + cc) * 256: (2 * h + cc) * 256 + 256]

            def _tr1(te, i=i, h=h, vg=SW.n):
                te.wait_ge(SW.h, vg)
                for cc in range(2):
                    te.transpose(ptb[:, cc * 128:(cc + 1) * 128], ksl(cc), ctx.identb[:])
                for cc in range(2):
                    te.transpose(ptb[:, 256 + cc * 128: 256 + (cc + 1) * 128], qsl(cc), ctx.identb[:])
                for cc in range(2):
                    inst = te.transpose(ptb[:, 512 + cc * 128: 512 + (cc + 1) * 128], vsl(cc), ctx.identb[:])
                ST.inc(inst)
            ctx.on("tensor", _tr1)
            tn = ST.n

            def _ev1(v, i=i, h=h, tn=tn):
                v.wait_ge(ST.h, tn)
                v.tensor_copy(khtm, ptb[:, 0:256])
                v.tensor_copy(qhtm, ptb[:, 256:512])
                SW.inc(v.tensor_scalar(vtm, ptb[:, 512:768],
                                       beta_tm[:, 2 * i + h:2 * i + h + 1], None, ALU.mult))
            ctx.on("vector", _ev1)
            vn = SW.n

            def _sq2(s, vn=vn):
                s.wait_ge(SW.h, vn)
                SS.inc(s.activation(sqs[:], khtm[:], AF.Square))
            ctx.on("scalar", _sq2)
            sn = SS.n

            def _rk(v, i=i, h=h, sn=sn):
                v.wait_ge(SS.h, sn)
                vchain(ctx, v, v.reduce_sum(rows[:, 8:9], sqs[:], axis=AX))
                SW.inc(v.tensor_scalar(rows[:, 8:9], rows[:, 8:9], ctx.c(1e-6), None, ALU.add))
            ctx.on("vector", _rk)
            vn = SW.n

            def _rks(s, vn=vn):
                s.wait_ge(SW.h, vn)
                SS.inc(s.activation(rows[:, 9:10], rows[:, 8:9], AF.Sqrt))
            ctx.on("scalar", _rks)
            sn = SS.n

            def _rkn(v, i=i, h=h, sn=sn):
                v.wait_ge(SS.h, sn)
                vchain(ctx, v, v.reciprocal(rows[:, 10:11], rows[:, 9:10]))
                vchain(ctx, v, v.tensor_tensor(rows[:, 11:12], rows[:, 10:11], rows[:, 10:11], ALU.mult))
                vchain(ctx, v, v.tensor_tensor(rows[:, 11:12], rows[:, 11:12], rows[:, 8:9], ALU.mult))
                vchain(ctx, v, v.tensor_scalar(rows[:, 11:12], rows[:, 11:12], ctx.c(-0.5), ctx.c(1.5), ALU.mult, ALU.add))
                vchain(ctx, v, v.tensor_tensor(rkq[:, 0:1], rows[:, 10:11], rows[:, 11:12], ALU.mult))
                vchain(ctx, v, v.tensor_tensor(rkq[:, 2:3], rkq[:, 0:1], beta_tm[:, 2 * i + h:2 * i + h + 1], ALU.mult))
                vchain(ctx, v, v.tensor_scalar(rkq[:, 3:4], rkq[:, 2:3], ctx.c(-1.0), None, ALU.mult))
                vchain(ctx, v, v.tensor_scalar(kbtm[:], khtm[:], rkq[:, 2:3], None, ALU.mult))
                SW.inc(v.tensor_scalar(khtm[:], khtm[:], rkq[:, 0:1], None, ALU.mult))
            ctx.on("vector", _rkn)
            vn = SW.n

            def _sq3(s, vn=vn):
                s.wait_ge(SW.h, vn)
                SS.inc(s.activation(sqs[:], qhtm[:], AF.Square))
            ctx.on("scalar", _sq3)
            sn = SS.n

            def _rq(v, sn=sn):
                v.wait_ge(SS.h, sn)
                vchain(ctx, v, v.reduce_sum(rows[:, 8:9], sqs[:], axis=AX))
                SW.inc(v.tensor_scalar(rows[:, 8:9], rows[:, 8:9], ctx.c(1e-6), None, ALU.add))
            ctx.on("vector", _rq)
            vn = SW.n

            def _rqs(s, vn=vn):
                s.wait_ge(SW.h, vn)
                SS.inc(s.activation(rows[:, 9:10], rows[:, 8:9], AF.Sqrt))
            ctx.on("scalar", _rqs)
            sn = SS.n

            def _rqn(v, sn=sn):
                v.wait_ge(SS.h, sn)
                vchain(ctx, v, v.reciprocal(rows[:, 10:11], rows[:, 9:10]))
                vchain(ctx, v, v.tensor_tensor(rows[:, 11:12], rows[:, 10:11], rows[:, 10:11], ALU.mult))
                vchain(ctx, v, v.tensor_tensor(rows[:, 11:12], rows[:, 11:12], rows[:, 8:9], ALU.mult))
                vchain(ctx, v, v.tensor_scalar(rows[:, 11:12], rows[:, 11:12], ctx.c(-0.5), ctx.c(1.5), ALU.mult, ALU.add))
                SW.inc(v.tensor_tensor(rkq[:, 1:2], rows[:, 10:11], rows[:, 11:12], ALU.mult))
            ctx.on("vector", _rqn)
            vn = SW.n

            def _trr(te, vn=vn):
                te.wait_ge(SW.h, vn)
                te.transpose(pt2[0:1, 0:128], rkq[:, 0:1], ctx.ident[:])
                ST.inc(te.transpose(pt2[0:1, 128:256], rkq[:, 2:3], ctx.ident[:]))
            ctx.on("tensor", _trr)
            tn = ST.n

            def _evr(v, tn=tn):
                v.wait_ge(ST.h, tn)
                SW.inc(v.tensor_copy(rrow[:], pt2[0:1, 0:256]))
            ctx.on("vector", _evr)
            vn = SW.n

            def _bcr(te, vn=vn):
                te.wait_ge(SW.h, vn)
                te.matmul(pt2[:, 0:128], ctx.onesc[:], rrow[:, 0:128], start=True, stop=True)
                ST.inc(te.matmul(pt2[:, 128:256], ctx.onesc[:], rrow[:, 128:256], start=True, stop=True))
            ctx.on("tensor", _bcr)
            tn = ST.n

            def _mkm(v, tn=tn):
                v.wait_ge(ST.h, tn)
                v.tensor_tensor(RKm[:], SLm[:], pt2[:, 0:128], ALU.mult)
                SW.inc(v.tensor_tensor(RKm2[:], SUm[:], pt2[:, 128:256], ALU.mult))
            ctx.on("vector", _mkm)
            vn = SW.n

            def _gm(te, i=i, h=h, vn=vn):
                te.wait_ge(SW.h, vn)
                for cc in range(2):
                    inst = te.matmul(pA[:, 0:128], ksl(cc), ksl(cc),
                                     start=(cc == 0), stop=(cc == 1), skip_group_check=True)
                ST.inc(inst)
            ctx.on("tensor", _gm)
            tn = ST.n

            def _mka(v, tn=tn):
                v.wait_ge(ST.h, tn)
                vchain(ctx, v, v.tensor_scalar(Atile[:], pA[:, 0:128], rkq[:, 3:4], None, ALU.mult))
                vchain(ctx, v, v.tensor_tensor(Atile[:], Atile[:], RKm[:], ALU.mult))
                vchain(ctx, v, v.tensor_scalar(rows[:, 8:9], rkq[:, 0:1], ctx.c(-1.0), None, ALU.mult))
                vchain(ctx, v, v.tensor_scalar(Qtile[:], pA[:, 0:128], rows[:, 8:9], None, ALU.mult))
                vchain(ctx, v, v.tensor_tensor(Qtile[:], Qtile[:], RKm2[:], ALU.mult))
                vchain(ctx, v, v.tensor_copy(Ptile[:], Atile[:]))
                SW.inc(v.tensor_tensor(Ctile[:], Qtile[:], ctx.identb[:], ALU.add))
            ctx.on("vector", _mka)
            vn = SW.n

            for it in range(6):
                def _dbl1(te, it=it, vg=SW.n):
                    te.wait_ge(SW.h, vg)
                    te.matmul(pB[:, 0:128], Qtile, Ptile, start=True, stop=True, skip_group_check=True)
                    if it < 5:
                        ST.inc(te.matmul(pC[:, 0:128], Ptile, Qtile, start=True, stop=True, skip_group_check=True))
                    else:
                        ST.inc(te.matmul(pC[:, 0:128], Ptile, Qtile, start=True, stop=True, skip_group_check=True))
                ctx.on("tensor", _dbl1)
                tn = ST.n

                def _dev1(v, it=it, tn=tn):
                    v.wait_ge(ST.h, tn)
                    vchain(ctx, v, v.tensor_copy(Ptile, pB[:, 0:128]))
                    SW.inc(v.tensor_copy(Qtile, pC[:, 0:128]))
                ctx.on("vector", _dev1)
                vn = SW.n

                def _dbl2(te, vn=vn):
                    te.wait_ge(SW.h, vn)
                    ST.inc(te.matmul(pD[:, 0:128], Ptile, Ctile, start=True, stop=True, skip_group_check=True))
                ctx.on("tensor", _dbl2)
                tn = ST.n

                def _dev2(v, tn=tn):
                    v.wait_ge(ST.h, tn)
                    SW.inc(v.tensor_tensor(Ctile, Ctile, pD[:, 0:128], ALU.add))
                ctx.on("vector", _dev2)
            vn = SW.n

            def _uw(te, vn=vn):
                te.wait_ge(SW.h, vn)
                te.matmul(pA[:, 0:256], Ctile[:], vtm[:], start=True, stop=True, skip_group_check=True)
                te.matmul(pB[:, 0:128], kbtm[:, 0:128], Ctile[:], start=True, stop=True, skip_group_check=True)
                ST.inc(te.matmul(pB[:, 128:256], kbtm[:, 128:256], Ctile[:], start=True, stop=True, skip_group_check=True))
            ctx.on("tensor", _uw)
            tn = ST.n

            def _uwev(v, tn=tn):
                v.wait_ge(ST.h, tn)
                v.tensor_copy(usb[:], pA[:, 0:256])
                SW.inc(v.tensor_scalar(wsb[:], pB[:, 0:256], ctx.c(-1.0), None, ALU.mult))
            ctx.on("vector", _uwev)
            vn = SW.n

            def _al(te, i=i, h=h, vn=vn):
                te.wait_ge(SW.h, vn)
                for cc in range(2):
                    inst = te.matmul(pC[:, 0:128], ksl(cc), qsl(cc),
                                     start=(cc == 0), stop=(cc == 1), skip_group_check=True)
                ST.inc(inst)
            ctx.on("tensor", _al)
            tn = ST.n

            def _alev(v, tn=tn):
                v.wait_ge(ST.h, tn)
                vchain(ctx, v, v.tensor_scalar(altl[:], pC[:, 0:128], rkq[:, 0:1], None, ALU.mult))
                SW.inc(v.tensor_tensor(altl[:], altl[:], IUm[:], ALU.mult))
            ctx.on("vector", _alev)
            vn = SW.n

            def _up(te, h=h, vn=vn):
                te.wait_ge(SW.h, vn)
                te.matmul(pD[:, 0:256], ctx.identb[:], usb[:], start=True, stop=False, skip_group_check=True)
                te.matmul(pD[:, 0:256], wsb[:, 0:128], Sb(0), start=False, stop=False, skip_group_check=True)
                ST.inc(te.matmul(pD[:, 0:256], wsb[:, 128:256], Sb(1), start=False, stop=True, skip_group_check=True))
            ctx.on("tensor", _up)
            tn = ST.n

            def _upev(v, tn=tn):
                v.wait_ge(ST.h, tn)
                SW.inc(v.tensor_copy(upsb[:], pD[:, 0:256]))
            ctx.on("vector", _upev)
            vn = SW.n

            def _om(te, i=i, h=h, vn=vn):
                te.wait_ge(SW.h, vn)
                te.matmul(pA[:, 0:256], qsl(0), Sb(0), start=True, stop=False, skip_group_check=True)
                te.matmul(pA[:, 0:256], qsl(1), Sb(1), start=False, stop=False, skip_group_check=True)
                ST.inc(te.matmul(pA[:, 0:256], altl[:], upsb[:], start=False, stop=True, skip_group_check=True))
            ctx.on("tensor", _om)
            tn = ST.n

            def _oev(v, i=i, h=h, tn=tn):
                v.wait_ge(ST.h, tn)
                SW.inc(v.tensor_scalar(
                    delta[:, i * 512 + h * 256: i * 512 + h * 256 + 256],
                    pA[:, 0:256], rkq[:, 1:2], None, ALU.mult))
            ctx.on("vector", _oev)
            vn = SW.n

            def _su(te, vn=vn):
                te.wait_ge(SW.h, vn)
                te.matmul(pB[:, 0:256], khtm[:, 0:128], upsb[:], start=True, stop=True, skip_group_check=True)
                ST.inc(te.matmul(pC[:, 0:256], khtm[:, 128:256], upsb[:], start=True, stop=True, skip_group_check=True))
            ctx.on("tensor", _su)
            tn = ST.n

            def _sev(v, h=h, tn=tn):
                v.wait_ge(ST.h, tn)
                vchain(ctx, v, v.tensor_tensor(Sh(0), Sh(0), pB[:, 0:256], ALU.add))
                vchain(ctx, v, v.tensor_tensor(Sh(1), Sh(1), pC[:, 0:256], ALU.add))
                v.tensor_copy(Sb(0), Sh(0))
                SW.inc(v.tensor_copy(Sb(1), Sh(1)))
            ctx.on("vector", _sev)
    ctx.barrier()

    _dump("delta", delta[:, 0:16384])
    if stop_after == "pc":
        return

    def _ldw2p(sy):
        ctx.dma.inc(sy.dma_start(out=WoS, in_=WoL), 16)
        ctx.dma.inc(sy.dma_start(out=w1hsb, in_=w1hsL), 16)
        ctx.dma.inc(sy.dma_start(out=w1sb, in_=w1st), 16)
    ctx.on("sync", _ldw2p)
    dnw = ctx.dma.n

    def _wosc(v, dnw=dnw):
        v.wait_ge(ctx.dma.h, dnw)
        for c in range(4):
            inst = v.tensor_scalar(WoS[:, c * 1024:(c + 1) * 1024],
                                   WoS[:, c * 1024:(c + 1) * 1024],
                                   onws[:, c:c + 1], None, ALU.mult)
        SW.inc(inst)
    ctx.on("vector", _wosc)
    ctx.barrier()

    def fir(dstack, ntaps, dst):
        for c in range(4):
            for grp in range(2):
                nblk = (ntaps + 3) // 4
                for blk in range(nblk):
                    t0 = blk * 4
                    nt = min(4, ntaps - t0)
                    db = dbuf[blk % 2]

                    def _ld(sy, t0=t0, nt=nt, c=c, db=db, dstack=dstack, guard=ST.n):
                        sy.wait_ge(ST.h, guard)
                        ctx.dma.inc(sy.dma_start(
                            out=db[:, 0:nt * 128],
                            in_=dstack[t0:t0 + nt, c].rearrange("k p f -> p k f")), 16)
                    ctx.on("sync", _ld)
                    dn = ctx.dma.n
                    for j in range(nt):
                        tap = t0 + j

                        def _fm(te, c=c, grp=grp, tap=tap, j=j, db=db, dn=dn,
                                ntaps=ntaps, sg=SS.n):
                            te.wait_ge(ctx.dma.h, dn)
                            if tap == 0:
                                te.wait_ge(SS.h, sg)
                            for f4 in range(4):
                                t = grp * 4 + f4
                                inst = te.matmul(
                                    banks4[f4][:], db[:, j * 128:(j + 1) * 128],
                                    A4[:, c * VST + 64 + t * 512 - (ntaps - 1) + tap:
                                       c * VST + 64 + t * 512 - (ntaps - 1) + tap + 512],
                                    start=(tap == 0), stop=(tap == ntaps - 1),
                                    skip_group_check=True)
                            ST.inc(inst)
                        ctx.on("tensor", _fm)
                tn = ST.n

                def _fe(s, c=c, grp=grp, tn=tn, dst=dst):
                    s.wait_ge(ST.h, tn)
                    for f4 in range(4):
                        t = grp * 4 + f4
                        inst = s.activation(
                            dst[:, c * LB + t * 512: c * LB + t * 512 + 512],
                            banks4[f4][:], AF.Copy)
                    SS.inc(inst)
                ctx.on("scalar", _fe)
        ctx.barrier()

    fir(dfirS, 5, A2)
    fir(dfirL, 64, A3)
    _dump("short", A2[:])
    _dump("long", A3[:])
    if stop_after == "pd":
        return

    def stats_tile(i, q4):
        def _trs(te, i=i, vg=SW.n, sg=SS.n):
            te.wait_ge(SW.h, vg)
            te.wait_ge(SS.h, sg)
            for cc in range(4):
                te.transpose(ptb[:, cc * 128:(cc + 1) * 128],
                             A2[:, cc * LB + i * 128: cc * LB + (i + 1) * 128], ctx.identb[:])
            for cc in range(4):
                te.transpose(ptb[:, 512 + cc * 128: 512 + (cc + 1) * 128],
                             A3[:, cc * LB + i * 128: cc * LB + (i + 1) * 128], ctx.identb[:])
            for cc in range(4):
                inst = te.transpose(ptb[:, 1024 + cc * 128: 1024 + (cc + 1) * 128],
                                    A4[:, cc * VST + 64 + i * 128: cc * VST + 64 + (i + 1) * 128],
                                    ctx.identb[:])
            ST.inc(inst)
        ctx.on("tensor", _trs)
        tn = ST.n

        for h in range(2):
            srcs = [lambda h=h, i=i: ptb[:, h * 256: h * 256 + 256],
                    lambda h=h, i=i: ptb[:, 512 + h * 256: 512 + h * 256 + 256],
                    lambda h=h, i=i: delta[:, i * 512 + h * 256: i * 512 + h * 256 + 256],
                    lambda h=h, i=i: ptb[:, 1024 + h * 256: 1024 + h * 256 + 256]]
            for si, srcf in enumerate(srcs):
                def _st_s(s, srcf=srcf, tn=tn, vg=SW.n):
                    s.wait_ge(ST.h, tn)
                    s.wait_ge(SW.h, vg)
                    SS.inc(s.activation(sqs[:], srcf(), AF.Square))
                ctx.on("scalar", _st_s)
                sn = SS.n

                def _st_v(v, si=si, srcf=srcf, sn=sn, tn=tn):
                    v.wait_ge(ST.h, tn)
                    vchain(ctx, v, v.reduce_sum(rows[:, 8:9], srcf(), axis=AX))
                    v.wait_ge(SS.h, sn)
                    vchain(ctx, v, v.reduce_sum(rows[:, 9:10], sqs[:], axis=AX))
                    vchain(ctx, v, v.reduce_sum(rows[:, 10:11], srcf(), axis=AX,
                                                apply_absolute_value=True))
                    vchain(ctx, v, v.tensor_scalar(stt[:, si * 4: si * 4 + 1],
                                                   rows[:, 8:9], ctx.c(1.0 / 256.0), None, ALU.mult))
                    vchain(ctx, v, v.tensor_scalar(rows[:, 11:12], rows[:, 9:10],
                                                   ctx.c(1.0 / 256.0), None, ALU.mult))
                    vchain(ctx, v, v.tensor_tensor(rows[:, 8:9], stt[:, si * 4: si * 4 + 1],
                                                   stt[:, si * 4: si * 4 + 1], ALU.mult))
                    vchain(ctx, v, v.tensor_tensor(stt[:, si * 4 + 1: si * 4 + 2],
                                                   rows[:, 11:12], rows[:, 8:9], ALU.subtract))
                    SW.inc(v.tensor_scalar(stt[:, si * 4 + 2: si * 4 + 3],
                                           rows[:, 10:11], ctx.c(1.0 / 256.0), None, ALU.mult))
                ctx.on("vector", _st_v)
                vn = SW.n

                def _st_l2(s, si=si, vn=vn):
                    s.wait_ge(SW.h, vn)
                    SS.inc(s.activation(stt[:, si * 4 + 3: si * 4 + 4], rows[:, 9:10], AF.Sqrt))
                ctx.on("scalar", _st_l2)

            sn = SS.n

            def _stt(te, sn=sn, vg=SW.n):
                te.wait_ge(SS.h, sn)
                te.wait_ge(SW.h, vg)
                ST.inc(te.transpose(pt2[0:16, 0:128], stt[:], ctx.ident[:]))
            ctx.on("tensor", _stt)
            tn2 = ST.n

            def _ste(v, q4=q4, h=h, tn2=tn2):
                v.wait_ge(ST.h, tn2)
                SW.inc(v.tensor_copy(statsf[0:16, h * 512 + q4 * 128: h * 512 + (q4 + 1) * 128],
                                     pt2[0:16, 0:128]))
            ctx.on("vector", _ste)

    for t in range(NFT8):
        for q4 in range(4):
            hs_tile(t * 4 + q4, hsf, q4 * 128, 512, xtbuf=xtf, nbuf=1)
            stats_tile(t * 4 + q4, q4)
        ctx.barrier()
        for oc in range(NDC):
            def _shm(te, t=t, oc=oc, vg=SW.n, sg=SS.n):
                te.wait_ge(SW.h, vg)
                te.wait_ge(SS.h, sg)
                for k in range(NDC):
                    inst = te.matmul(pA[:], w1hsb[:, k * 1024 + oc * 128: k * 1024 + oc * 128 + 128],
                                     hsf[:, k * 512: k * 512 + 512],
                                     start=(k == 0), stop=(k == NDC - 1), skip_group_check=True)
                ST.inc(inst)
            ctx.on("tensor", _shm)
            tn = ST.n

            def _shev(s, tn=tn):
                s.wait_ge(ST.h, tn)
                SS.inc(s.activation(s_sh, pA[:], AF.Copy))
            ctx.on("scalar", _shev)
            sn = SS.n

            for h in range(2):
                def _hsm(te, t=t, oc=oc, h=h, sn=sn, vg=SW.n):
                    te.wait_ge(SS.h, sn)
                    te.wait_ge(SW.h, vg)
                    ST.inc(te.matmul(pB[:], w1sb[0:16, oc * 128:(oc + 1) * 128],
                                     statsf[0:16, h * 512: h * 512 + 512],
                                     start=True, stop=True, skip_group_check=True))
                ctx.on("tensor", _hsm)
                tn = ST.n

                def _h1(v, tn=tn):
                    v.wait_ge(ST.h, tn)
                    SW.inc(v.tensor_tensor(scr[:, 0:512], s_sh, pB[:], ALU.add))
                ctx.on("vector", _h1)
                vn = SW.n

                def _h1g(s, oc=oc, vn=vn):
                    s.wait_ge(SW.h, vn)
                    SS.inc(s.activation(h1t, scr[:, 0:512], AF.Gelu,
                                        bias=b1sb[:, oc:oc + 1]))
                ctx.on("scalar", _h1g)
                sn2 = SS.n

                def _lgm(te, t=t, oc=oc, h=h, sn2=sn2):
                    te.wait_ge(SS.h, sn2)
                    inst = te.matmul(pl[h][:], w2sb[:, oc * 4:(oc + 1) * 4], h1t,
                                     start=(oc == 0), stop=(oc == NDC - 1), skip_group_check=True)
                    if oc == NDC - 1:
                        ST.inc(inst)
                ctx.on("tensor", _lgm)
        tn = ST.n

        def _lge(v, tn=tn):
            v.wait_ge(ST.h, tn)
            vchain(ctx, v, v.tensor_scalar(lgsb[:, 0:512], pl[0][:], b2sb[0:4, 0:1], None, ALU.add))
            SW.inc(v.tensor_scalar(lgsb[:, 512:1024], pl[1][:], b2sb[0:4, 0:1], None, ALU.add))
        ctx.on("vector", _lge)
        vn = SW.n

        for h in range(2):
            for q4 in range(4):
                def _plt(te, h=h, q4=q4, vn=vn, vg=SW.n):
                    te.wait_ge(SW.h, max(vn, vg))
                    ST.inc(te.transpose(pt2[:, 0:4],
                                        lgsb[:, h * 512 + q4 * 128: h * 512 + (q4 + 1) * 128],
                                        ctx.ident[0:4, 0:4]))
                ctx.on("tensor", _plt)
                tn2 = ST.n

                def _sfx(v, t=t, h=h, q4=q4, tn2=tn2):
                    v.wait_ge(ST.h, tn2)
                    i = t * 4 + q4
                    sl = slice(i * 8 + h * 4, i * 8 + h * 4 + 4)
                    vchain(ctx, v, v.tensor_scalar(probs[:, sl], pt2[:, 0:4],
                                                   rtau[:, h:h + 1], None, ALU.mult))
                    vchain(ctx, v, v.reduce_max(rows[:, 8:9], probs[:, sl], axis=AX))
                    SW.inc(v.tensor_scalar(probs[:, sl], probs[:, sl],
                                           rows[:, 8:9], None, ALU.subtract))
                ctx.on("vector", _sfx)
                vn2 = SW.n

                def _sfe(s, t=t, h=h, q4=q4, vn2=vn2):
                    s.wait_ge(SW.h, vn2)
                    i = t * 4 + q4
                    sl = slice(i * 8 + h * 4, i * 8 + h * 4 + 4)
                    SS.inc(s.activation(probs[:, sl], probs[:, sl], AF.Exp))
                ctx.on("scalar", _sfe)
                sn2 = SS.n

                def _sfn(v, t=t, h=h, q4=q4, sn2=sn2):
                    v.wait_ge(SS.h, sn2)
                    i = t * 4 + q4
                    sl = slice(i * 8 + h * 4, i * 8 + h * 4 + 4)
                    vchain(ctx, v, v.reduce_sum(rows[:, 8:9], probs[:, sl], axis=AX))
                    vchain(ctx, v, v.reciprocal(rows[:, 9:10], rows[:, 8:9]))
                    vchain(ctx, v, v.tensor_scalar(probs[:, sl], probs[:, sl],
                                                   rows[:, 9:10], None, ALU.mult))
                    SW.inc(v.tensor_scalar(probs[:, sl], probs[:, sl],
                                           ctx.c(0.8), ctx.c(0.05), ALU.mult, ALU.add))
                ctx.on("vector", _sfn)
    ctx.barrier()
    _dump("probs", probs[:])
    if stop_after == "pf":
        return

    for i in range(NTT):
        def _trb(te, i=i, vg=SW.n, sg=SS.n):
            te.wait_ge(SW.h, vg)
            te.wait_ge(SS.h, sg)
            for cc in range(4):
                te.transpose(ptb[:, cc * 128:(cc + 1) * 128],
                             A2[:, cc * LB + i * 128: cc * LB + (i + 1) * 128], ctx.identb[:])
            for cc in range(4):
                te.transpose(ptb[:, 512 + cc * 128: 512 + (cc + 1) * 128],
                             A3[:, cc * LB + i * 128: cc * LB + (i + 1) * 128], ctx.identb[:])
            for cc in range(4):
                inst = te.transpose(ptb[:, 1024 + cc * 128: 1024 + (cc + 1) * 128],
                                    A4[:, cc * VST + 64 + i * 128: cc * VST + 64 + (i + 1) * 128],
                                    ctx.identb[:])
            ST.inc(inst)
        ctx.on("tensor", _trb)
        tn = ST.n

        def _bl(v, i=i, tn=tn):
            v.wait_ge(ST.h, tn)
            for h in range(2):
                sl = slice(h * 256, h * 256 + 256)
                pb_ = lambda p, i=i, h=h: probs[:, i * 8 + h * 4 + p: i * 8 + h * 4 + p + 1]
                vchain(ctx, v, v.tensor_scalar(otm[:, sl], ptb[:, h * 256: h * 256 + 256], pb_(0), None, ALU.mult))
                vchain(ctx, v, v.scalar_tensor_tensor(otm[:, sl], ptb[:, 512 + h * 256: 512 + h * 256 + 256], pb_(1), otm[:, sl], ALU.mult, ALU.add))
                vchain(ctx, v, v.scalar_tensor_tensor(otm[:, sl], delta[:, i * 512 + h * 256: i * 512 + h * 256 + 256], pb_(2), otm[:, sl], ALU.mult, ALU.add))
                if h == 1:
                    SW.inc(v.scalar_tensor_tensor(otm[:, sl], ptb[:, 1024 + h * 256: 1024 + h * 256 + 256], pb_(3), otm[:, sl], ALU.mult, ALU.add))
                else:
                    vchain(ctx, v, v.scalar_tensor_tensor(otm[:, sl], ptb[:, 1024 + h * 256: 1024 + h * 256 + 256], pb_(3), otm[:, sl], ALU.mult, ALU.add))
        ctx.on("vector", _bl)
        vn = SW.n

        def _rs_s(s, vn=vn):
            s.wait_ge(SW.h, vn)
            SS.inc(s.activation(sqf2[:, 0:512], otm, AF.Square))
        ctx.on("scalar", _rs_s)
        sn = SS.n

        def _rs_v(v, sn=sn):
            v.wait_ge(SS.h, sn)
            for h in range(2):
                sl = slice(h * 256, h * 256 + 256)
                vchain(ctx, v, v.reduce_sum(rows[:, 8:9], sqf2[:, sl], axis=AX))
                if h == 1:
                    SW.inc(v.tensor_scalar(rows[:, 10 + h: 11 + h], rows[:, 8:9],
                                           ctx.c(1.0 / 256.0), ctx.c(1e-5), ALU.mult, ALU.add))
                else:
                    vchain(ctx, v, v.tensor_scalar(rows[:, 10 + h: 11 + h], rows[:, 8:9],
                                                   ctx.c(1.0 / 256.0), ctx.c(1e-5), ALU.mult, ALU.add))
        ctx.on("vector", _rs_v)
        vn = SW.n

        def _rs_q(s, vn=vn):
            s.wait_ge(SW.h, vn)
            SS.inc(s.activation(rows[:, 8:10], rows[:, 10:12], AF.Sqrt))
        ctx.on("scalar", _rs_q)
        sn = SS.n

        def _rs_a(v, sn=sn):
            v.wait_ge(SS.h, sn)
            for h in range(2):
                vchain(ctx, v, v.reciprocal(rows[:, 4 + h: 5 + h], rows[:, 8 + h: 9 + h]))
                vchain(ctx, v, v.tensor_tensor(rows[:, 6:7], rows[:, 4 + h: 5 + h], rows[:, 4 + h: 5 + h], ALU.mult))
                vchain(ctx, v, v.tensor_tensor(rows[:, 6:7], rows[:, 6:7], rows[:, 10 + h: 11 + h], ALU.mult))
                vchain(ctx, v, v.tensor_scalar(rows[:, 6:7], rows[:, 6:7], ctx.c(-0.5), ctx.c(1.5), ALU.mult, ALU.add))
                vchain(ctx, v, v.tensor_tensor(rows[:, 4 + h: 5 + h], rows[:, 4 + h: 5 + h], rows[:, 6:7], ALU.mult))
                if h == 1:
                    SW.inc(v.tensor_scalar(otm[:, h * 256:(h + 1) * 256],
                                           otm[:, h * 256:(h + 1) * 256],
                                           rows[:, 4 + h: 5 + h], None, ALU.mult))
                else:
                    vchain(ctx, v, v.tensor_scalar(otm[:, h * 256:(h + 1) * 256],
                                                   otm[:, h * 256:(h + 1) * 256],
                                                   rows[:, 4 + h: 5 + h], None, ALU.mult))
        ctx.on("vector", _rs_a)
        vn = SW.n

        def _tro(te, vn=vn):
            te.wait_ge(SW.h, vn)
            for cc in range(4):
                inst = te.transpose(pt2[:, 512 + cc * 128: 512 + (cc + 1) * 128],
                                    otm[:, cc * 128:(cc + 1) * 128], ctx.ident[:])
            ST.inc(inst)
        ctx.on("tensor", _tro)
        tn = ST.n

        def _evo(v, tn=tn):
            v.wait_ge(ST.h, tn)
            SW.inc(v.tensor_copy(ofm[:], pt2[:, 512:1024]))
        ctx.on("vector", _evo)
        vn = SW.n

        for half in range(2):
            bank = [pA, pB][half]

            def _wo(te, half=half, bank=bank, vn=vn, vg=SW.n):
                te.wait_ge(SW.h, max(vn, vg))
                for cc in range(4):
                    inst = te.matmul(bank[:], ofm[:, cc * 128:(cc + 1) * 128],
                                     WoS[:, cc * 1024 + half * 512: cc * 1024 + half * 512 + 512],
                                     start=(cc == 0), stop=(cc == 3), skip_group_check=True)
                ST.inc(inst)
            ctx.on("tensor", _wo)
        tn = ST.n

        for half in range(2):
            def _pst(v, half=half, tn=tn, dg=ctx.dma.n):
                v.wait_ge(ST.h, tn)
                v.wait_ge(ctx.dma.h, dg)
                SW.inc(v.tensor_copy(pstg, [pA, pB][half][:]))
            ctx.on("vector", _pst)
            vn = SW.n

            def _sto(sy, i=i, half=half, vn=vn):
                sy.wait_ge(SW.h, vn)
                ctx.dma.inc(sy.dma_start(
                    out=rs_in[i * 128:(i + 1) * 128, half * 512:(half + 1) * 512],
                    in_=pstg), 16)
            ctx.on("sync", _sto)
    ctx.barrier()

    def _rs(g, dg=ctx.dma.n):
        g.wait_ge(ctx.dma.h, dg)
        ctx.cc.inc(g.collective_compute("ReduceScatter", ALU.add,
                                        [[0, 1], [2, 3], [4, 5], [6, 7]],
                                        [rs_in.opt()], [rs_out.opt()]))
    ctx.on("gpsimd", _rs)
    ccn = ctx.cc.n

    for i in range(NT):
        buf = i % 2
        xv = xt[:, buf * D:(buf + 1) * D]

        def _lf(sy, i=i, xv=xv, ccn=ccn, guard=SW.n, dg=ctx.dmo.n):
            sy.wait_ge(ctx.cc.h, ccn)
            sy.wait_ge(SW.h, guard)
            sy.wait_ge(ctx.dmo.h, dg)
            ctx.dma.inc(sy.dma_start(out=xv, in_=xhalf[i * 128:(i + 1) * 128, :]), 16)
            ctx.dma.inc(sy.dma_start(out=lnst[:], in_=rs_out[i * 128:(i + 1) * 128, :]), 16)
        ctx.on("sync", _lf)
        dn = ctx.dma.n

        def _ad(v, dn=dn):
            v.wait_ge(ctx.dma.h, dn)
            SW.inc(v.tensor_tensor(lnst[:], lnst[:], xv, ALU.add))
        ctx.on("vector", _ad)
        vn = SW.n

        def _st(sy, i=i, vn=vn):
            sy.wait_ge(SW.h, vn)
            ctx.dmo.inc(sy.dma_start(out=xout[i * 128:(i + 1) * 128, :], in_=lnst[:]), 16)
        ctx.on("sync", _st)


L0A_DBG = {
    "qh": ([128, 4 * LB], BF16), "vh": ([128, 4 * VST], BF16),
    "beta": ([128, 2 * NTT], F32), "delta": ([128, 16384], BF16),
    "short": ([128, 4 * LB], BF16), "long": ([128, 4 * LB], BF16),
    "probs": ([128, 8 * NTT], F32),
}


def build_l0a(stop_after=None, dbg_tags=()):
    nc = bass.Bass("TRN2", target_bir_lowering=False, debug=False)
    xin = nc.dram_tensor("xin", [LB, D], F32, kind="ExternalInput").ap()
    xhalf = nc.dram_tensor("xhalf", [TPC, D], F32, kind="ExternalInput").ap()
    WqL = nc.dram_tensor("WqL", [128, 4096], BF16, kind="ExternalInput").ap()
    WkL = nc.dram_tensor("WkL", [128, 4096], BF16, kind="ExternalInput").ap()
    WvL = nc.dram_tensor("WvL", [128, 4096], BF16, kind="ExternalInput").ap()
    WbL = nc.dram_tensor("WbL", [128, 16], BF16, kind="ExternalInput").ap()
    dconv = nc.dram_tensor("dconv", [3, 4, 4, 128, 128], BF16, kind="ExternalInput").ap()
    dfirS = nc.dram_tensor("dfirS", [5, 4, 128, 128], BF16, kind="ExternalInput").ap()
    dfirL = nc.dram_tensor("dfirL", [64, 4, 128, 128], BF16, kind="ExternalInput").ap()
    WoL = nc.dram_tensor("WoL", [128, 4096], BF16, kind="ExternalInput").ap()
    onwc = nc.dram_tensor("onwc", [128, 4], F32, kind="ExternalInput").ap()
    w1hsL = nc.dram_tensor("w1hsL", [128, 8192], BF16, kind="ExternalInput").ap()
    w1st = nc.dram_tensor("w1st", [16, 1024], BF16, kind="ExternalInput").ap()
    w2L = nc.dram_tensor("w2L", [128, 32], BF16, kind="ExternalInput").ap()
    b1c = nc.dram_tensor("b1c", [128, 8], F32, kind="ExternalInput").ap()
    b2r = nc.dram_tensor("b2r", [4, 1], F32, kind="ExternalInput").ap()
    ltr = nc.dram_tensor("ltr", [1, 4], F32, kind="ExternalInput").ap()
    xout = nc.dram_tensor("xout", [TPC, D], F32, kind="ExternalOutput").ap()
    dbg = {t: nc.dram_tensor("dbg_" + t, list(L0A_DBG[t][0]), L0A_DBG[t][1],
                             kind="ExternalOutput").ap() for t in dbg_tags}
    rs_in = nc.dram_tensor("rs_in", [LB, D], F32, kind="Internal").ap()
    rs_out = nc.dram_tensor("rs_out", [TPC, D], F32, kind="Internal").ap()

    with ExitStack() as es:
        ctx = Ctx(nc, es)
        emit_l0a(ctx, xin, xhalf, xout, WqL, WkL, WvL, WbL, dconv, dfirS, dfirL,
                 WoL, onwc, w1hsL, w1st, w2L, b1c, b2r, ltr, rs_in, rs_out,
                 stop_after=stop_after, dbg=dbg)
        ctx.on("sync", lambda sy: sy.wait_ge(ctx.dmo.h, ctx.dmo.n))
    return nc


def run_l0a(x_full, g, stop_after=None, dbg_tags=()):
    nc = build_l0a(stop_after, dbg_tags)
    in_maps = []
    for core in range(NCORES):
        b, hp = core // 2, core % 2
        lt4 = g["l0_lt"]
        ltcore = np.zeros((1, 4), np.float32)
        ltcore[0, 0:2] = lt4[0, 2 * hp: 2 * hp + 2]
        in_maps.append({
            "xin": np.ascontiguousarray(x_full[b]),
            "xhalf": np.ascontiguousarray(x_full[b, hp * TPC:(hp + 1) * TPC]),
            "WqL": g[f"l0_WqL{hp}"], "WkL": g[f"l0_WkL{hp}"], "WvL": g[f"l0_WvL{hp}"],
            "WbL": g[f"l0_WbL{hp}"], "dconv": g[f"l0_dconv{hp}"],
            "dfirS": g[f"l0_dfirS{hp}"], "dfirL": g[f"l0_dfirL{hp}"],
            "WoL": g[f"l0_WoL{hp}"], "onwc": g[f"l0_onw{hp}"],
            "w1hsL": g["l0_w1hsL"], "w1st": g["l0_w1st"], "w2L": g["l0_w2L"],
            "b1c": g["l0_b1"], "b2r": g["l0_b2"], "ltr": ltcore,
        })
    res = run_bass_kernel_spmd(nc, in_maps, core_ids=list(range(NCORES)))
    out = np.empty((B, L, D), np.float32)
    for core in range(NCORES):
        b, hp = core // 2, core % 2
        out[b, hp * TPC:(hp + 1) * TPC] = res.results[core]["xout"]
    if dbg_tags:
        return out, res
    return out





def build_fused(blocks=('a', 'f', 't', 'm')):
    nc = bass.Bass("TRN2", target_bir_lowering=False, debug=False)
    dt = lambda n, s, t, k="ExternalInput": nc.dram_tensor(n, list(s), t, kind=k).ap()
    a_xin = dt("a_xin", [LB, D], F32)
    a_xhalf = dt("a_xhalf", [TPC, D], F32)
    a_WqL = dt("a_WqL", [128, 4096], BF16)
    a_WkL = dt("a_WkL", [128, 4096], BF16)
    a_WvL = dt("a_WvL", [128, 4096], BF16)
    a_WbL = dt("a_WbL", [128, 16], BF16)
    a_dconv = dt("a_dconv", [3, 4, 4, 128, 128], BF16)
    a_dfirS = dt("a_dfirS", [5, 4, 128, 128], BF16)
    a_dfirL = dt("a_dfirL", [64, 4, 128, 128], BF16)
    a_WoL = dt("a_WoL", [128, 4096], BF16)
    a_onwc = dt("a_onwc", [128, 4], F32)
    a_w1hsL = dt("a_w1hsL", [128, 8192], BF16)
    a_w1st = dt("a_w1st", [16, 1024], BF16)
    a_w2L = dt("a_w2L", [128, 32], BF16)
    a_b1c = dt("a_b1c", [128, 8], F32)
    a_b2r = dt("a_b2r", [4, 1], F32)
    a_ltr = dt("a_ltr", [1, 4], F32)
    rs_in = dt("rs_in", [LB, D], F32, "Internal")
    rs_out = dt("rs_out", [TPC, D], F32, "Internal")
    f_qwT = dt("f_qwT", [6, D, D], I32)
    f_screpT = dt("f_screpT", [6, D, D], F32)
    f_bias6 = dt("f_bias6", [6, 128, 8], F32)
    f_laT = dt("f_laT", [6, 128, 256], BF16)
    f_lbT = dt("f_lbT", [6, 32, D], BF16)
    f_gam = dt("f_gam", [3, 128, 8], F32)
    f_bet = dt("f_bet", [3, 128, 8], F32)
    t_WqL = dt("t_WqL", [128, NDC * 1024], BF16)
    t_WkL = dt("t_WkL", [128, NDC * 1024], BF16)
    t_WvL = dt("t_WvL", [128, NDC * 1024], BF16)
    t_WoL = dt("t_WoL", [128, NDC * 1024], BF16)
    t_projT = dt("t_projT", [64, 64], BF16)
    t_W1L = dt("t_W1L", [128, NDC * 4096], BF16)
    t_b1c = dt("t_b1c", [128, 32], F32)
    t_W2L = dt("t_W2L", [128, 32 * 1024], BF16)
    t_b2c = dt("t_b2c", [128, 8], F32)
    t_g1c = dt("t_g1c", [128, 8], F32)
    t_be1c = dt("t_be1c", [128, 8], F32)
    t_g2c = dt("t_g2c", [128, 8], F32)
    t_be2c = dt("t_be2c", [128, 8], F32)
    kv_in = dt("kv_in", [130, 512], F32, "Internal")
    kv_out = dt("kv_out", [130, 512], F32, "Internal")
    mx_in = dt("mx_in", [1, 1], F32, "Internal")
    mx_out = dt("mx_out", [1, 1], F32, "Internal")
    m_EX = dt("m_EX", [10, 128, 3 * 4096], BF16)
    m_gateTf = dt("m_gateTf", [128, 64], F32)
    m_lbrow = dt("m_lbrow", [8, 1], F32)
    cc_in = dt("cc_in", [1, 16], F32, "Internal")
    cc_out = dt("cc_out", [1, 16], F32, "Internal")
    s1 = dt("s1", [TPC, D], F32, "Internal")
    s2 = dt("s2", [TPC, D], F32, "Internal")
    s3 = dt("s3", [TPC, D], F32, "Internal")
    xout = dt("xout", [TPC, D], F32, "ExternalOutput")

    chain = []
    if 'a' in blocks:
        chain.append(('a', lambda c: emit_l0a(
            c, a_xin, a_xhalf, s1, a_WqL, a_WkL, a_WvL, a_WbL, a_dconv,
            a_dfirS, a_dfirL, a_WoL, a_onwc, a_w1hsL, a_w1st, a_w2L,
            a_b1c, a_b2r, a_ltr, rs_in, rs_out)))
    if 'f' in blocks:
        fin = s1 if 'a' in blocks else a_xhalf
        chain.append(('f', lambda c, fin=fin: emit_l0f(
            c, fin, s2, f_qwT, f_screpT, f_bias6, f_laT, f_lbT, f_gam, f_bet)))
    if 't' in blocks:
        tin = s2 if 'f' in blocks else a_xhalf
        chain.append(('t', lambda c, tin=tin: emit_l1a(
            c, tin, s3, t_WqL, t_WkL, t_WvL, t_WoL, t_projT, t_W1L, t_b1c,
            t_W2L, t_b2c, t_g1c, t_be1c, t_g2c, t_be2c, kv_in, kv_out,
            mx_in, mx_out)))
    if 'm' in blocks:
        min_ = s3 if 't' in blocks else a_xhalf
        chain.append(('m', lambda c, min_=min_: emit_l1f(
            c, min_, xout, m_EX, m_gateTf, m_lbrow, cc_in, cc_out)))
    outs = {'a': s1, 'f': s2, 't': s3, 'm': xout}

    with ExitStack() as es0:
        ctx = Ctx(nc, es0)
        for bi, (tag, fn) in enumerate(chain):
            with ExitStack() as eb:
                ctx.es = eb
                fn(ctx)
                if bi < len(chain) - 1 and SEAM_MODE == "reset":
                    ctx.seam_reset()
                else:
                    ctx.barrier()
        ctx.es = es0
        if chain[-1][0] != 'm':
            sfin = outs[chain[-1][0]]
            def _cp(sy, sfin=sfin):
                for i in range(4):
                    ctx.dmo.inc(sy.dma_start(out=xout[i * 512:(i + 1) * 512, :],
                                             in_=sfin[i * 512:(i + 1) * 512, :]), 16)
            ctx.on("sync", _cp)
        ctx.on("sync", lambda sy: sy.wait_ge(ctx.dmo.h, ctx.dmo.n))
    return nc


FUSED_BLOCKS = ('a', 'f', 't', 'm')
SEAM_MODE = 'plain'


def kernel(**inputs):
    g = _host_prep(inputs)
    nc = build_fused(FUSED_BLOCKS)
    in_maps = []
    for core in range(NCORES):
        b, hp = core // 2, core % 2
        lt4 = g["l0_lt"]
        ltcore = np.zeros((1, 4), np.float32)
        ltcore[0, 0:2] = lt4[0, 2 * hp: 2 * hp + 2]
        in_maps.append({
            "a_xin": np.ascontiguousarray(g["x"][b]),
            "a_xhalf": np.ascontiguousarray(g["x"][b, hp * TPC:(hp + 1) * TPC]),
            "a_WqL": g[f"l0_WqL{hp}"], "a_WkL": g[f"l0_WkL{hp}"],
            "a_WvL": g[f"l0_WvL{hp}"], "a_WbL": g[f"l0_WbL{hp}"],
            "a_dconv": g[f"l0_dconv{hp}"], "a_dfirS": g[f"l0_dfirS{hp}"],
            "a_dfirL": g[f"l0_dfirL{hp}"], "a_WoL": g[f"l0_WoL{hp}"],
            "a_onwc": g[f"l0_onw{hp}"], "a_w1hsL": g["l0_w1hsL"],
            "a_w1st": g["l0_w1st"], "a_w2L": g["l0_w2L"], "a_b1c": g["l0_b1"],
            "a_b2r": g["l0_b2"], "a_ltr": ltcore,
            "f_qwT": g["l0f_qwT"], "f_screpT": g["l0f_screpT"],
            "f_bias6": g["l0f_bias"], "f_laT": g["l0f_laT"], "f_lbT": g["l0f_lbT"],
            "f_gam": g["l0f_gam"], "f_bet": g["l0f_bet"],
            "t_WqL": g["l1_WqL"], "t_WkL": g["l1_WkL"], "t_WvL": g["l1_WvL"],
            "t_WoL": g["l1_WoL"], "t_projT": g["l1_projT"], "t_W1L": g["l1_W1L"],
            "t_b1c": g["l1_b1"], "t_W2L": g["l1_W2L"], "t_b2c": g["l1_b2"],
            "t_g1c": g["l1_ln1g"], "t_be1c": g["l1_ln1b"],
            "t_g2c": g["l1_ln2g"], "t_be2c": g["l1_ln2b"],
            "m_EX": g["m_EX"], "m_gateTf": g["m_gateTf"], "m_lbrow": g["m_lb"],
        })
    res = run_bass_kernel_spmd(nc, in_maps, core_ids=list(range(NCORES)))
    out = np.empty((B, L, D), np.float32)
    for core in range(NCORES):
        b, hp = core // 2, core % 2
        out[b, hp * TPC:(hp + 1) * TPC] = res.results[core]["xout"]
    return out




def emit_l1a(ctx, xin, xout, WqL, WkL, WvL, WoL, projT, W1L, b1c, W2L, b2c,
             g1c, be1c, g2c, be2c, kv_in, kv_out, mx_in, mx_out,
             stop_after=None, dbg=None):
    DN = 0.3535533905932738
    b1c_d, b2c_d, g1c_d, be1c_d, g2c_d, be2c_d = b1c, b2c, g1c, be1c, g2c, be2c
    RATIO = 0.125
    EPSK = 1e-6
    ctx.const_pool([1.0 / D, 1e-5, 1e-6, 0.0, -0.5, 1.5, 1.0, DN, RATIO,
                    RATIO * EPSK, 0.70710678118654752])
    ctx.emit_init()

    xt = ctx.sbuf("xt", [128, 2 * D], F32)
    scr = ctx.sbuf("scr", [128, D], F32)
    lnst = scr[:]
    rows = ctx.sbuf("rows", [128, 8], F32)
    sqf = ctx.sbuf("sqf", [128, D], BF16)
    rowstats = ctx.sbuf("rowstats", [128, 2 * NT], F32)
    hbA = ctx.sbuf("hbA", [128, NDC * TPC], BF16)
    bufA = ctx.sbuf("bufA", [128, NDC * TPC], BF16)
    bufB = ctx.sbuf("bufB", [128, NDC * TPC], BF16)
    bufC = ctx.sbuf("bufC", [128, NDC * TPC], BF16)
    bufD = ctx.sbuf("bufD", [128, NDC * TPC], BF16)
    Wb_ = [ctx.sbuf(f"Wb{i}", [128, NDC * 512], BF16) for i in range(2)]
    dq = ctx.sbuf("dq", [128, 16 * NT], F32)
    dk = ctx.sbuf("dk", [128, 16 * NT], F32)
    prj = ctx.sbuf("prj", [128, 64], BF16)
    mxr = ctx.sbuf("mxr", [128, 4], F32)
    mxrow = ctx.sbuf("mxrow", [1, 128], F32)
    kvloc = ctx.sbuf("kvloc", [128, 512], F32)
    kfs = ctx.sbuf("kfs", [1, 1024], F32)
    kvaug = ctx.sbuf("kvaug", [128, 8 * 65], BF16)
    t1 = scr[:, 0:512]
    t2 = scr[:, 512:1024]
    rcol = ctx.sbuf("rcol", [128, 4], F32)
    rws = ctx.sbuf("rws", [1, 3 * 512], F32)

    pt2 = ctx.psum("pt2", [128, 1024])
    pA = ctx.psum("pA", [128, 512])
    pB = ctx.psum("pB", [128, 512])
    pC = ctx.psum("pC", [128, 512])
    pD = ctx.psum("pD", [128, 512])
    prow = [ctx.psum(f"prow{i}", [1, 512]) for i in range(2)]

    b1c = ctx.sbuf("b1s", [128, 32], F32)
    b2c = ctx.sbuf("b2s", [128, 8], F32)
    g1c = ctx.sbuf("g1s", [128, 8], F32)
    be1c = ctx.sbuf("be1s", [128, 8], F32)
    g2c = ctx.sbuf("g2s", [128, 8], F32)
    be2c = ctx.sbuf("be2s", [128, 8], F32)

    def _ldpar(sy):
        for dst, srcap in ((b1c, b1c_d), (b2c, b2c_d), (g1c, g1c_d),
                           (be1c, be1c_d), (g2c, g2c_d), (be2c, be2c_d)):
            ctx.dma.inc(sy.dma_start(out=dst[:], in_=srcap), 16)
    ctx.on("sync", _ldpar)

    SW = ctx.sem["vector"]; SS = ctx.sem["scalar"]; ST = ctx.sem["tensor"]

    def ld(sy_ops):
        ctx.on("sync", sy_ops)
        return ctx.dma.n

    def _ldprj(sy):
        ctx.dma.inc(sy.dma_start(out=prj[0:64, :], in_=projT), 16)
        ctx.dma.inc(sy.dma_start(out=prj[64:128, :], in_=projT), 16)
    dn0 = ld(_ldprj)
    def _dump(tag, ap_src):
        if dbg is not None and tag in dbg:
            ctx.barrier()
            def _st(sy, ap_src=ap_src, apd=dbg[tag]):
                ctx.dmo.inc(sy.dma_start(out=apd, in_=ap_src), 16)
            ctx.on("sync", _st)

    def _stop(tag):
        return stop_after == tag

    emit_tm_ln_to_fm(ctx, xin, xt, lnst, rows, sqf, None, hbA, pt2,
                     hb_only=True, rowsave=rowstats)
    ctx.barrier()
    _dump("hb", hbA[:])
    if _stop("p1"):
        return

    def proj_fm(WL, dstbuf, scale_dn):
        WLv = WL.rearrange("p (k f) -> p k f", k=NDC)
        for half in range(2):
            wb = Wb_[half]

            def _ldw(sy, half=half, wb=wb, guard=ST.n):
                sy.wait_ge(ST.h, guard)
                ctx.dma.inc(sy.dma_start(
                    out=wb[:], in_=WLv[:, :, half * 512:(half + 1) * 512]), 16)
            ctx.on("sync", _ldw)
            dn = ctx.dma.n
            for t in range(NFT):
                for o4 in range(4):
                    oc = half * 4 + o4
                    bank = [pA, pB][(t * 4 + o4) % 2]

                    def _mm(te, t=t, o4=o4, bank=bank, dn=dn, wb=wb, vg=SW.n):
                        te.wait_ge(ctx.dma.h, dn)
                        te.wait_ge(SW.h, vg)
                        for k in range(NDC):
                            inst = te.matmul(
                                bank[:], wb[:, k * 512 + o4 * 128: k * 512 + o4 * 128 + 128],
                                hbA[:, k * TPC + t * 512: k * TPC + t * 512 + 512],
                                start=(k == 0), stop=(k == NDC - 1), skip_group_check=True)
                        ST.inc(inst)
                    ctx.on("tensor", _mm)
                    tn = ST.n

                    def _ev(v, t=t, oc=oc, bank=bank, tn=tn, dstbuf=dstbuf, scale_dn=scale_dn):
                        v.wait_ge(ST.h, tn)
                        sl = slice(oc * TPC + t * 512, oc * TPC + t * 512 + 512)
                        if scale_dn:
                            SW.inc(v.tensor_scalar(dstbuf[:, sl], bank[:], ctx.c(DN), None, ALU.mult))
                        else:
                            SW.inc(v.tensor_copy(dstbuf[:, sl], bank[:]))
                    ctx.on("vector", _ev)
        ctx.barrier()

    proj_fm(WqL, bufA, True)
    _dump("qs", bufA[:])
    if _stop("pq"):
        return
    proj_fm(WkL, bufB, True)

    WvLv = WvL.rearrange("p (k f) -> p k f", k=NDC)
    for half in range(2):
        wb = Wb_[half]

        def _ldwv(sy, half=half, wb=wb, guard=ST.n):
            sy.wait_ge(ST.h, guard)
            ctx.dma.inc(sy.dma_start(
                out=wb[:], in_=WvLv[:, :, half * 512:(half + 1) * 512]), 16)
        ctx.on("sync", _ldwv)
        dn = ctx.dma.n
        for col in range(NT):
            bank = [pA, pB][col % 2]

            def _mmv(te, col=col, bank=bank, dn=dn, wb=wb, vg=SW.n):
                te.wait_ge(ctx.dma.h, dn)
                te.wait_ge(SW.h, vg)
                for k in range(NDC):
                    inst = te.matmul(
                        bank[:], hbA[:, k * TPC + col * 128: k * TPC + col * 128 + 128],
                        wb[:, k * 512:(k + 1) * 512],
                        start=(k == 0), stop=(k == NDC - 1), skip_group_check=True)
                ST.inc(inst)
            ctx.on("tensor", _mmv)
            tn = ST.n

            def _evv(v, col=col, half=half, bank=bank, tn=tn):
                v.wait_ge(ST.h, tn)
                sl = slice(col * 1024 + half * 512, col * 1024 + half * 512 + 512)
                SW.inc(v.tensor_copy(bufC[:, sl], bank[:]))
            ctx.on("vector", _evv)
    ctx.barrier()

    ptb = pt2[:].bitcast(BF16)
    def diag_of(srcbuf, dst):
        for i in range(NT):
            def _tr(te, i=i, vg=SW.n, sg=SS.n):
                te.wait_ge(SW.h, vg)
                te.wait_ge(SS.h, sg)
                for c in range(NDC):
                    inst = te.transpose(ptb[:, c * 128:(c + 1) * 128],
                                        srcbuf[:, c * TPC + i * 128: c * TPC + (i + 1) * 128],
                                        ctx.identb[:])
                ST.inc(inst)
            ctx.on("tensor", _tr)
            tn = ST.n

            def _sq(s, tn=tn):
                s.wait_ge(ST.h, tn)
                SS.inc(s.activation(sqf[:], ptb[:, 0:1024], AF.Square,
                                    scale=ctx.c(0.70710678118654752)))
            ctx.on("scalar", _sq)
            sn = SS.n

            def _rd(v, i=i, sn=sn, dst=dst):
                v.wait_ge(SS.h, sn)
                for h in range(16):
                    inst = v.reduce_sum(dst[:, i * 16 + h: i * 16 + h + 1],
                                        sqf[:, h * 64:(h + 1) * 64], axis=AX)
                SW.inc(inst)
            ctx.on("vector", _rd)
    ctx.barrier()
    _dump("vtm", bufC[:])
    if _stop("pv"):
        return
    diag_of(bufA, dq)
    diag_of(bufB, dk)
    ctx.barrier()
    _dump("dq", dq[:])
    if _stop("pd"):
        return

    def _mz(v):
        SW.inc(v.memset(mxr[:, 0:1], -3e38))
    ctx.on("vector", _mz)
    bank_guard4 = [None, None]
    for h in range(16):
        for col in range(NT):
            par = (h * NT + col) % 2
            bank = [pA, pB][par]

            def _mmk(te, h=h, col=col, bank=bank, bg=bank_guard4[par]):
                if bg is not None:
                    te.wait_ge(SW.h, bg)
                ST.inc(te.matmul(bank[:, 0:64],
                                 bufB[(h % 2) * 64:(h % 2) * 64 + 64,
                                      (h // 2) * TPC + col * 128: (h // 2) * TPC + (col + 1) * 128],
                                 prj[(h % 2) * 64:(h % 2) * 64 + 64, :],
                                 start=True, stop=True, skip_group_check=True))
            ctx.on("tensor", _mmk)
            tn = ST.n

            def _mx(v, bank=bank, tn=tn):
                v.wait_ge(ST.h, tn)
                vchain(ctx, v, v.reduce_max(mxr[:, 1:2], bank[:, 0:64], axis=AX))
                SW.inc(v.tensor_tensor(mxr[:, 0:1], mxr[:, 0:1], mxr[:, 1:2], ALU.max))
            ctx.on("vector", _mx)
            bank_guard4[par] = SW.n
    vn = SW.n

    def _mfin(te, vn=vn):
        te.wait_ge(SW.h, vn)
        ST.inc(te.transpose(pt2[0:1, 0:128], mxr[:, 0:1], ctx.ident[:]))
    ctx.on("tensor", _mfin)
    tn = ST.n

    def _mfin2(v, tn=tn):
        v.wait_ge(ST.h, tn)
        SW.inc(v.reduce_max(mxrow[0:1, 0:1], pt2[0:1, 0:128], axis=AX))
    ctx.on("vector", _mfin2)
    vn = SW.n
    dn = ld(lambda sy, vn=vn: (sy.wait_ge(SW.h, vn),
                               ctx.dma.inc(sy.dma_start(out=mx_in, in_=mxrow[0:1, 0:1]), 16)))

    def _ccm(g, dn=dn):
        g.wait_ge(ctx.dma.h, dn)
        ctx.cc.inc(g.collective_compute("AllReduce", ALU.max,
                                        [[0, 1, 2, 3, 4, 5, 6, 7]],
                                        [mx_in.opt()], [mx_out.opt()]))
    ctx.on("gpsimd", _ccm)
    ccn = ctx.cc.n
    dn = ld(lambda sy, ccn=ccn: (sy.wait_ge(ctx.cc.h, ccn),
                                 ctx.dma.inc(sy.dma_start(out=mxrow[0:1, 0:1], in_=mx_out), 16)))

    def _mbc(te, dn=dn):
        te.wait_ge(ctx.dma.h, dn)
        ST.inc(te.matmul(pt2[:, 0:1], ctx.onesc[:], mxrow[0:1, 0:1], start=True, stop=True))
    ctx.on("tensor", _mbc)
    tn = ST.n

    def _mbc2(v, tn=tn):
        v.wait_ge(ST.h, tn)
        SW.inc(v.tensor_copy(mxr[:, 2:3], pt2[:, 0:1]))
    ctx.on("vector", _mbc2)
    ctx.barrier()

    def feat(srcbuf, dstbuf, dcol, is_q):
        bank_guard = [None, None]
        for h in range(16):
            for col in range(NT):
                par = (h * NT + col) % 2
                bank = [pA, pB][par]

                def _mmf(te, h=h, col=col, bank=bank, srcbuf=srcbuf,
                         bg=bank_guard[par]):
                    if bg is not None:
                        te.wait_ge(SW.h, bg)
                    ST.inc(te.matmul(bank[:, 0:64],
                                     srcbuf[(h % 2) * 64:(h % 2) * 64 + 64,
                                            (h // 2) * TPC + col * 128: (h // 2) * TPC + (col + 1) * 128],
                                     prj[(h % 2) * 64:(h % 2) * 64 + 64, :],
                                     start=True, stop=True, skip_group_check=True))
                ctx.on("tensor", _mmf)
                tn = ST.n

                def _evf(v, h=h, col=col, bank=bank, tn=tn, dcol=dcol, is_q=is_q):
                    v.wait_ge(ST.h, tn)
                    if is_q:
                        vchain(ctx, v, v.reduce_max(mxr[:, 3:4], bank[:, 0:64], axis=AX))
                        mcol = mxr[:, 3:4]
                    else:
                        mcol = mxr[:, 2:3]
                    SW.inc(v.tensor_scalar(t1[:, 0:64], bank[:, 0:64],
                                           dcol[:, col * 16 + h: col * 16 + h + 1],
                                           mcol, ALU.subtract, ALU.subtract))
                ctx.on("vector", _evf)
                vn = SW.n

                def _exf(s, vn=vn):
                    s.wait_ge(SW.h, vn)
                    SS.inc(s.activation(t2[:, 0:64], t1[:, 0:64], AF.Exp))
                ctx.on("scalar", _exf)
                sn = SS.n

                def _scf(v, h=h, col=col, sn=sn, dstbuf=dstbuf):
                    v.wait_ge(SS.h, sn)
                    SW.inc(v.tensor_scalar(
                        dstbuf[:, col * 1024 + h * 64: col * 1024 + h * 64 + 64],
                        t2[:, 0:64], ctx.c(RATIO), ctx.c(RATIO * EPSK), ALU.mult, ALU.add))
                ctx.on("vector", _scf)
                bank_guard[par] = SW.n

    _dump("mx", mxr[:, 2:3])
    if _stop("pm"):
        return
    feat(bufA, hbA, dq, True)
    ctx.barrier()
    _dump("qf", hbA[:])
    if _stop("pqf"):
        return
    feat(bufB, bufA, dk, False)
    ctx.barrier()
    _dump("kf", bufA[:])
    if _stop("pkf"):
        return
    qf_tm, kf_tm, v_tm = hbA, bufA, bufC
    o1fA = bufA[:].bitcast(F32)
    o1fB = bufB[:].bitcast(F32)

    def o1f_sl(c, a, b):
        base = o1fA if c < 4 else o1fB
        cc = c % 4
        return base[:, cc * TPC + a: cc * TPC + b]

    for hp in range(8):
        for par in range(2):
            h = hp * 2 + par
            for i in range(NT):
                def _mmkv(te, h=h, par=par, i=i, vg=SW.n):
                    if i == 0:
                        te.wait_ge(SW.h, vg)
                    inst = te.matmul(pA[par * 64:par * 64 + 64, 0:64],
                                     kf_tm[:, i * 1024 + h * 64: i * 1024 + (h + 1) * 64],
                                     v_tm[:, i * 1024 + h * 64: i * 1024 + (h + 1) * 64],
                                     start=(i == 0), stop=(i == NT - 1),
                                     skip_group_check=True)
                    if i == NT - 1:
                        ST.inc(inst)
                ctx.on("tensor", _mmkv)
        tn = ST.n

        def _evkv(v, hp=hp, tn=tn):
            v.wait_ge(ST.h, tn)
            SW.inc(v.tensor_copy(kvloc[:, hp * 64:(hp + 1) * 64], pA[:, 0:64]))
        ctx.on("vector", _evkv)
        vn = SW.n
        ctx.on("tensor", lambda te, vn=vn: te.wait_ge(SW.h, vn))

    for i in range(NT):
        def _mks(te, i=i):
            te.matmul(prow[0][:], ctx.onesk[:],
                      kf_tm[:, i * 1024: i * 1024 + 512],
                      start=(i == 0), stop=(i == NT - 1), skip_group_check=True)
            inst = te.matmul(prow[1][:], ctx.onesk[:],
                             kf_tm[:, i * 1024 + 512: i * 1024 + 1024],
                             start=(i == 0), stop=(i == NT - 1), skip_group_check=True)
            if i == NT - 1:
                ST.inc(inst)
        ctx.on("tensor", _mks)
    tn = ST.n

    def _evks(v, tn=tn):
        v.wait_ge(ST.h, tn)
        v.tensor_copy(kfs[:, 0:512], prow[0][:])
        SW.inc(v.tensor_copy(kfs[:, 512:1024], prow[1][:]))
    ctx.on("vector", _evks)
    vn = SW.n

    def _stkv(sy, vn=vn):
        sy.wait_ge(SW.h, vn)
        ctx.dma.inc(sy.dma_start(out=kv_in[0:128, :], in_=kvloc[:]), 16)
        ctx.dma.inc(sy.dma_start(out=kv_in[128:129, :], in_=kfs[:, 0:512]), 16)
        ctx.dma.inc(sy.dma_start(out=kv_in[129:130, :], in_=kfs[:, 512:1024]), 16)
    ctx.on("sync", _stkv)
    dn = ctx.dma.n

    def _cckv(g, dn=dn):
        g.wait_ge(ctx.dma.h, dn)
        ctx.cc.inc(g.collective_compute("AllReduce", ALU.add,
                                        [[0, 1], [2, 3], [4, 5], [6, 7]],
                                        [kv_in.opt()], [kv_out.opt()]))
    ctx.on("gpsimd", _cckv)
    ccn = ctx.cc.n

    def _ldkv(sy, ccn=ccn):
        sy.wait_ge(ctx.cc.h, ccn)
        ctx.dma.inc(sy.dma_start(out=kvloc[:], in_=kv_out[0:128, :]), 16)
        ctx.dma.inc(sy.dma_start(out=kfs[:, 0:512], in_=kv_out[128:129, :]), 16)
        ctx.dma.inc(sy.dma_start(out=kfs[:, 512:1024], in_=kv_out[129:130, :]), 16)
    ctx.on("sync", _ldkv)
    dn = ctx.dma.n

    def _kva(v, dn=dn):
        v.wait_ge(ctx.dma.h, dn)
        src8 = kvloc[:].rearrange("p (hp c) -> p hp c", hp=8)
        dst8 = kvaug[:].rearrange("p (hp c) -> p hp c", hp=8)[:, :, 0:64]
        SW.inc(v.tensor_copy(dst8, src8))
    ctx.on("vector", _kva)
    for hp in range(8):
        def _trk(te, hp=hp, vg=SW.n, dn=dn):
            te.wait_ge(ctx.dma.h, dn)
            te.wait_ge(SW.h, vg)
            ST.inc(te.transpose(pB[:, 0:1],
                                kfs[0:1, hp * 128:(hp + 1) * 128],
                                ctx.ident[0:1, 0:1]))
        ctx.on("tensor", _trk)
        tn = ST.n

        def _evk2(v, hp=hp, tn=tn):
            v.wait_ge(ST.h, tn)
            SW.inc(v.tensor_copy(kvaug[:, hp * 65 + 64: hp * 65 + 65], pB[:, 0:1]))
        ctx.on("vector", _evk2)
        vn = SW.n
        ctx.on("tensor", lambda te, vn=vn: te.wait_ge(SW.h, vn))
    ctx.barrier()
    _dump("kvaug", kvaug[:])
    if _stop("pkv"):
        return

    qf_fm = bufB
    for i in range(NT):
        def _trq(te, i=i, vg=SW.n):
            te.wait_ge(SW.h, vg)
            srcq = qf_tm[:, i * 1024:(i + 1) * 1024]
            for c in range(NDC):
                inst = te.transpose(ptb[:, c * 128:(c + 1) * 128],
                                    srcq[:, c * 128:(c + 1) * 128], ctx.identb[:])
            ST.inc(inst)
        ctx.on("tensor", _trq)
        tn = ST.n

        def _evq(v, i=i, tn=tn):
            v.wait_ge(ST.h, tn)
            dst = qf_fm[:].rearrange("p (c t) -> p c t", c=NDC)[:, :, i * 128:(i + 1) * 128]
            SW.inc(v.tensor_copy(dst, ptb[:, 0:1024].rearrange("p (c t) -> p c t", c=NDC)))
        ctx.on("vector", _evq)
    ctx.barrier()

    attn_tm = bufC
    bank_guard8 = [None, None]
    for h in range(16):
        for col in range(NT):
            par = (h * NT + col) % 2
            bank = [pA, pB][par]

            def _mmn(te, h=h, col=col, bank=bank, bg=bank_guard8[par]):
                if bg is not None:
                    te.wait_ge(SW.h, bg)
                ST.inc(te.matmul(
                    bank[:, 0:65],
                    qf_fm[(h % 2) * 64:(h % 2) * 64 + 64,
                          (h // 2) * TPC + col * 128: (h // 2) * TPC + (col + 1) * 128],
                    kvaug[(h % 2) * 64:(h % 2) * 64 + 64,
                          (h // 2) * 65: (h // 2) * 65 + 65],
                    start=True, stop=True, skip_group_check=True))
            ctx.on("tensor", _mmn)
            tn = ST.n

            def _evn(v, h=h, col=col, bank=bank, tn=tn):
                v.wait_ge(ST.h, tn)
                vchain(ctx, v, v.reciprocal(rcol[:, 0:1], bank[:, 64:65]))
                SW.inc(v.tensor_scalar(
                    attn_tm[:, col * 1024 + h * 64: col * 1024 + h * 64 + 64],
                    bank[:, 0:64], rcol[:, 0:1], None, ALU.mult))
            ctx.on("vector", _evn)
            bank_guard8[par] = SW.n
    ctx.barrier()
    _dump("attn", attn_tm[:])
    if _stop("pat"):
        return

    attn_fm = bufD
    for i in range(NT):
        def _tra(te, i=i, vg=SW.n):
            te.wait_ge(SW.h, vg)
            srca = attn_tm[:, i * 1024:(i + 1) * 1024]
            for c in range(NDC):
                inst = te.transpose(ptb[:, c * 128:(c + 1) * 128],
                                    srca[:, c * 128:(c + 1) * 128], ctx.identb[:])
            ST.inc(inst)
        ctx.on("tensor", _tra)
        tn = ST.n

        def _eva(v, i=i, tn=tn):
            v.wait_ge(ST.h, tn)
            dst = attn_fm[:].rearrange("p (c t) -> p c t", c=NDC)[:, :, i * 128:(i + 1) * 128]
            SW.inc(v.tensor_copy(dst, ptb[:, 0:1024].rearrange("p (c t) -> p c t", c=NDC)))
        ctx.on("vector", _eva)
    ctx.barrier()

    o1b = hbA
    WoLv = WoL.rearrange("p (k f) -> p k f", k=NDC)

    def _ldwo(sy, guard=ST.n):
        sy.wait_ge(ST.h, guard)
        ctx.dma.inc(sy.dma_start(out=Wb_[0][:], in_=WoLv[:, :, 0:512]), 16)
        ctx.dma.inc(sy.dma_start(out=Wb_[1][:], in_=WoLv[:, :, 512:1024]), 16)
    dn = ld(_ldwo)
    for i in range(NT):
        buf = i % 2
        xv = xt[:, buf * D:(buf + 1) * D]

        def _ldx(sy, i=i, xv=xv, guard=ctx.sem["vector"].n):
            sy.wait_ge(SW.h, guard)
            ctx.dma.inc(sy.dma_start(out=xv, in_=xin[i * 128:(i + 1) * 128, :]), 16)
        ctx.on("sync", _ldx)
        dxn = ctx.dma.n

        for half in range(2):
            bank = [pA, pB][half]

            def _mmo(te, i=i, half=half, bank=bank, dn=dn, vg=SW.n):
                te.wait_ge(ctx.dma.h, dn)
                te.wait_ge(SW.h, vg)
                for k in range(NDC):
                    inst = te.matmul(
                        bank[:], attn_fm[:, k * TPC + i * 128: k * TPC + (i + 1) * 128],
                        Wb_[half][:, k * 512:(k + 1) * 512],
                        start=(k == 0), stop=(k == NDC - 1), skip_group_check=True)
                ST.inc(inst)
            ctx.on("tensor", _mmo)
        tn = ST.n

        def _z(v, i=i, tn=tn, dxn=dxn, xv=xv):
            v.wait_ge(ST.h, tn)
            v.wait_ge(ctx.dma.h, dxn)
            vchain(ctx, v, v.tensor_scalar(lnst[:], xv, rowstats[:, 2 * i:2 * i + 1],
                                           rowstats[:, 2 * i + 1:2 * i + 2],
                                           ALU.subtract, ALU.mult))
            vchain(ctx, v, v.tensor_tensor(lnst[:, 0:512], lnst[:, 0:512], pA[:], ALU.add))
            SW.inc(v.tensor_tensor(lnst[:, 512:1024], lnst[:, 512:1024], pB[:], ALU.add))
        ctx.on("vector", _z)
        vn = SW.n

        def _st1(s, vn=vn):
            s.wait_ge(SW.h, vn)
            SS.inc(s.activation(sqf[:], lnst[:], AF.Square))
        ctx.on("scalar", _st1)
        sqn = SS.n

        def _r1(v, sqn=sqn):
            vchain(ctx, v, v.reduce_sum(rows[:, 0:1], lnst[:], axis=AX))
            v.wait_ge(SS.h, sqn)
            vchain(ctx, v, v.reduce_sum(rows[:, 1:2], sqf[:], axis=AX))
            vchain(ctx, v, v.tensor_scalar(rows[:, 2:3], rows[:, 0:1], ctx.c(1.0 / D), None, ALU.mult))
            vchain(ctx, v, v.tensor_scalar(rows[:, 3:4], rows[:, 1:2], ctx.c(1.0 / D), None, ALU.mult))
            vchain(ctx, v, v.tensor_tensor(rows[:, 4:5], rows[:, 2:3], rows[:, 2:3], ALU.mult))
            vchain(ctx, v, v.tensor_tensor(rows[:, 3:4], rows[:, 3:4], rows[:, 4:5], ALU.subtract))
            SW.inc(v.tensor_scalar(rows[:, 7:8], rows[:, 3:4], ctx.c(1e-6), None, ALU.add))
        ctx.on("vector", _r1)
        vn = SW.n

        def _s1(s, vn=vn):
            s.wait_ge(SW.h, vn)
            SS.inc(s.activation(rows[:, 5:6], rows[:, 7:8], AF.Sqrt))
        ctx.on("scalar", _s1)
        sn = SS.n

        def _a1(v, sn=sn):
            v.wait_ge(SS.h, sn)
            vchain(ctx, v, v.reciprocal(rows[:, 6:7], rows[:, 5:6]))
            vchain(ctx, v, v.tensor_tensor(rows[:, 4:5], rows[:, 6:7], rows[:, 6:7], ALU.mult))
            vchain(ctx, v, v.tensor_tensor(rows[:, 4:5], rows[:, 4:5], rows[:, 7:8], ALU.mult))
            vchain(ctx, v, v.tensor_scalar(rows[:, 4:5], rows[:, 4:5], ctx.c(-0.5), ctx.c(1.5), ALU.mult, ALU.add))
            vchain(ctx, v, v.tensor_tensor(rows[:, 6:7], rows[:, 6:7], rows[:, 4:5], ALU.mult))
            SW.inc(v.tensor_scalar(lnst[:], lnst[:], rows[:, 2:3], rows[:, 6:7],
                                   ALU.subtract, ALU.mult))
        ctx.on("vector", _a1)
        vn = SW.n

        def _tr1(te, vn=vn):
            te.wait_ge(SW.h, vn)
            for c in range(NDC):
                inst = te.transpose(pt2[:, c * 128:(c + 1) * 128],
                                    lnst[:, c * 128:(c + 1) * 128], ctx.ident[:])
            ST.inc(inst)
        ctx.on("tensor", _tr1)
        tn = ST.n

        def _ev1(v, i=i, tn=tn):
            v.wait_ge(ST.h, tn)
            for c in range(NDC):
                v.tensor_scalar(o1f_sl(c, i * 128, (i + 1) * 128),
                                pt2[:, c * 128:(c + 1) * 128],
                                g1c[:, c:c + 1], be1c[:, c:c + 1], ALU.mult, ALU.add)
            for c in range(NDC):
                inst = v.tensor_scalar(o1b[:, c * TPC + i * 128: c * TPC + (i + 1) * 128],
                                       pt2[:, c * 128:(c + 1) * 128],
                                       g1c[:, c:c + 1], be1c[:, c:c + 1], ALU.mult, ALU.add)
            SW.inc(inst)
        ctx.on("vector", _ev1)
    ctx.barrier()
    _dump("o1b", o1b[:])
    if _stop("po1"):
        return

    htile = bufD
    for t in range(NFT):
        for mc in range(32):
            wb = Wb_[mc % 2]

            def _ldw1(sy, mc=mc, wb=wb, guard=ST.n):
                sy.wait_ge(ST.h, guard)
                ctx.dma.inc(sy.dma_start(
                    out=wb[:, 0:1024],
                    in_=W1L[:].rearrange("p (k f) -> p k f", k=NDC)[:, :, mc * 128:(mc + 1) * 128]), 16)
            ctx.on("sync", _ldw1)
            dwn = ctx.dma.n
            bank = [pA, pB][mc % 2]

            def _mm1(te, t=t, mc=mc, wb=wb, bank=bank, dwn=dwn, vg=SW.n, sg=SS.n):
                te.wait_ge(ctx.dma.h, dwn)
                te.wait_ge(SW.h, vg)
                te.wait_ge(SS.h, sg)
                for k in range(NDC):
                    inst = te.matmul(bank[:],
                                     wb[:, k * 128:(k + 1) * 128],
                                     o1b[:, k * TPC + t * 512: k * TPC + t * 512 + 512],
                                     start=(k == 0), stop=(k == NDC - 1), skip_group_check=True)
                ST.inc(inst)
            ctx.on("tensor", _mm1)
            tn = ST.n

            def _el_v(v, mc=mc, bank=bank, tn=tn):
                v.wait_ge(ST.h, tn)
                v.tensor_scalar(t1[:], bank[:], b1c[:, mc:mc + 1], ctx.c(0.0), ALU.add, ALU.min)
                SW.inc(v.tensor_scalar(t2[:], bank[:], b1c[:, mc:mc + 1], ctx.c(0.0), ALU.add, ALU.max))
            ctx.on("vector", _el_v)
            vn = SW.n

            def _el_s(s, vn=vn):
                s.wait_ge(SW.h, vn)
                SS.inc(s.activation(t1[:], t1[:], AF.Exp))
            ctx.on("scalar", _el_s)
            sn = SS.n

            def _el_h(v, mc=mc, sn=sn):
                v.wait_ge(SS.h, sn)
                SW.inc(v.scalar_tensor_tensor(
                    htile[:, mc * 512:(mc + 1) * 512], t1[:], ctx.c(1.0), t2[:],
                    ALU.subtract, ALU.add))
            ctx.on("vector", _el_h)

        for grp in range(8):
            wb = Wb_[grp % 2]

            def _ldw2(sy, grp=grp, wb=wb, guard=ST.n):
                sy.wait_ge(ST.h, guard)
                ctx.dma.inc(sy.dma_start(
                    out=wb[:], in_=W2L[:, grp * 4096:(grp + 1) * 4096]), 16)
            ctx.on("sync", _ldw2)
            dwn = ctx.dma.n
            for oc in range(NDC):
                bank = [pC, pD][oc % 2]

                def _mm2(te, grp=grp, oc=oc, wb=wb, bank=bank, dwn=dwn, vg=SW.n):
                    te.wait_ge(ctx.dma.h, dwn)
                    te.wait_ge(SW.h, vg)
                    for kk in range(4):
                        inst = te.matmul(
                            bank[:], wb[:, kk * 1024 + oc * 128: kk * 1024 + oc * 128 + 128],
                            htile[:, (grp * 4 + kk) * 512:(grp * 4 + kk) * 512 + 512],
                            start=(kk == 0), stop=(kk == 3), skip_group_check=True)
                    ST.inc(inst)
                ctx.on("tensor", _mm2)
                tn = ST.n

                def _acc2(v, t=t, grp=grp, oc=oc, bank=bank, tn=tn):
                    v.wait_ge(ST.h, tn)
                    reg = o1f_sl(oc, t * 512, t * 512 + 512)
                    if grp == 0:
                        SW.inc(v.scalar_tensor_tensor(reg, bank[:], b2c[:, oc:oc + 1],
                                                      reg, ALU.add, ALU.add))
                    else:
                        SW.inc(v.tensor_tensor(reg, reg, bank[:], ALU.add))
                ctx.on("vector", _acc2)
    ctx.barrier()

    _dump("z2a", o1fA[:, 0:8192])
    _dump("z2b", o1fB[:, 0:8192])
    if _stop("pz2"):
        return
    zb = o1b
    for t in range(NFT):
        for k in range(NDC):
            def _cz(s, t=t, k=k):
                sl = slice(k * TPC + t * 512, k * TPC + t * 512 + 512)
                SS.inc(s.activation(zb[:, sl], o1f_sl(k, t * 512, t * 512 + 512), AF.Copy))
            ctx.on("scalar", _cz)
    ctx.barrier()
    emit_fm_ln_apply(ctx, o1f_sl, zb, g2c, be2c, rws, t1, prow, pt2, eps=1e-6)
    ctx.barrier()

    emit_fm_to_tm_add_sl(ctx, xin, xout, o1f_sl, xt, lnst, pt2)


def emit_fm_ln_apply(ctx, zf_sl, zb, gc, bc, rws, tmp, prow, pb, eps):
    SW = ctx.sem["vector"]; SS = ctx.sem["scalar"]; ST = ctx.sem["tensor"]
    sqc = tmp.bitcast(BF16)
    for t in range(NFT):
        for k in range(NDC):
            def _sq(s, t=t, k=k, guard=ST.n, vguard=SW.n):
                s.wait_ge(ST.h, guard)
                if k == 0:
                    s.wait_ge(SW.h, vguard)
                sl = slice(k * TPC + t * 512, k * TPC + t * 512 + 512)
                SS.inc(s.activation(sqc[:, 0:512], zb[:, sl], AF.Square))
            ctx.on("scalar", _sq)
            sn = SS.n

            def _mm(te, t=t, k=k, sn=sn):
                te.wait_ge(SS.h, sn)
                sl = slice(k * TPC + t * 512, k * TPC + t * 512 + 512)
                te.matmul(prow[0][:], ctx.onesk[:], zb[:, sl],
                          start=(k == 0), stop=(k == NDC - 1), skip_group_check=True)
                inst = te.matmul(prow[1][:], ctx.onesk[:], sqc[:, 0:512],
                                 start=(k == 0), stop=(k == NDC - 1), skip_group_check=True)
                ST.inc(inst)
            ctx.on("tensor", _mm)
        tn = ST.n

        def _rows_v(v, tn=tn):
            v.wait_ge(ST.h, tn)
            s0 = tmp[0:1, 0:512]
            mrow = rws[:, 1024:1536]
            vrow = rws[:, 0:512]
            vchain(ctx, v, v.tensor_scalar(mrow, prow[0][:], ctx.c(1.0 / D)[0:1, :], None, ALU.mult))
            vchain(ctx, v, v.tensor_scalar(vrow, prow[1][:], ctx.c(1.0 / D)[0:1, :], None, ALU.mult))
            vchain(ctx, v, v.tensor_tensor(s0, mrow, mrow, ALU.mult))
            vchain(ctx, v, v.tensor_tensor(vrow, vrow, s0, ALU.subtract))
            SW.inc(v.tensor_scalar(vrow, vrow, ctx.c(eps)[0:1, :], None, ALU.add))
        ctx.on("vector", _rows_v)
        vn = SW.n

        def _rows_s(s, vn=vn):
            s.wait_ge(SW.h, vn)
            SS.inc(s.activation(tmp[0:1, 0:512], rws[:, 0:512], AF.Sqrt))
        ctx.on("scalar", _rows_s)
        sn = SS.n

        def _rows_v2(v, sn=sn):
            v.wait_ge(SS.h, sn)
            r1 = rws[:, 512:1024]
            r2 = rws[:, 1024:1536]
            tt = tmp[0:1, 0:512]
            vchain(ctx, v, v.reciprocal(r1, tt))
            vchain(ctx, v, v.tensor_tensor(tt, r1, r1, ALU.mult))
            vchain(ctx, v, v.tensor_tensor(tt, tt, rws[:, 0:512], ALU.mult))
            vchain(ctx, v, v.tensor_scalar(tt, tt, ctx.c(-0.5)[0:1, :], ctx.c(1.5)[0:1, :], ALU.mult, ALU.add))
            vchain(ctx, v, v.tensor_tensor(r1, r1, tt, ALU.mult))
            SW.inc(v.tensor_tensor(r2, r2, r1, ALU.mult))
        ctx.on("vector", _rows_v2)
        vn = SW.n

        def _bc(te, vn=vn):
            te.wait_ge(SW.h, vn)
            te.matmul(pb[:, 0:512], ctx.onesc[:], rws[:, 512:1024], start=True, stop=True)
            inst = te.matmul(pb[:, 512:1024], ctx.onesc[:], rws[:, 1024:1536],
                             start=True, stop=True)
            ST.inc(inst)
        ctx.on("tensor", _bc)
        tn = ST.n

        for k in range(NDC):
            def _ap(v, t=t, k=k, tn=tn):
                v.wait_ge(ST.h, tn)
                sl = slice(k * TPC + t * 512, k * TPC + t * 512 + 512)
                reg = zf_sl(k, t * 512, t * 512 + 512)
                v.tensor_tensor(tmp[:, 0:512], reg, pb[:, 0:512], ALU.mult)
                v.tensor_tensor(tmp[:, 0:512], tmp[:, 0:512], pb[:, 512:1024], ALU.subtract)
                SW.inc(v.tensor_scalar(reg, tmp[:, 0:512],
                                       gc[:, k:k + 1], bc[:, k:k + 1], ALU.mult, ALU.add))
            ctx.on("vector", _ap)


L1A_DBG_SHAPES = {
    "hb": ([128, NDC * TPC], BF16), "qs": ([128, NDC * TPC], BF16),
    "vtm": ([128, NDC * TPC], BF16), "dq": ([128, 16 * NT], F32),
    "mx": ([128, 1], F32), "qf": ([128, NDC * TPC], BF16),
    "kf": ([128, NDC * TPC], BF16), "kvaug": ([128, 8 * 65], BF16),
    "attn": ([128, NDC * TPC], BF16), "o1b": ([128, NDC * TPC], BF16),
    "z2a": ([128, 8192], F32), "z2b": ([128, 8192], F32),
}


def build_l1a(stop_after=None, dbg_tags=()):
    nc = bass.Bass("TRN2", target_bir_lowering=False, debug=False)
    xin = nc.dram_tensor("xin", [TPC, D], F32, kind="ExternalInput").ap()
    WqL = nc.dram_tensor("WqL", [128, NDC * 1024], BF16, kind="ExternalInput").ap()
    WkL = nc.dram_tensor("WkL", [128, NDC * 1024], BF16, kind="ExternalInput").ap()
    WvL = nc.dram_tensor("WvL", [128, NDC * 1024], BF16, kind="ExternalInput").ap()
    WoL = nc.dram_tensor("WoL", [128, NDC * 1024], BF16, kind="ExternalInput").ap()
    projT = nc.dram_tensor("projT", [64, 64], BF16, kind="ExternalInput").ap()
    W1L = nc.dram_tensor("W1L", [128, NDC * 4096], BF16, kind="ExternalInput").ap()
    b1c = nc.dram_tensor("b1c", [128, 32], F32, kind="ExternalInput").ap()
    W2L = nc.dram_tensor("W2L", [128, 32 * 1024], BF16, kind="ExternalInput").ap()
    b2c = nc.dram_tensor("b2c", [128, 8], F32, kind="ExternalInput").ap()
    g1c = nc.dram_tensor("g1c", [128, 8], F32, kind="ExternalInput").ap()
    be1c = nc.dram_tensor("be1c", [128, 8], F32, kind="ExternalInput").ap()
    g2c = nc.dram_tensor("g2c", [128, 8], F32, kind="ExternalInput").ap()
    be2c = nc.dram_tensor("be2c", [128, 8], F32, kind="ExternalInput").ap()
    xout = nc.dram_tensor("xout", [TPC, D], F32, kind="ExternalOutput").ap()
    dbg = {t: nc.dram_tensor("dbg_" + t, list(L1A_DBG_SHAPES[t][0]),
                             L1A_DBG_SHAPES[t][1], kind="ExternalOutput").ap()
           for t in dbg_tags}
    kv_in = nc.dram_tensor("kv_in", [130, 512], F32, kind="Internal").ap()
    kv_out = nc.dram_tensor("kv_out", [130, 512], F32, kind="Internal").ap()
    mx_in = nc.dram_tensor("mx_in", [1, 1], F32, kind="Internal").ap()
    mx_out = nc.dram_tensor("mx_out", [1, 1], F32, kind="Internal").ap()
    with ExitStack() as es:
        ctx = Ctx(nc, es)
        emit_l1a(ctx, xin, xout, WqL, WkL, WvL, WoL, projT, W1L, b1c, W2L, b2c,
                 g1c, be1c, g2c, be2c, kv_in, kv_out, mx_in, mx_out,
                 stop_after=stop_after, dbg=dbg)
        ctx.on("sync", lambda sy: sy.wait_ge(ctx.dmo.h, ctx.dmo.n))
    return nc


def run_l1a(x2_full, g, stop_after=None, dbg_tags=()):
    nc = build_l1a(stop_after, dbg_tags)
    in_maps = []
    for core in range(NCORES):
        b, j = core // 2, core % 2
        in_maps.append({
            "xin": np.ascontiguousarray(x2_full[b, j * TPC:(j + 1) * TPC]),
            "WqL": g["l1_WqL"], "WkL": g["l1_WkL"], "WvL": g["l1_WvL"],
            "WoL": g["l1_WoL"], "projT": g["l1_projT"], "W1L": g["l1_W1L"],
            "b1c": g["l1_b1"], "W2L": g["l1_W2L"], "b2c": g["l1_b2"],
            "g1c": g["l1_ln1g"], "be1c": g["l1_ln1b"],
            "g2c": g["l1_ln2g"], "be2c": g["l1_ln2b"],
        })
    res = run_bass_kernel_spmd(nc, in_maps, core_ids=list(range(NCORES)))
    out = np.empty((B, L, D), np.float32)
    for core in range(NCORES):
        b, j = core // 2, core % 2
        out[b, j * TPC:(j + 1) * TPC] = res.results[core]["xout"]
    if dbg_tags:
        return out, res
    return out





def build_fused(blocks=('a', 'f', 't', 'm')):
    nc = bass.Bass("TRN2", target_bir_lowering=False, debug=False)
    dt = lambda n, s, t, k="ExternalInput": nc.dram_tensor(n, list(s), t, kind=k).ap()
    a_xin = dt("a_xin", [LB, D], F32)
    a_xhalf = dt("a_xhalf", [TPC, D], F32)
    a_WqL = dt("a_WqL", [128, 4096], BF16)
    a_WkL = dt("a_WkL", [128, 4096], BF16)
    a_WvL = dt("a_WvL", [128, 4096], BF16)
    a_WbL = dt("a_WbL", [128, 16], BF16)
    a_dconv = dt("a_dconv", [3, 4, 4, 128, 128], BF16)
    a_dfirS = dt("a_dfirS", [5, 4, 128, 128], BF16)
    a_dfirL = dt("a_dfirL", [64, 4, 128, 128], BF16)
    a_WoL = dt("a_WoL", [128, 4096], BF16)
    a_onwc = dt("a_onwc", [128, 4], F32)
    a_w1hsL = dt("a_w1hsL", [128, 8192], BF16)
    a_w1st = dt("a_w1st", [16, 1024], BF16)
    a_w2L = dt("a_w2L", [128, 32], BF16)
    a_b1c = dt("a_b1c", [128, 8], F32)
    a_b2r = dt("a_b2r", [4, 1], F32)
    a_ltr = dt("a_ltr", [1, 4], F32)
    rs_in = dt("rs_in", [LB, D], F32, "Internal")
    rs_out = dt("rs_out", [TPC, D], F32, "Internal")
    f_qwT = dt("f_qwT", [6, D, D], I32)
    f_screpT = dt("f_screpT", [6, D, D], F32)
    f_bias6 = dt("f_bias6", [6, 128, 8], F32)
    f_laT = dt("f_laT", [6, 128, 256], BF16)
    f_lbT = dt("f_lbT", [6, 32, D], BF16)
    f_gam = dt("f_gam", [3, 128, 8], F32)
    f_bet = dt("f_bet", [3, 128, 8], F32)
    t_WqL = dt("t_WqL", [128, NDC * 1024], BF16)
    t_WkL = dt("t_WkL", [128, NDC * 1024], BF16)
    t_WvL = dt("t_WvL", [128, NDC * 1024], BF16)
    t_WoL = dt("t_WoL", [128, NDC * 1024], BF16)
    t_projT = dt("t_projT", [64, 64], BF16)
    t_W1L = dt("t_W1L", [128, NDC * 4096], BF16)
    t_b1c = dt("t_b1c", [128, 32], F32)
    t_W2L = dt("t_W2L", [128, 32 * 1024], BF16)
    t_b2c = dt("t_b2c", [128, 8], F32)
    t_g1c = dt("t_g1c", [128, 8], F32)
    t_be1c = dt("t_be1c", [128, 8], F32)
    t_g2c = dt("t_g2c", [128, 8], F32)
    t_be2c = dt("t_be2c", [128, 8], F32)
    kv_in = dt("kv_in", [130, 512], F32, "Internal")
    kv_out = dt("kv_out", [130, 512], F32, "Internal")
    mx_in = dt("mx_in", [1, 1], F32, "Internal")
    mx_out = dt("mx_out", [1, 1], F32, "Internal")
    m_EX = dt("m_EX", [10, 128, 3 * 4096], BF16)
    m_gateTf = dt("m_gateTf", [128, 64], F32)
    m_lbrow = dt("m_lbrow", [8, 1], F32)
    cc_in = dt("cc_in", [1, 16], F32, "Internal")
    cc_out = dt("cc_out", [1, 16], F32, "Internal")
    s1 = dt("s1", [TPC, D], F32, "Internal")
    s2 = dt("s2", [TPC, D], F32, "Internal")
    s3 = dt("s3", [TPC, D], F32, "Internal")
    xout = dt("xout", [TPC, D], F32, "ExternalOutput")

    chain = []
    if 'a' in blocks:
        chain.append(('a', lambda c: emit_l0a(
            c, a_xin, a_xhalf, s1, a_WqL, a_WkL, a_WvL, a_WbL, a_dconv,
            a_dfirS, a_dfirL, a_WoL, a_onwc, a_w1hsL, a_w1st, a_w2L,
            a_b1c, a_b2r, a_ltr, rs_in, rs_out)))
    if 'f' in blocks:
        fin = s1 if 'a' in blocks else a_xhalf
        chain.append(('f', lambda c, fin=fin: emit_l0f(
            c, fin, s2, f_qwT, f_screpT, f_bias6, f_laT, f_lbT, f_gam, f_bet)))
    if 't' in blocks:
        tin = s2 if 'f' in blocks else a_xhalf
        chain.append(('t', lambda c, tin=tin: emit_l1a(
            c, tin, s3, t_WqL, t_WkL, t_WvL, t_WoL, t_projT, t_W1L, t_b1c,
            t_W2L, t_b2c, t_g1c, t_be1c, t_g2c, t_be2c, kv_in, kv_out,
            mx_in, mx_out)))
    if 'm' in blocks:
        min_ = s3 if 't' in blocks else a_xhalf
        chain.append(('m', lambda c, min_=min_: emit_l1f(
            c, min_, xout, m_EX, m_gateTf, m_lbrow, cc_in, cc_out)))
    outs = {'a': s1, 'f': s2, 't': s3, 'm': xout}

    with ExitStack() as es0:
        ctx = Ctx(nc, es0)
        for bi, (tag, fn) in enumerate(chain):
            with ExitStack() as eb:
                ctx.es = eb
                fn(ctx)
                if bi < len(chain) - 1 and SEAM_MODE == "reset":
                    ctx.seam_reset()
                else:
                    ctx.barrier()
        ctx.es = es0
        if chain[-1][0] != 'm':
            sfin = outs[chain[-1][0]]
            def _cp(sy, sfin=sfin):
                for i in range(4):
                    ctx.dmo.inc(sy.dma_start(out=xout[i * 512:(i + 1) * 512, :],
                                             in_=sfin[i * 512:(i + 1) * 512, :]), 16)
            ctx.on("sync", _cp)
        ctx.on("sync", lambda sy: sy.wait_ge(ctx.dmo.h, ctx.dmo.n))
    return nc


FUSED_BLOCKS = ('a', 'f', 't', 'm')
SEAM_MODE = 'plain'


def kernel(**inputs):
    g = _host_prep(inputs)
    nc = build_fused(FUSED_BLOCKS)
    in_maps = []
    for core in range(NCORES):
        b, hp = core // 2, core % 2
        lt4 = g["l0_lt"]
        ltcore = np.zeros((1, 4), np.float32)
        ltcore[0, 0:2] = lt4[0, 2 * hp: 2 * hp + 2]
        in_maps.append({
            "a_xin": np.ascontiguousarray(g["x"][b]),
            "a_xhalf": np.ascontiguousarray(g["x"][b, hp * TPC:(hp + 1) * TPC]),
            "a_WqL": g[f"l0_WqL{hp}"], "a_WkL": g[f"l0_WkL{hp}"],
            "a_WvL": g[f"l0_WvL{hp}"], "a_WbL": g[f"l0_WbL{hp}"],
            "a_dconv": g[f"l0_dconv{hp}"], "a_dfirS": g[f"l0_dfirS{hp}"],
            "a_dfirL": g[f"l0_dfirL{hp}"], "a_WoL": g[f"l0_WoL{hp}"],
            "a_onwc": g[f"l0_onw{hp}"], "a_w1hsL": g["l0_w1hsL"],
            "a_w1st": g["l0_w1st"], "a_w2L": g["l0_w2L"], "a_b1c": g["l0_b1"],
            "a_b2r": g["l0_b2"], "a_ltr": ltcore,
            "f_qwT": g["l0f_qwT"], "f_screpT": g["l0f_screpT"],
            "f_bias6": g["l0f_bias"], "f_laT": g["l0f_laT"], "f_lbT": g["l0f_lbT"],
            "f_gam": g["l0f_gam"], "f_bet": g["l0f_bet"],
            "t_WqL": g["l1_WqL"], "t_WkL": g["l1_WkL"], "t_WvL": g["l1_WvL"],
            "t_WoL": g["l1_WoL"], "t_projT": g["l1_projT"], "t_W1L": g["l1_W1L"],
            "t_b1c": g["l1_b1"], "t_W2L": g["l1_W2L"], "t_b2c": g["l1_b2"],
            "t_g1c": g["l1_ln1g"], "t_be1c": g["l1_ln1b"],
            "t_g2c": g["l1_ln2g"], "t_be2c": g["l1_ln2b"],
            "m_EX": g["m_EX"], "m_gateTf": g["m_gateTf"], "m_lbrow": g["m_lb"],
        })
    res = run_bass_kernel_spmd(nc, in_maps, core_ids=list(range(NCORES)))
    out = np.empty((B, L, D), np.float32)
    for core in range(NCORES):
        b, hp = core // 2, core % 2
        out[b, hp * TPC:(hp + 1) * TPC] = res.results[core]["xout"]
    return out



def emit_fm_to_tm_add_sl(ctx, xin, xout, sl_fn, xt, ost, pt2, ntile=NT):
    for i in range(ntile):
        buf = i % 2
        xv = xt[:, buf * D:(buf + 1) * D]

        def _load(sy, i=i, xv=xv):
            sy.wait_ge(ctx.sem["vector"].h, ctx.sem["vector"].n)
            ctx.dma.inc(sy.dma_start(out=xv, in_=xin[i * 128:(i + 1) * 128, :]), 16)
        ctx.on("sync", _load)
        dn = ctx.dma.n

        def _tr(te, i=i):
            te.wait_ge(ctx.sem["vector"].h, ctx.sem["vector"].n)
            for c in range(NDC):
                inst = te.transpose(pt2[:, c * 128:(c + 1) * 128],
                                    sl_fn(c, i * 128, (i + 1) * 128), ctx.ident[:])
            ctx.sem["tensor"].inc(inst)
        ctx.on("tensor", _tr)
        tn = ctx.sem["tensor"].n

        def _add(v, tn=tn, dn=dn, xv=xv):
            v.wait_ge(ctx.sem["tensor"].h, tn)
            v.wait_ge(ctx.dma.h, dn)
            ctx.sem["vector"].inc(v.tensor_tensor(ost[:], pt2[:], xv, ALU.add))
        ctx.on("vector", _add)
        vn = ctx.sem["vector"].n

        def _store(sy, i=i, vn=vn):
            sy.wait_ge(ctx.sem["vector"].h, vn)
            ctx.dmo.inc(sy.dma_start(out=xout[i * 128:(i + 1) * 128, :], in_=ost[:]), 16)
        ctx.on("sync", _store)
        dmo_n = ctx.dmo.n

        def _guard(v, dmo_n=dmo_n):
            v.wait_ge(ctx.dmo.h, dmo_n)
        ctx.on("vector", _guard)
